# version 22
# baseline (speedup 1.0000x reference)
import numpy as np
import ml_dtypes
from contextlib import ExitStack

import concourse.bass as bass
import concourse.mybir as mybir
from concourse.bass_utils import run_bass_kernel_spmd

F32 = mybir.dt.float32
BF16 = mybir.dt.bfloat16
AF = mybir.ActivationFunctionType
ALU = mybir.AluOpType
AX = mybir.AxisListType

NCORES = 8
T = 1024
D = 4096
DC = D // 128
HID = 11008
HCN = HID // 128
SEC_SIZES = [22, 22, 21, 21]
LR = 16
LSC = 2.0
NH = 32
QR = 1536
QRC = QR // 128
KVR = 512
KVRC = KVR // 128
RD = 64
ND = 128
VD = 128
QK = RD + ND
SEQ = NCORES * T
KTN = SEQ // 128

ENGS = ("sync", "tensor", "vector", "scalar", "gpsimd")
sIn, sW, sPE, sACT, sDVE, sPOOL, sOut, sCC = "in", "w", "pe", "act", "dve", "pool", "out", "cc"


class K:
    def __init__(self, nc, stack):
        self.nc = nc
        self.stack = stack
        self.q = {e: [] for e in ENGS}
        self.sems = {}
        import collections
        self.cnt = collections.defaultdict(int)

    def sem(self, name):
        if name not in self.sems:
            self.sems[name] = self.stack.enter_context(self.nc.semaphore(name))
            self.cnt[name] = 0
        return self.sems[name]

    ENGSEM = {"vector": sDVE, "scalar": sACT, "gpsimd": sPOOL}

    def emit(self, eng, fn, waits=(), inc=None, by=1):
        es = self.ENGSEM.get(eng)
        if es is not None:
            waits = list(waits) + [(es, self.cnt[es])]
            if inc is None:
                inc = es
        waits = tuple((n, v) for (n, v) in waits if v > 0)
        for n, _ in waits:
            self.sem(n)
        if inc is not None:
            self.sem(inc)

        def go(e, fn=fn, waits=waits, inc=inc, by=by):
            for (n, v) in waits:
                e.wait_ge(self.sems[n], v)
            ins = fn(e)
            if inc is not None:
                ins.then_inc(self.sems[inc], by)

        self.q[eng].append(go)
        if inc is not None:
            self.cnt[inc] += by
            return self.cnt[inc]
        return None

    def wait(self, eng, name, v):
        if v > 0:
            self.sem(name)
            self.q[eng].append(lambda e, n=name, v=v: e.wait_ge(self.sems[n], v))

    def dma(self, eng, out, in_, inc, waits=()):
        return self.emit(eng, lambda e: e.dma_start(out=out, in_=in_),
                         waits=waits, inc=inc, by=16)

    def dmat(self, eng, out, in_, inc, waits=()):
        return self.emit(eng, lambda e: e.dma_start_transpose(out=out, in_=in_),
                         waits=waits, inc=inc, by=16)

    def finish(self):
        block = self.stack.enter_context(self.nc.Block())
        for e in ENGS:
            def run(eng, e=e):
                for f in self.q[e]:
                    f(eng)
            getattr(block, e)(run)


def tload(w4, tile, c0=0, nch=None, flat_cols=None):
    if nch is None:
        return w4[tile]
    return w4[tile][:, c0:c0 + nch, :]


def wview(w, c0, nch, o0, no):
    return w[c0 * 128:(c0 + nch) * 128, o0:o0 + no].rearrange("(c d) o -> d c o", d=128)


def sview(sb, no):
    return sb.rearrange("d (c o) -> d c o", o=no)


class Ctx:
    overlap_hook = None


def build_common(nc, k, stack, c):
    ec = stack.enter_context
    c.AB = ec(nc.sbuf_tensor("AB", [128, 80 * 1024], BF16))
    c.AF32 = ec(nc.sbuf_tensor("AF32", [128, 4 * 1024], F32))
    c.ROWS = ec(nc.sbuf_tensor("ROWS", [1, 2 * 1024], F32))
    c.SM = ec(nc.sbuf_tensor("SM", [128, 176], F32))
    c.SMB = ec(nc.sbuf_tensor("SMB", [128, 160], BF16))
    c.L16 = ec(nc.sbuf_tensor("L16", [16, 3840], BF16))
    c.W2SB = ec(nc.sbuf_tensor("W2SB", [128, 2 * 2816], BF16))
    c.SINK = ec(nc.sbuf_tensor("SINK", [128, 1024], F32))
    c.EYE = ec(nc.sbuf_tensor("EYE", [24, 24], F32))
    c.LOGF = c.SINK[0:24, :]
    c.ps = [ec(nc.psum_tensor(f"ps{i}", [128, 512], F32)) for i in range(8)]

    c.ab = lambda c0, n: c.AB[:, c0 * 1024:(c0 + n) * 1024]
    c.af = lambda c0, n: c.AF32[:, c0 * 1024:(c0 + n) * 1024]

    c.ones_row = c.SM[0:1, 0:128]
    c.ones_col = c.SMB[:, 0:1]
    c.ones_col_f = c.SM[:, 130:131]
    c.eps_ln = c.SM[0:1, 129:130]

    c.ld_const = [
        k.dma("sync", c.SM[0:1, 0:128], c.d["ones_row"], sIn),
        k.dma("sync", c.SMB[:, 0:1], c.d["ones_col"], sIn),
        k.dma("sync", c.SM[0:1, 128:130], c.d["misc_f"], sIn),
        k.dma("sync", c.SM[:, 130:131], c.d["ones_col_f"], sIn),
    ]
    c.const_done = c.ld_const[-1]


def stat_rows(k, c, src_ab, nchunks, wait_fn):
    pe_after = {}
    last = None
    for ci in range(nchunks):
        par = ci % 2
        waits = list(wait_fn(ci))
        if ci >= 2:
            waits.append((sPE, pe_after[ci - 2]))
        a = k.emit("scalar", lambda e, ci=ci, par=par: e.activation(
            c.ab(64 + par, 1), src_ab[:, ci * 1024:(ci + 1) * 1024], AF.Square),
            waits=waits, inc=sACT)
        for h in range(2):
            k.emit("tensor", lambda e, ci=ci, h=h: e.matmul(
                c.ps[2 + h][0:1, :], c.ones_col,
                src_ab[:, ci * 1024 + h * 512: ci * 1024 + h * 512 + 512],
                start=(ci == 0), stop=(ci == nchunks - 1)),
                waits=list(wait_fn(ci)) + [(sIn, c.const_done)], inc=sPE)
        for h in range(2):
            last = k.emit("tensor", lambda e, par=par, h=h, ci=ci: e.matmul(
                c.ps[4 + h][0:1, :], c.ones_col,
                c.ab(64 + par, 1)[:, h * 512:h * 512 + 512],
                start=(ci == 0), stop=(ci == nchunks - 1)),
                waits=[(sACT, a)], inc=sPE)
            pe_after[ci] = last
    return last


def ln_apply(k, c, src_ab, dst_ab, nchunks, pe_stats, r0):
    rm = c.ROWS[0:1, 0:1024]
    ra = c.ROWS[0:1, 1024:2048]
    rt = c.SINK[0:1, 0:1024]
    for h in range(2):
        k.emit("vector", lambda e, h=h: e.tensor_copy(rm[:, h * 512:h * 512 + 512], c.ps[2 + h][0:1, :]),
               waits=[(sPE, pe_stats)], inc=sDVE)
    for h in range(2):
        k.emit("vector", lambda e, h=h: e.tensor_copy(rt[:, h * 512:h * 512 + 512], c.ps[4 + h][0:1, :]),
               inc=sDVE)
    k.emit("vector", lambda e: e.tensor_scalar_mul(rm, rm, 1.0 / D), inc=sDVE)
    k.emit("vector", lambda e: e.tensor_scalar_mul(rt, rt, 1.0 / D), inc=sDVE)
    k.emit("vector", lambda e: e.tensor_mul(ra, rm, rm), inc=sDVE)
    dv = k.emit("vector", lambda e: e.tensor_sub(rt, rt, ra), inc=sDVE)
    a1 = k.emit("scalar", lambda e: e.activation(ra, rt, AF.Sqrt, bias=c.eps_ln),
                waits=[(sDVE, dv), (sIn, c.const_done)], inc=sACT)
    dv = k.emit("vector", lambda e: e.reciprocal(ra, ra), waits=[(sACT, a1)], inc=sDVE)
    k.wait("vector", sDVE, dv)
    dv = k.emit("vector", lambda e: e.scalar_tensor_tensor(rm, rm, -1.0, ra, ALU.mult, ALU.mult),
                inc=sDVE)
    pe_rep = None
    for h in range(2):
        k.emit("tensor", lambda e, h=h: e.matmul(
            c.ps[0 + h][:, :], c.ones_row, ra[:, h * 512:h * 512 + 512], start=True, stop=True),
            waits=[(sDVE, dv)], inc=sPE)
    for h in range(2):
        pe_rep = k.emit("tensor", lambda e, h=h: e.matmul(
            c.ps[2 + h][:, :], c.ones_row, rm[:, h * 512:h * 512 + 512], start=True, stop=True),
            inc=sPE)
    scr = c.af(0, 1)
    last = None
    for ci in range(nchunks):
        for h in range(2):
            sl = slice(ci * 1024 + h * 512, ci * 1024 + h * 512 + 512)
            k.emit("vector", lambda e, sl=sl, h=h: e.tensor_mul(
                scr[:, 0:512], src_ab[:, sl], c.ps[0 + h][:, :]),
                waits=[(sPE, pe_rep)], inc=sDVE)
            last = k.emit("vector", lambda e, sl=sl, h=h: e.tensor_add(
                dst_ab[:, sl], scr[:, 0:512], c.ps[2 + h][:, :]),
                inc=sDVE)
    return last


def emit_p1(nc, k, stack, c, stage=99):
    d = c.d
    xbf = c.ab(0, 32)
    xn = c.ab(32, 32)
    gsec = c.ab(0, 22)
    w1b = [c.ab(64, 4), c.ab(68, 4)]
    w3b = [c.ab(72, 4), c.ab(76, 4)]
    w2b = [c.W2SB[:, 0:2816], c.W2SB[:, 2816:5632]]

    for i in range(4):
        k.dma("gpsimd", sview(c.ab(8 * i, 8), 1024),
              d["x"][i * 1024:(i + 1) * 1024, :].rearrange("(c d) t -> d c t", d=128), "x")
    xdone = k.cnt["x"]

    pe_stats = stat_rows(k, c, xbf, DC, lambda ci: [("x", xdone)])
    ln_done = ln_apply(k, c, xbf, xn, DC, pe_stats, 0)
    if stage <= 0:
        for i in range(4):
            k.dma("sync", d["xn1"][i * 1024:(i + 1) * 1024, :].rearrange("(c d) t -> d c t", d=128),
                  sview(c.ab(32 + 8 * i, 8), 1024), sOut, waits=[(sDVE, ln_done)])
        return

    xnA = {"a1t": c.L16[:, 0:1024], "a3t": c.L16[:, 1024:2048]}
    a_evac = {}
    for ni, nm in enumerate(("a1t", "a3t")):
        k.dma("gpsimd", sview(c.W2SB[:, ni * 512:(ni + 1) * 512], 16),
              wview(d[nm], 0, DC, 0, LR), "lw", waits=[(sPE, pe_stats)] if ni == 0 else [])
    lwc = k.cnt["lw"]
    prev_evac = 0
    for ni, nm in enumerate(("a1t", "a3t")):
        pe = None
        for ci in range(DC):
            for h in range(2):
                pe = k.emit("tensor", lambda e, ci=ci, h=h, ni=ni: e.matmul(
                    c.ps[6 + h][0:16, :], c.W2SB[:, ni * 512 + ci * 16:ni * 512 + ci * 16 + 16],
                    xn[:, ci * 1024 + h * 512:ci * 1024 + h * 512 + 512],
                    start=(ci == 0), stop=(ci == DC - 1)),
                    waits=[("lw", lwc), (sDVE, ln_done), (sACT, prev_evac)], inc=sPE)
        ae = None
        for h in range(2):
            ae = k.emit("scalar", lambda e, h=h, nm=nm: e.mul(
                xnA[nm][0:16, h * 512:h * 512 + 512], c.ps[6 + h][0:16, :], LSC),
                waits=[(sPE, pe)], inc=sACT)
        a_evac[nm] = ae
        prev_evac = ae

    hc_base = 0
    pe_lora2 = {}
    cons_hc = {}
    w2_cons = {}
    lora2_evac = None
    w2_rmw = {}
    part_f32 = c.af(1, 1)
    silu_scr = c.af(2, 1)
    x_f32 = c.af(3, 1)
    prev_part_user = {}
    pe_stat_mm = {}
    x1_store_last = None
    last_hc_pe = None

    for s, ssz in enumerate(SEC_SIZES):
        for j in range(ssz):
            hc = hc_base + j
            slot = hc % 2
            wsem = f"wh{slot}"
            gate = [(sPE, pe_lora2.get(hc - 2, 0))]
            if hc < 2:
                gate.append((sPE, pe_stats))
                gate.append((sACT, a_evac["a3t"]))
            k.dma("gpsimd", w1b[slot], d["w1l"][hc].rearrange("d c o -> d (c o)"),
                  wsem, waits=gate)
            k.dma("gpsimd", w3b[slot], d["w3l"][hc].rearrange("d c o -> d (c o)"), wsem)
            k.dma("gpsimd", c.L16[:, 3328 + slot * 256:3328 + slot * 256 + 128],
                  d["b1t"][:, hc * 128:hc * 128 + 128], wsem)
            k.dma("gpsimd", c.L16[:, 3328 + slot * 256 + 128:3328 + slot * 256 + 256],
                  d["b3t"][:, hc * 128:hc * 128 + 128], wsem)
            k.dma("gpsimd", c.SMB[:, 128 + slot * 16:128 + slot * 16 + 16],
                  d["a2t"][hc * 128:hc * 128 + 128, :], wsem)
            wdone = k.cnt[wsem]
            pe = None
            ag = asl = None
            cons = cons_hc.get(hc - 1, ((0, 0), (0, 0)))
            for h in range(2):
                for ci in range(DC):
                    k.emit("tensor", lambda e, ci=ci, h=h, slot=slot: e.matmul(
                        c.ps[0 + h][:, :], w1b[slot][:, ci * 128:ci * 128 + 128],
                        xn[:, ci * 1024 + h * 512:ci * 1024 + h * 512 + 512],
                        start=(ci == 0), stop=False),
                        waits=([(wsem, wdone), (sDVE, ln_done),
                                (sDVE, cons[h][0]), (sACT, cons[h][1])]
                               if ci == 0 else []), inc=sPE)
                k.emit("tensor", lambda e, h=h, slot=slot: e.matmul(
                    c.ps[0 + h][:, :], c.L16[0:16, 3328 + slot * 256:3328 + slot * 256 + 128],
                    c.L16[0:16, 0 + h * 512:0 + h * 512 + 512], start=False, stop=True),
                    waits=[(sACT, a_evac["a1t"])], inc=sPE)
                for ci in range(DC):
                    k.emit("tensor", lambda e, ci=ci, h=h, slot=slot: e.matmul(
                        c.ps[2 + h][:, :], w3b[slot][:, ci * 128:ci * 128 + 128],
                        xn[:, ci * 1024 + h * 512:ci * 1024 + h * 512 + 512],
                        start=(ci == 0), stop=False), inc=sPE)
                pe = k.emit("tensor", lambda e, h=h, slot=slot: e.matmul(
                    c.ps[2 + h][:, :], c.L16[0:16, 3328 + slot * 256 + 128:3328 + slot * 256 + 256],
                    c.L16[0:16, 1024 + h * 512:1024 + h * 512 + 512], start=False, stop=True),
                    waits=[(sACT, a_evac["a3t"])], inc=sPE)
                asl = k.emit("scalar", lambda e, h=h: e.activation(
                    silu_scr[:, h * 512:h * 512 + 512], c.ps[0 + h][:, :], AF.Silu),
                    waits=[(sPE, pe)], inc=sACT)
                ag = k.emit("vector", lambda e, h=h, j=j: e.tensor_mul(
                    gsec[:, j * 1024 + h * 512:j * 1024 + h * 512 + 512],
                    silu_scr[:, h * 512:h * 512 + 512], c.ps[2 + h][:, :]),
                    waits=[(sACT, asl)], inc=sDVE)
                if h == 0:
                    cons_h0 = (ag, asl)
            cons_hc[hc] = (cons_h0, (ag, asl))
            last_hc_pe = pe
            for h in range(2):
                pe_lora2[hc] = k.emit("tensor", lambda e, h=h, j=j, slot=slot, hc=hc: e.matmul(
                    c.ps[6 + h][0:16, :], c.SMB[:, 128 + slot * 16:128 + slot * 16 + 16],
                    gsec[:, j * 1024 + h * 512:j * 1024 + h * 512 + 512],
                    start=(hc == 0), stop=(hc == HCN - 1), skip_group_check=True),
                    waits=[(sDVE, ag)], inc=sPE)

        for ot in range(DC):
            slot = ot % 2
            w2sem = f"w2{slot}"
            gate = [(sDVE, w2_cons.get((s, ot - 2), (0, 0))[0]),
                    (sACT, w2_cons.get((s, ot - 2), (0, 0))[1])]
            if s == 0 and ot < 2:
                gate.append((sACT, a_evac["a3t"]))
            k.dma("gpsimd", w2b[slot][:, 0:ssz * 128],
                  d["w2l"][ot][:, hc_base:hc_base + ssz, :].rearrange("d c o -> d (c o)"),
                  w2sem, waits=gate)
            if s == len(SEC_SIZES) - 1:
                k.dma("gpsimd", c.L16[:, 3072 + slot * 128:3072 + slot * 128 + 128],
                      d["b2t"][:, ot * 128:(ot + 1) * 128], w2sem)
            w2done = k.cnt[w2sem]
            pe = None
            for h in range(2):
                for jj in range(ssz):
                    pe = k.emit("tensor", lambda e, jj=jj, h=h, slot=slot, ssz=ssz: e.matmul(
                        c.ps[4 + h][:, :], w2b[slot][:, jj * 128:jj * 128 + 128],
                        gsec[:, jj * 1024 + h * 512:jj * 1024 + h * 512 + 512],
                        start=(jj == 0), stop=(jj == ssz - 1)),
                        waits=([(w2sem, w2done), (sDVE, w2_cons.get((s, ot - 1), (0, 0))[0]),
                                (sACT, w2_cons.get((s, ot - 1), (0, 0))[1])]
                               if (h == 0 and jj == 0) else []), inc=sPE)
            if s < len(SEC_SIZES) - 1:
                cpb = [c.af(0, 1), c.SINK[:, 0:1024]][ot % 2]
                dvc2 = None
                for h in range(2):
                    dvc2 = k.emit("vector", lambda e, h=h, cpb=cpb: e.tensor_copy(
                        cpb[:, h * 512:h * 512 + 512], c.ps[4 + h][:, :]),
                        waits=[(sPE, pe)] + ([(sDVE, w2_rmw.get((s, ot - 2), 0)),
                                              (sOut, k.cnt[sOut] if s == 0 and ot >= 2 else 0)]
                                             if h == 0 else []),
                        inc=sDVE)
                dv = dvc2
                if s > 0:
                    pl = k.dma("sync", part_f32, d["part"][ot * 128:(ot + 1) * 128, :], "pp",
                               waits=[(sOut, k.cnt[sOut])])
                    dv = k.emit("vector", lambda e, cpb=cpb: e.tensor_add(
                        part_f32, part_f32, cpb),
                        waits=[(sDVE, dvc2), ("pp", k.cnt["pp"])], inc=sDVE)
                    k.dma("sync", d["part"][ot * 128:(ot + 1) * 128, :], part_f32, sOut,
                          waits=[(sDVE, dv)])
                else:
                    k.dma("sync", d["part"][ot * 128:(ot + 1) * 128, :], cpb, sOut,
                          waits=[(sDVE, dvc2)])
                w2_rmw[(s, ot)] = dv
                w2_cons[(s, ot)] = (dvc2, 0)
            else:
                if lora2_evac is None:
                    for h in range(2):
                        lora2_evac = k.emit("scalar", lambda e, h=h: e.mul(
                            c.L16[:, 2048 + h * 512:2048 + h * 512 + 512],
                            c.ps[6 + h][0:16, :], LSC),
                            waits=[(sPE, pe_lora2[HCN - 1])], inc=sACT)
                for h in range(2):
                    pe = k.emit("tensor", lambda e, h=h, slot=slot: e.matmul(
                        c.ps[4 + h][:, :], c.L16[0:16, 3072 + slot * 128:3072 + slot * 128 + 128],
                        c.L16[0:16, 2048 + h * 512:2048 + h * 512 + 512],
                        start=False, stop=True, skip_group_check=True),
                        waits=[(sACT, lora2_evac)], inc=sPE)
                pl = k.dma("sync", part_f32, d["part"][ot * 128:(ot + 1) * 128, :], "pp",
                           waits=[(sACT, prev_part_user.get(ot - 1, 0)),
                                  (sOut, k.cnt[sOut])])
                xl = k.dma("sync", x_f32, d["x"][ot * 128:(ot + 1) * 128, :], "px",
                           waits=[(sDVE, w2_cons.get((s, ot - 1), (0, 0))[0])])
                dv = None
                for h in range(2):
                    hs = slice(h * 512, h * 512 + 512)
                    dv = k.emit("vector", lambda e, h=h, hs=hs: e.tensor_add(
                        part_f32[:, hs], part_f32[:, hs], c.ps[4 + h][:, :]),
                        waits=[(sPE, pe)] + ([("pp", k.cnt["pp"])] if h == 0 else []), inc=sDVE)
                ag1 = k.emit("scalar", lambda e: e.activation(
                    silu_scr, part_f32, AF.Silu), waits=[(sDVE, dv)], inc=sACT)
                prev_part_user[ot] = ag1
                dv = k.emit("vector", lambda e, ot=ot: e.tensor_mul(
                    silu_scr, xn[:, ot * 1024:(ot + 1) * 1024], silu_scr),
                    waits=[(sACT, ag1)], inc=sDVE)
                dv = k.emit("vector", lambda e: e.tensor_add(
                    silu_scr, silu_scr, x_f32),
                    waits=[("px", k.cnt["px"])], inc=sDVE)
                x1_store_last = k.dma("sync", d["x1"][ot * 128:(ot + 1) * 128, :],
                                      silu_scr, sOut, waits=[(sDVE, dv)])
                par2 = ot % 2
                bq = k.emit("vector", lambda e, par2=par2: e.tensor_copy(
                    c.ab(64 + par2, 1), silu_scr),
                    waits=[(sPE, pe_stat_mm.get(ot - 2, 0))], inc=sDVE)
                sqa = k.emit("scalar", lambda e, par2=par2: e.activation(
                    c.ab(66 + par2, 1), c.ab(64 + par2, 1), AF.Square),
                    waits=[(sDVE, bq)] + ([(sACT, lora2_evac)] if ot == 0 else []),
                    inc=sACT)
                pstat = None
                for h in range(2):
                    k.emit("tensor", lambda e, h=h, par2=par2, ot=ot: e.matmul(
                        c.ps[2 + h][0:1, :], c.ones_col,
                        c.ab(64 + par2, 1)[:, h * 512:h * 512 + 512],
                        start=(ot == 0), stop=(ot == DC - 1)),
                        waits=[(sDVE, bq), (sIn, c.const_done)], inc=sPE)
                for h in range(2):
                    pstat = k.emit("tensor", lambda e, h=h, par2=par2, ot=ot: e.matmul(
                        c.ps[6 + h][0:1, :], c.ones_col,
                        c.ab(66 + par2, 1)[:, h * 512:h * 512 + 512],
                        start=(ot == 0), stop=(ot == DC - 1)),
                        waits=[(sACT, sqa)], inc=sPE)
                pe_stat_mm[ot] = pstat
                w2_cons[(s, ot)] = (dv, ag1)
        hc_base += ssz
        if stage <= 1 + s:
            return

    if stage <= 4:
        return
    xn1 = c.ab(32, 32)
    rm = c.ROWS[0:1, 0:1024]
    ra = c.ROWS[0:1, 1024:2048]
    rt = c.SINK[0:1, 0:1024]
    pe_stats1 = k.cnt[sPE]
    for h in range(2):
        k.emit("vector", lambda e, h=h: e.tensor_copy(rm[:, h * 512:h * 512 + 512],
                                                      c.ps[2 + h][0:1, :]),
               waits=[(sPE, pe_stats1)], inc=sDVE)
    for h in range(2):
        k.emit("vector", lambda e, h=h: e.tensor_copy(rt[:, h * 512:h * 512 + 512],
                                                      c.ps[6 + h][0:1, :]), inc=sDVE)
    k.emit("vector", lambda e: e.tensor_scalar_mul(rm, rm, 1.0 / D), inc=sDVE)
    k.emit("vector", lambda e: e.tensor_scalar_mul(rt, rt, 1.0 / D), inc=sDVE)
    k.emit("vector", lambda e: e.tensor_mul(ra, rm, rm), inc=sDVE)
    dv = k.emit("vector", lambda e: e.tensor_sub(rt, rt, ra), inc=sDVE)
    a1 = k.emit("scalar", lambda e: e.activation(ra, rt, AF.Sqrt, bias=c.eps_ln),
                waits=[(sDVE, dv), (sIn, c.const_done)], inc=sACT)
    dv = k.emit("vector", lambda e: e.reciprocal(ra, ra), waits=[(sACT, a1)], inc=sDVE)
    k.wait("vector", sDVE, dv)
    dv = k.emit("vector", lambda e: e.scalar_tensor_tensor(rm, rm, -1.0, ra, ALU.mult, ALU.mult),
                inc=sDVE)
    pe_rep = None
    for h in range(2):
        k.emit("tensor", lambda e, h=h: e.matmul(
            c.ps[0 + h][:, :], c.ones_row, ra[:, h * 512:h * 512 + 512], start=True, stop=True),
            waits=[(sDVE, dv)], inc=sPE)
    for h in range(2):
        pe_rep = k.emit("tensor", lambda e, h=h: e.matmul(
            c.ps[2 + h][:, :], c.ones_row, rm[:, h * 512:h * 512 + 512], start=True, stop=True),
            inc=sPE)
    x1c = [c.af(1, 1), c.af(3, 1)]
    ln1_done = None
    apl = {}
    for ci in range(DC):
        par = ci % 2
        k.dma("sync", x1c[par], d["x1"][ci * 128:(ci + 1) * 128, :], "x1r",
              waits=([(sOut, k.cnt[sOut])] if ci < 2 else
                     [(sDVE, apl.get(ci - 2, 0))]))
        x1r = k.cnt["x1r"]
        for h in range(2):
            sl = slice(ci * 1024 + h * 512, ci * 1024 + h * 512 + 512)
            k.emit("vector", lambda e, par=par, h=h: e.tensor_mul(
                c.af(0, 1)[:, 0:512], x1c[par][:, h * 512:h * 512 + 512], c.ps[0 + h][:, :]),
                waits=[(sPE, pe_rep), ("x1r", x1r)], inc=sDVE)
            ln1_done = k.emit("vector", lambda e, sl=sl, h=h: e.tensor_add(
                xn1[:, sl], c.af(0, 1)[:, 0:512], c.ps[2 + h][:, :]), inc=sDVE)
        apl[ci] = ln1_done
    c.apl = apl
    for i in range(4):
        k.dma("sync", d["xn1"][i * 1024:(i + 1) * 1024, :]
              .rearrange("(c d) t -> d c t", d=128),
              sview(c.ab(32 + 8 * i, 8), 1024), sOut,
              waits=[(sDVE, ln1_done)])
    c.ln1_done = ln1_done
    c.xn1 = xn1
def emit_p2(nc, k, stack, c, load_xn1=True, core=None, stage2=99, part="all"):
    d = c.d
    xn1 = c.ab(32, 32)
    gqa = c.SM[:, 131:143]
    gkva = c.SM[:, 143:147]
    Rt = c.SMB[0:64, 16:80]
    ld2 = [
        k.dma("sync", c.SM[:, 131:143], d["gqa"], sIn),
        k.dma("sync", c.SM[:, 143:147], d["gkva"], sIn),
        k.dma("sync", c.SMB[0:64, 16:80], d["rt"], sIn),
        k.dma("sync", c.AF32[0:64, 1024:2048], d["cos"], sIn),
        k.dma("sync", c.AF32[0:64, 2048:3072], d["sin"], sIn),
    ]
    cos_sb = c.AF32[0:64, 1024:2048]
    sin_sb = c.AF32[0:64, 2048:3072]
    p2c = k.cnt[sIn]
    c.const_done = p2c

    if load_xn1:
        for i in range(4):
            k.dma("sync", sview(c.ab(32 + 8 * i, 8), 1024),
                  d["xn1"][i * 1024:(i + 1) * 1024, :].rearrange("(c d) t -> d c t", d=128),
                  "x")
    xn1done = k.cnt["x"]
    if stage2 <= -5:
        k.wait("gpsimd", "x", xn1done)
        k.wait("sync", sIn, k.cnt[sIn])
        return

    wslt = [c.ab(64, 4), c.ab(68, 4)]
    sqscr = c.af(3, 1)
    def rms_rows(pe_stat, denom, r0):
        rr = c.ROWS[0:1, r0:r0 + 1024]
        dvc = None
        for h in range(2):
            dvc = k.emit("vector", lambda e, h=h: e.tensor_copy(rr[:, h * 512:h * 512 + 512],
                                                                c.ps[6 + h][0:1, :]),
                         waits=[(sPE, pe_stat)], inc=sDVE)
        sc = k.emit("scalar", lambda e: e.activation(rr, rr, AF.Sqrt, scale=1.0 / denom),
                    waits=[(sDVE, dvc)], inc=sACT)
        dv = k.emit("vector", lambda e: e.tensor_scalar_add(rr, rr, 1e-6), waits=[(sACT, sc)], inc=sDVE)
        dv = k.emit("vector", lambda e: e.reciprocal(rr, rr), inc=sDVE)
        k.wait("vector", sDVE, dv)
        pe_rep = None
        for h in range(2):
            pe_rep = k.emit("tensor", lambda e, h=h: e.matmul(
                c.ps[0 + h][:, :], c.ones_row, rr[:, h * 512:h * 512 + 512],
                start=True, stop=True), waits=[(sDVE, dv)], inc=sPE)
        return pe_rep

    kcn = c.ab(4, 8)[:, 0:4 * 1024]
    krb = c.ab(8, 1)[0:64, :]
    kev = {}
    pe_kv = {}
    for ot in range(5):
        par = ot % 2
        wsem = f"wh{par}"
        nr = 128 if ot < 4 else 64
        k.dma("gpsimd", wslt[par][:, 0:DC * nr],
              (d["wkval"][ot].rearrange("d c o -> d (c o)") if ot < 4
               else d["wkvrl"].rearrange("d c o -> d (c o)")), wsem,
              waits=[(sPE, pe_kv.get(ot - 2, 0))])
        wdone = k.cnt[wsem]
        apl = getattr(c, "apl", {})
        pe = None
        for h in range(2):
            for ci in range(DC):
                pe = k.emit("tensor", lambda e, ci=ci, h=h, par=par, nr=nr: e.matmul(
                    c.ps[0 + h][0:nr, :], wslt[par][:, ci * nr:ci * nr + nr],
                    xn1[:, ci * 1024 + h * 512:ci * 1024 + h * 512 + 512],
                    start=(ci == 0), stop=(ci == DC - 1)),
                    waits=([(wsem, wdone), (sACT, kev.get(ot - 1, (0, 0))[0] or 0),
                            (sDVE, kev.get(ot - 1, (0, 0))[1]),
                            (sDVE, apl.get(0, 0) if ot == 0 else 0)]
                           if (h == 0 and ci == 0) else
                           ([(sDVE, apl.get(ci, 0))] if (h == 0 and ot == 0) else [])),
                    inc=sPE)
        pe_kv[ot] = pe
        aq = dq = None
        sqb = c.ab(24 + par, 1)
        for h in range(2):
            if ot < 4:
                dq = k.emit("vector", lambda e, h=h, ot=ot: e.tensor_copy(
                    kcn[:, ot * 1024 + h * 512:ot * 1024 + h * 512 + 512], c.ps[0 + h][:, :]),
                    waits=[(sPE, pe)], inc=sDVE)
                aq = k.emit("scalar", lambda e, h=h, ot=ot, sqb=sqb: e.activation(
                    sqb[:, h * 512:h * 512 + 512],
                    kcn[:, ot * 1024 + h * 512:ot * 1024 + h * 512 + 512], AF.Square),
                    waits=[(sDVE, dq)], inc=sACT)
            else:
                dq = k.emit("vector", lambda e, h=h: e.tensor_copy(
                    krb[:, h * 512:h * 512 + 512], c.ps[0 + h][0:64, :]),
                    waits=[(sPE, pe)], inc=sDVE)
        if ot < 4:
            for h in range(2):
                k.emit("tensor", lambda e, h=h, ot=ot, sqb=sqb: e.matmul(
                    c.ps[6 + h][0:1, :], c.ones_col, sqb[:, h * 512:h * 512 + 512],
                    start=(ot == 0), stop=(ot == 3)),
                    waits=[(sACT, aq)], inc=sPE)
        kev[ot] = (aq if aq else 0, dq)
    pe_kstat = k.cnt[sPE]
    pe_rep = rms_rows(pe_kstat, KVR, 1024)
    dv_kcn = None
    for ot in range(4):
        for h in range(2):
            sl = slice(ot * 1024 + h * 512, ot * 1024 + h * 512 + 512)
            k.emit("vector", lambda e, sl=sl, h=h: e.tensor_mul(
                sqscr[:, h * 512:h * 512 + 512], kcn[:, sl], c.ps[0 + h][:, :]),
                waits=[(sPE, pe_rep)], inc=sDVE)
            dv_kcn = k.emit("vector", lambda e, sl=sl, ot=ot, h=h: e.tensor_scalar(
                kcn[:, sl], sqscr[:, h * 512:h * 512 + 512], gkva[:, ot:ot + 1], None, ALU.mult),
                inc=sDVE)

    pe_rot = None
    for h in range(2):
        pe_rot = k.emit("tensor", lambda e, h=h: e.matmul(
            c.ps[2][0:64, :] if h == 0 else c.ps[3][0:64, :], Rt,
            krb[:, h * 512:h * 512 + 512], start=True, stop=True),
            waits=[(sDVE, kev[4][1]), (sIn, p2c)], inc=sPE)
    dv = None
    for h in range(2):
        hs = slice(h * 512, h * 512 + 512)
        k.emit("vector", lambda e, h=h, hs=hs: e.tensor_mul(
            c.af(3, 1)[0:64, hs], c.ps[2 + h][0:64, :], sin_sb[:, hs]),
            waits=[(sPE, pe_rot)], inc=sDVE)
        k.emit("vector", lambda e, h=h, hs=hs: e.tensor_mul(
            c.af(0, 1)[0:64, hs], krb[:, hs], cos_sb[:, hs]), inc=sDVE)
        dv = k.emit("vector", lambda e, h=h, hs=hs: e.tensor_add(
            c.ab(9, 1)[0:64, hs], c.af(0, 1)[0:64, hs], c.af(3, 1)[0:64, hs]), inc=sDVE)
    k.dma("sync", d["krc"], c.ab(9, 1)[0:64, :], "ct0", waits=[(sDVE, dv)])

    if stage2 <= -1:
        return
    stg2 = [c.ab(10, 1), c.ab(11, 1)]
    pe_kn = {}
    st_kn = {}
    for hh in range(NH):
        par = hh % 2
        wsem = f"wh{par}"
        k.dma("gpsimd", wslt[par][:, 0:KVRC * 128],
              d["wknl"][hh].rearrange("d c o -> d (c o)"), wsem,
              waits=[(sPE, pe_kn.get(hh - 2, pe_rot if hh < 2 else 0))])
        wdone = k.cnt[wsem]
        pe = None
        for h in range(2):
            for ci in range(KVRC):
                pe = k.emit("tensor", lambda e, ci=ci, h=h, par=par: e.matmul(
                    c.ps[0 + h][:, :], wslt[par][:, ci * 128:ci * 128 + 128],
                    kcn[:, ci * 1024 + h * 512:ci * 1024 + h * 512 + 512],
                    start=(ci == 0), stop=(ci == KVRC - 1)),
                    waits=([(wsem, wdone), (sDVE, dv_kcn),
                            (sDVE, st_kn.get(hh - 1, (0, 0))[1])]
                           if (h == 0 and ci == 0) else []), inc=sPE)
        pe_kn[hh] = pe
        dv = None
        for h in range(2):
            dv = k.emit("vector", lambda e, h=h, par=par: e.tensor_copy(
                stg2[par][:, h * 512:h * 512 + 512], c.ps[0 + h][:, :]),
                waits=[(sPE, pe), (f"ct{par}", k.cnt[f"ct{par}"])], inc=sDVE)
        sid = k.dma("sync", d["knc"][hh * 128:(hh + 1) * 128, :], stg2[par], f"ct{par}",
                    waits=[(sDVE, dv)])
        st_kn[hh] = (sid, dv)

    pe_v = {}
    st_v = {}
    for hg in range(8):
        par = hg % 2
        wsem = f"wh{par}"
        k.dma("gpsimd", wslt[par][:, 0:4 * 512],
              d["wvl"][hg].rearrange("d c o -> d (c o)"), wsem,
              waits=[(sPE, pe_v.get((hg - 2, 7), pe_kn.get(31, 0) if hg < 2 else 0))])
        wdone = k.cnt[wsem]
        for ts in range(8):
            pe = None
            for ci in range(KVRC):
                pe = k.emit("tensor", lambda e, ci=ci, par=par, ts=ts: e.matmul(
                    c.ps[2 + ts % 2][:, :],
                    kcn[:, ci * 1024 + ts * 128:ci * 1024 + ts * 128 + 128],
                    wslt[par][:, ci * 512:(ci + 1) * 512],
                    start=(ci == 0), stop=(ci == KVRC - 1)),
                    waits=([(wsem, wdone), (sDVE, st_v.get((hg, ts - 2),
                            st_v.get((hg - 1, 6 + ts % 2), (0, 0)))[1])]
                           if ci == 0 else []), inc=sPE)
            pe_v[(hg, ts)] = pe
            dv = k.emit("vector", lambda e, par=par, ts=ts: e.tensor_copy(
                stg2[ts % 2][:, 0:512], c.ps[2 + ts % 2][:, :]),
                waits=[(sPE, pe), (f"ct{ts % 2}", k.cnt[f"ct{ts % 2}"])], inc=sDVE)
            sid = k.dma("sync", d["vc"][ts * 128:(ts + 1) * 128, hg * 512:(hg + 1) * 512],
                        stg2[ts % 2][:, 0:512], f"ct{ts % 2}", waits=[(sDVE, dv)])
            st_v[(hg, ts)] = (sid, dv)

    k.wait("gpsimd", "ct0", k.cnt["ct0"])
    k.wait("gpsimd", "ct1", k.cnt["ct1"])
    rg = [list(range(NCORES))]
    k.emit("gpsimd", lambda e: e.collective_compute(
        "AllGather", ALU.bypass, replica_groups=rg,
        ins=[d["knc"].opt()], outs=[d["kng"].opt()]), inc="cc")
    k.emit("gpsimd", lambda e: e.collective_compute(
        "AllGather", ALU.bypass, replica_groups=rg,
        ins=[d["vc"].opt()], outs=[d["vg"].opt()]), inc="cc")
    cc_done = k.emit("gpsimd", lambda e: e.collective_compute(
        "AllGather", ALU.bypass, replica_groups=rg,
        ins=[d["krc"].opt()], outs=[d["krg"].opt()]), inc="cc")
    if c.overlap_hook is not None:
        c.overlap_hook()

    qc = c.ab(0, 12)
    wslt = [c.ab(64, 4), c.ab(68, 4)]
    sqscr = c.af(3, 1)
    evq = {}
    pe_q = {}
    dve_hmat = k.cnt[sDVE]
    for ot in range(QRC):
        par = ot % 2
        wsem = f"wh{par}"
        gate = [(sPE, pe_q.get(ot - 2, pe_v.get((7, 7), 0) if ot < 2 else 0))]
        k.dma("gpsimd", wslt[par], d["wqal"][ot].rearrange("d c o -> d (c o)"),
              wsem, waits=gate)
        wdone = k.cnt[wsem]
        pe = None
        for h in range(2):
            for ci in range(DC):
                pe = k.emit("tensor", lambda e, ci=ci, h=h, par=par: e.matmul(
                    c.ps[0 + h][:, :], wslt[par][:, ci * 128:ci * 128 + 128],
                    xn1[:, ci * 1024 + h * 512:ci * 1024 + h * 512 + 512],
                    start=(ci == 0), stop=(ci == DC - 1)),
                    waits=([(wsem, wdone), ("x", xn1done),
                            (sDVE, dve_hmat if ot == 0 else 0),
                            (sACT, evq.get(ot - 1, (0, 0))[0]),
                            (sDVE, evq.get(ot - 1, (0, 0))[1])]
                           if (h == 0 and ci == 0) else []), inc=sPE)
        pe_q[ot] = pe
        aq = dq = None
        sqb = c.W2SB[:, 1024:2048]
        for h in range(2):
            dq = k.emit("vector", lambda e, h=h, ot=ot: e.tensor_copy(
                qc[:, ot * 1024 + h * 512:ot * 1024 + h * 512 + 512], c.ps[0 + h][:, :]),
                waits=[(sPE, pe)], inc=sDVE)
            aq = k.emit("scalar", lambda e, h=h, ot=ot, sqb=sqb: e.activation(
                sqb[:, h * 512:h * 512 + 512],
                qc[:, ot * 1024 + h * 512:ot * 1024 + h * 512 + 512], AF.Square),
                waits=[(sDVE, dq)], inc=sACT)
        for h in range(2):
            k.emit("tensor", lambda e, h=h, ot=ot, sqb=sqb: e.matmul(
                c.ps[6 + h][0:1, :], c.ones_col, sqb[:, h * 512:h * 512 + 512],
                start=(ot == 0), stop=(ot == QRC - 1)),
                waits=[(sACT, aq), (sIn, c.const_done)], inc=sPE)
        evq[ot] = (aq, dq)
    pe_qstat = k.cnt[sPE]
    if stage2 <= -4:
        k.wait("tensor", sPE, pe_qstat)
        return

    pe_rep = rms_rows(pe_qstat, QR, 0)
    qcn = c.ab(12, 12)
    dv_qcn = None
    for ot in range(QRC):
        for h in range(2):
            sl = slice(ot * 1024 + h * 512, ot * 1024 + h * 512 + 512)
            k.emit("vector", lambda e, sl=sl, h=h: e.tensor_mul(
                sqscr[:, h * 512:h * 512 + 512], qc[:, sl], c.ps[0 + h][:, :]),
                waits=[(sPE, pe_rep), (sIn, p2c)], inc=sDVE)
            dv_qcn = k.emit("vector", lambda e, sl=sl, ot=ot, h=h: e.tensor_scalar(
                qcn[:, sl], sqscr[:, h * 512:h * 512 + 512], gqa[:, ot:ot + 1], None, ALU.mult),
                inc=sDVE)

    if stage2 <= -3:
        return
    stg = [c.ab(0, 1), c.ab(1, 1)]
    pe_qb = {}
    st_q = {}
    for ot in range(48):
        par = ot % 2
        wsem = f"wh{par}"
        k.dma("gpsimd", wslt[par][:, 0:QRC * 128],
              d["wqbl"][ot].rearrange("d c o -> d (c o)"), wsem,
              waits=[(sPE, pe_qb.get(ot - 2, pe_rep if ot < 2 else 0))])
        wdone = k.cnt[wsem]
        pe = None
        for h in range(2):
            for ci in range(QRC):
                pe = k.emit("tensor", lambda e, ci=ci, h=h, par=par: e.matmul(
                    c.ps[2 + h][:, :], wslt[par][:, ci * 128:ci * 128 + 128],
                    qcn[:, ci * 1024 + h * 512:ci * 1024 + h * 512 + 512],
                    start=(ci == 0), stop=(ci == QRC - 1)),
                    waits=([(wsem, wdone), (sDVE, dv_qcn),
                            (sDVE, st_q.get(ot - 1, (0, 0))[1])]
                           if (h == 0 and ci == 0) else []), inc=sPE)
        pe_qb[ot] = pe
        dv = None
        for h in range(2):
            dv = k.emit("vector", lambda e, h=h, par=par: e.tensor_copy(
                stg[par][:, h * 512:h * 512 + 512], c.ps[2 + h][:, :]),
                waits=[(sPE, pe), ("sq", k.cnt["sq"])], inc=sDVE)
        sq_id = k.dma("sync", d["q"][ot * 128:(ot + 1) * 128, :], stg[par], "sq",
                      waits=[(sDVE, dv)])
        st_q[ot] = (sq_id, dv)
    q_done = k.cnt["sq"]

    if stage2 <= -2:
        return
    if stage2 <= 0:
        return

    c.p2s = dict(q_done=q_done, dvq_const=p2c)
    if part == "proj_stop":
        return
    kh = [c.ab(0, 8), c.ab(8, 8)]
    vh = [c.ab(16, 8), c.ab(24, 8)]
    krh = c.ab(32, 8)[0:64, :]
    qn = [c.W2SB[:, 0:1024], c.W2SB[:, 1024:2048]]
    exps = [c.ab(42, 1), c.ab(43, 1), c.ab(41, 1)]
    ao = c.ab(44, 32)
    qraw = [c.ab(76, 1)[0:64, :], c.ab(77, 1)[0:64, :]]
    qrop = [c.ab(78, 1)[0:64, :], c.ab(79, 1)[0:64, :]]
    lacc = c.af(0, 1)
    rl = c.af(3, 1)
    kng4 = d["kng"].rearrange("(rr hh dd) t -> hh dd rr t", hh=NH, dd=128)
    vg4 = d["vg"].rearrange("(kt kk) (hh dv) -> hh kk kt dv", kk=128, hh=NH)

    k.wait("sync", "cc", cc_done)
    krh_ld = k.dma("sync", sview(krh, 1024), d["krg"].rearrange("(rr dd) t -> dd rr t", dd=64), "krh")
    pe_av = {}
    cons_head = {}
    norm_done = {}
    pe_s_last = {}
    rope_done = {}
    pe_l_done = {}

    def emit_rope(hh, first_waits=()):
        par = hh % 2
        pe_rot = None
        for h in range(2):
            pe_rot = k.emit("tensor", lambda e, h=h, par=par: e.matmul(
                c.ps[6][0:64, :] if h == 0 else c.ps[7][0:64, :], Rt,
                qraw[par][:, h * 512:h * 512 + 512], start=True, stop=True),
                waits=((list(first_waits) + [(sIn, p2c)]) if h == 0 else []), inc=sPE)
        dvq = None
        for h in range(2):
            hs = slice(h * 512, h * 512 + 512)
            psb = c.ps[6] if h == 0 else c.ps[7]
            k.emit("vector", lambda e, hs=hs, psb=psb: e.tensor_mul(
                rl[0:64, hs], psb[0:64, :], sin_sb[:, hs]),
                waits=([(sPE, pe_rot)] + ([(sPE, pe_s_last.get(hh - 2, 0))] if h == 0 else [])),
                inc=sDVE)
            k.emit("vector", lambda e, hs=hs, par=par, psb=psb: e.tensor_mul(
                psb[0:64, :], qraw[par][:, hs], cos_sb[:, hs]), inc=sDVE)
            dvq = k.emit("vector", lambda e, hs=hs, par=par, psb=psb: e.tensor_add(
                qrop[par][:, hs], rl[0:64, hs], psb[0:64, :]), inc=sDVE)
        rope_done[hh] = dvq
        return dvq

    def emit_qload(hh, gate_scores):
        par = hh % 2
        asem = f"at{par}"
        qw = [(sPE, gate_scores)] if gate_scores else []
        if hh == 0:
            qw.append(("sq", q_done))
        k.dma("sync", qn[par], d["q"][hh * QK:hh * QK + 128, :], asem, waits=qw)
        k.dma("sync", qraw[par], d["q"][hh * QK + 128:hh * QK + QK, :], asem)
        return k.cnt[asem]

    q0_done = emit_qload(0, 0)
    for hh in range(NH):
        par = hh % 2
        asem = f"at{par}"
        gate = [(sPE, pe_av.get(hh - 2, 0)), (sDVE, cons_head.get(hh - 2, 0))]
        if hh < 2:
            gate.append(("sq", q_done))
        k.dma("sync", sview(kh[par], 1024), kng4[hh], asem, waits=gate)
        k.dma("sync", vh[par][:, :].rearrange("d (kt dv) -> d kt dv", dv=128), vg4[hh], asem)
        adone = k.cnt[asem]
        if getattr(c, "pending_tail", None) is not None:
            c.pending_tail()
            pe_l_done[hh - 1] = k.cnt[sPE]
            c.pending_tail = None
        if hh == 0:
            dvq = emit_rope(0, first_waits=[(asem, q0_done), (sDVE, k.cnt[sDVE])])
        else:
            dvq = rope_done[hh]
        exp_done = {}
        pool_done = {}
        pe_s = {}
        pe_avk = {}
        ace = k.cnt[sACT]

        def emit_scores(kt):
            sp = kt % 2
            pe = None
            for h in range(2):
                k.emit("tensor", lambda e, h=h, kt=kt, sp=sp, par=par: e.matmul(
                    c.ps[2 * sp + h][:, :], kh[par][:, kt * 128:(kt + 1) * 128],
                    qn[par][:, h * 512:h * 512 + 512], start=True, stop=False),
                    waits=(([(sACT, exp_done.get(kt - 2, ace if kt < 2 else 0)),
                             (sDVE, dvq)]) if h == 0 else []), inc=sPE)
            for h in range(2):
                pe = k.emit("tensor", lambda e, h=h, kt=kt, sp=sp, par=par: e.matmul(
                    c.ps[2 * sp + h][:, :], krh[:, kt * 128:(kt + 1) * 128],
                    qrop[par][:, h * 512:h * 512 + 512], start=False, stop=True),
                    waits=([("krh", krh_ld)] if (hh == 0 and kt == 0 and h == 0) else []),
                    inc=sPE)
            pe_s[kt] = pe
            pe_s_last[hh] = pe

        def emit_exp_pool(kt):
            ae = None
            for h in range(2):
                ae = k.emit("scalar", lambda e, h=h, kt=kt: e.activation(
                    exps[kt % 3][:, h * 512:h * 512 + 512], c.ps[2 * (kt % 2) + h][:, :],
                    AF.Exp, scale=float(QK) ** -0.5),
                    waits=[(sPE, pe_s[kt])] + ([(sPE, pe_avk.get(kt - 3, 0)),
                                                (sPOOL, pool_done.get(kt - 3, (0, 0))[0]),
                                                (sDVE, pool_done.get(kt - 3, (0, 0))[1])]
                           if h == 0 else []),
                    inc=sACT)
            exp_done[kt] = ae
            if kt == 0:
                pd = k.emit("gpsimd", lambda e: e.tensor_copy(lacc[:, 0:512], exps[0][:, 0:512]),
                            waits=[(sACT, ae), (sDVE, dvq),
                                   (sPE, pe_l_done.get(hh - 1, 0))], inc=sPOOL)
                dd_ = k.emit("vector", lambda e: e.tensor_copy(lacc[:, 512:1024], exps[0][:, 512:1024]),
                             waits=[(sACT, ae), (sPE, pe_l_done.get(hh - 1, 0))], inc=sDVE)
            else:
                pd = k.emit("gpsimd", lambda e, kt=kt: e.tensor_add(
                    lacc[:, 0:512], lacc[:, 0:512], exps[kt % 3][:, 0:512]),
                    waits=[(sACT, ae)], inc=sPOOL)
                dd_ = k.emit("vector", lambda e, kt=kt: e.tensor_add(
                    lacc[:, 512:1024], lacc[:, 512:1024], exps[kt % 3][:, 512:1024]),
                    waits=[(sACT, ae)], inc=sDVE)
            pool_done[kt] = (pd, dd_)

        def emit_av(kt):
            pe2 = None
            for h in range(2):
                pe2 = k.emit("tensor", lambda e, h=h, kt=kt, par=par: e.matmul(
                    c.ps[4 + h][:, :], vh[par][:, kt * 128:(kt + 1) * 128],
                    exps[kt % 3][:, h * 512:h * 512 + 512],
                    start=(kt == 0), stop=(kt == KTN - 1)),
                    waits=([(sACT, exp_done[kt])] if h == 0 else []) +
                          ([(sDVE, cons_head.get(hh - 1, 0))] if (kt == 0 and h == 0) else []),
                    inc=sPE)
            pe_avk[kt] = pe2

        emit_scores(0)
        emit_exp_pool(0)
        emit_scores(1)
        emit_exp_pool(1)
        if hh + 1 < NH:
            nq = emit_qload(hh + 1, pe_s_last.get(hh - 1, 0))
            emit_rope(hh + 1, first_waits=[(f"at{(hh + 1) % 2}", nq),
                                           (sDVE, norm_done.get(hh - 1, 0))])
        for kt in range(2, KTN):
            emit_scores(kt)
            emit_exp_pool(kt)
            emit_av(kt - 2)
        emit_av(KTN - 2)
        emit_av(KTN - 1)
        pe_av[hh] = pe_avk[KTN - 1]
        evo = None
        for h in range(2):
            evo = k.emit("vector", lambda e, h=h, hh=hh: e.tensor_copy(
                ao[:, hh * 1024 + h * 512:hh * 1024 + h * 512 + 512],
                c.ps[4 + h][:, :]),
                waits=([(sPE, pe_av[hh])] if h == 0 else []), inc=sDVE)
        cons_head[hh] = evo

        def head_tail(hh=hh, pool_last=pool_done[KTN - 1]):
            pe_l = None
            for h in range(2):
                pe_l = k.emit("tensor", lambda e, h=h: e.matmul(
                    c.ps[6 + h][0:1, :], c.ones_col_f, lacc[:, h * 512:h * 512 + 512],
                    start=True, stop=True),
                    waits=[(sPOOL, pool_last[0]), (sDVE, pool_last[1])]
                          + ([(sDVE, norm_done.get(hh - 1, 0)),
                              (sDVE, rope_done.get(hh + 1, 0))] if h == 0 else []),
                    inc=sPE)
            rr = c.ROWS[0:1, 0:1024]
            dv = None
            for h in range(2):
                dv = k.emit("vector", lambda e, h=h: e.tensor_copy(
                    rr[:, h * 512:h * 512 + 512], c.ps[6 + h][0:1, :]),
                    waits=[(sPE, pe_l)], inc=sDVE)
            dv = k.emit("vector", lambda e: e.reciprocal(rr, rr), inc=sDVE)
            k.wait("vector", sDVE, dv)
            pe_rl = None
            for h in range(2):
                pe_rl = k.emit("tensor", lambda e, h=h: e.matmul(
                    c.ps[6 + h][:, :], c.ones_row, rr[:, h * 512:h * 512 + 512],
                    start=True, stop=True), waits=[(sDVE, dv)], inc=sPE)
            for h in range(2):
                norm_done[hh] = k.emit("vector", lambda e, h=h, hh=hh: e.tensor_mul(
                    ao[:, hh * 1024 + h * 512:hh * 1024 + h * 512 + 512],
                    ao[:, hh * 1024 + h * 512:hh * 1024 + h * 512 + 512],
                    c.ps[6 + h][:, :]),
                    waits=[(sPE, pe_rl)], inc=sDVE)
        c.pending_tail = head_tail

    if stage2 <= 1:
        for i in range(4):
            k.dma("sync", d["s1"][i * 1024:(i + 1) * 1024, :]
                  .rearrange("(c d) t -> d c t", d=128),
                  sview(c.ab(44 + 8 * i, 8), 1024), sOut,
                  waits=[(sDVE, cons_head[NH - 1])])
        return

    if getattr(c, "pending_tail", None) is not None:
        c.pending_tail()
        c.pending_tail = None

    woslt = [c.ab(32, 4), c.ab(36, 4)]
    sstg = [c.af(1, 1), c.af(2, 1)]
    pe_wo = {}
    st_s1 = {}
    for ot in range(DC):
        par = ot % 2
        wsem = f"wh{par}"
        k.dma("gpsimd", woslt[par], d["wol"][ot].rearrange("d c o -> d (c o)"),
              wsem, waits=[(sPE, pe_wo.get(ot - 2, pe_av.get(NH - 1, 0) if ot < 2 else 0))])
        wdone = k.cnt[wsem]
        pe = None
        for h in range(2):
            for ci in range(DC):
                pe = k.emit("tensor", lambda e, ci=ci, h=h, par=par: e.matmul(
                    c.ps[0 + h][:, :], woslt[par][:, ci * 128:ci * 128 + 128],
                    ao[:, ci * 1024 + h * 512:ci * 1024 + h * 512 + 512],
                    start=(ci == 0), stop=(ci == DC - 1)),
                    waits=([(wsem, wdone), (sDVE, norm_done[NH - 1]),
                            (sACT, k.cnt[sACT] if ot == 0 else 0),
                            (sDVE, st_s1.get(ot - 1, (0, 0))[1])]
                           if (h == 0 and ci == 0) else []), inc=sPE)
        pe_wo[ot] = pe
        dv = None
        for h in range(2):
            dv = k.emit("vector", lambda e, h=h, par=par: e.tensor_copy(
                sstg[par][:, h * 512:h * 512 + 512], c.ps[0 + h][:, :]),
                waits=[(sPE, pe), ("so", k.cnt["so"])], inc=sDVE)
        sid = k.dma("sync", d["s1"][ot * 128:(ot + 1) * 128, :], sstg[par], "so",
                    waits=[(sDVE, dv)])
        st_s1[ot] = (sid, dv)
    c.s1_done = k.cnt["so"]
    c.att_done_pe = pe_av.get(NH - 1, 0)
    c.att_done_dve = norm_done.get(NH - 1, 0)
    c.wo_done_pe = pe_wo.get(DC - 1, 0)


def build_p2(stage2=99):
    nc = bass.Bass("TRN2", target_bir_lowering=False)
    c = Ctx()
    c.d = {}
    d = c.d
    dr = lambda name, shape, dt, kind: nc.dram_tensor(name, shape, dt, kind=kind).ap()
    d["xn1"] = dr("xn1", [D, T], BF16, "ExternalInput")
    for nm, sh in (("wqal", [QRC, 128, DC, 128]), ("wqbl", [48, 128, QRC, 128]),
                   ("wkval", [4, 128, DC, 128]), ("wkvrl", [128, DC, 64]),
                   ("wknl", [NH, 128, KVRC, 128]), ("wvl", [8, 128, KVRC, 512]),
                   ("wol", [DC, 128, DC, 128])):
        d[nm] = dr(nm, sh, F32, "ExternalInput")
    d["gqa"] = dr("gqa", [128, QRC], F32, "ExternalInput")
    d["gkva"] = dr("gkva", [128, KVRC], F32, "ExternalInput")
    d["rt"] = dr("rt", [RD, RD], BF16, "ExternalInput")
    d["cos"] = dr("cos", [RD, T], F32, "ExternalInput")
    d["sin"] = dr("sin", [RD, T], F32, "ExternalInput")
    d["ones_row"] = dr("ones_row", [1, 128], F32, "ExternalInput")
    d["ones_col"] = dr("ones_col", [128, 1], BF16, "ExternalInput")
    d["ones_col_f"] = dr("ones_col_f", [128, 1], F32, "ExternalInput")
    d["misc_f"] = dr("misc_f", [1, 2], F32, "ExternalInput")
    d["s1"] = dr("s1", [D, T], F32, "ExternalOutput")
    d["q"] = dr("q", [NH * QK, T], BF16, "Internal")
    d["knc"] = dr("knc", [NH * 128, T], BF16, "Internal")
    d["vc"] = dr("vc", [T, NH * VD], BF16, "Internal")
    d["krc"] = dr("krc", [RD, T], BF16, "Internal")
    d["kng"] = dr("kng", [NCORES * NH * 128, T], BF16, "Internal")
    d["vg"] = dr("vg", [SEQ, NH * VD], BF16, "Internal")
    d["krg"] = dr("krg", [NCORES * RD, T], BF16, "Internal")

    stack = ExitStack()
    with stack:
        k = K(nc, stack)
        build_common(nc, k, stack, c)
        emit_p2(nc, k, stack, c, load_xn1=True, stage2=stage2)
        k.wait("sync", "so", k.cnt["so"])
        k.wait("sync", sOut, k.cnt[sOut])
        k.finish()
    return nc


def p2_inputs(core, ins, xn1=None):
    tr = lambda a: np.ascontiguousarray(np.asarray(a).T)
    t0 = core * T
    inv = (10000.0 ** (-(np.arange(0, RD, 2, dtype=np.float64) / RD)))
    freqs = np.outer(np.arange(t0, t0 + T, dtype=np.float64), inv)
    emb = np.concatenate([freqs, freqs], 1)
    def til2(w, nt, nc_, no):
        w = np.asarray(w)
        return np.ascontiguousarray(w.reshape(nc_, 128, nt, no).transpose(2, 1, 0, 3))
    wkvb = np.asarray(ins["L1_attn_w_kvb"]).reshape(KVR, NH, 2, 128)
    wkn = np.ascontiguousarray(wkvb[:, :, 0, :].reshape(KVR, NH * 128))
    wv = np.ascontiguousarray(wkvb[:, :, 1, :].reshape(KVR, NH * 128))
    wkva = np.asarray(ins["L1_attn_w_kva"])
    m = {
        "wqal": til2(ins["L1_attn_w_qa"], QRC, DC, 128),
        "wqbl": til2(ins["L1_attn_w_qb"], 48, QRC, 128),
        "wkval": til2(wkva[:, 0:KVR], 4, DC, 128),
        "wkvrl": np.ascontiguousarray(wkva[:, KVR:].reshape(DC, 128, RD).transpose(1, 0, 2)),
        "wknl": til2(wkn, NH, KVRC, 128),
        "wvl": til2(wv, 8, KVRC, 512),
        "wol": til2(ins["L1_attn_w_o"], DC, DC, 128),
        "gqa": np.ascontiguousarray(np.asarray(ins["L1_attn_g_qa"]).reshape(QRC, 128).T.astype(np.float32)),
        "gkva": np.ascontiguousarray(np.asarray(ins["L1_attn_g_kva"]).reshape(KVRC, 128).T.astype(np.float32)),
        "rt": _make_rt(),
        "cos": np.ascontiguousarray(np.cos(emb).T.astype(np.float32)),
        "sin": np.ascontiguousarray(np.sin(emb).T.astype(np.float32)),
        "ones_row": np.ones((1, 128), np.float32),
        "ones_col": np.ones((128, 1), ml_dtypes.bfloat16),
        "ones_col_f": np.ones((128, 1), np.float32),
        "misc_f": np.array([[1.0, 1e-5]], np.float32),
    }
    if xn1 is not None:
        m["xn1"] = np.ascontiguousarray(xn1[t0:t0 + T].T.astype(ml_dtypes.bfloat16))
    return m


def _make_rt():
    rt = np.zeros((RD, RD), np.float32)
    half = RD // 2
    for j in range(half):
        rt[j, j + half] = 1.0
    for j in range(half, RD):
        rt[j, j - half] = -1.0
    return rt.astype(ml_dtypes.bfloat16)


def emit_p3_rest(nc, k, stack, c, xn1done, stage3=99):
    d = c.d
    SINK = c.SINK
    xn1t = c.ab(0, 32)
    hp = SINK[:, 896:928]
    hpo = SINK[:, 928:960]
    Hres = SINK[:, 200:328]
    hres_done = c.p3s["hres_done"]
    for g in range(8):
        k.dmat("sync", xn1t[:, g * 4096:(g + 1) * 4096],
               d["xn1"][:, g * 128:(g + 1) * 128], "tm",
               waits=([("x", xn1done), (sPE, getattr(c, "att_done_pe", 0)),
                       (sDVE, getattr(c, "att_done_dve", 0))] if g == 0 else []))
    tmdone = k.cnt["tm"]

    xp = c.AF32[:, 0:1024]
    stg = [c.ab(64, 1), c.ab(65, 1)]
    for g in range(8):
        par = g % 2
        for n in range(4):
            if n == 0:
                k.emit("vector", lambda e, g=g, n=n: e.tensor_scalar(
                    xp, xn1t[:, g * 4096 + n * 1024:g * 4096 + n * 1024 + 1024],
                    hp[:, g * 4 + n:g * 4 + n + 1], None, ALU.mult),
                    waits=[("tm", tmdone), ("xps", k.cnt["xps"])])
            else:
                k.emit("vector", lambda e, g=g, n=n: e.scalar_tensor_tensor(
                    xp, xn1t[:, g * 4096 + n * 1024:g * 4096 + n * 1024 + 1024],
                    hp[:, g * 4 + n:g * 4 + n + 1], xp, ALU.mult, ALU.add))
        dvs = k.emit("vector", lambda e, par=par: e.tensor_copy(stg[par], xp),
                     waits=[(sPE, getattr(c, "wo_done_pe", 0)) if g == 0 else (sPE, 0)],
                     inc=sDVE)
        k.dma("sync", d["xpre"][g * 128:(g + 1) * 128, :], stg[par], "xps",
              waits=[(sDVE, dvs)])

    xpf = c.ab(72, 8)
    for cc2 in range(8):
        k.dmat("sync", xpf[:, cc2 * 1024:(cc2 + 1) * 1024],
               d["xpre"][:, cc2 * 128:(cc2 + 1) * 128], "tm2",
               waits=[("xps", k.cnt["xps"])] if cc2 == 0 else [])
    tm2 = k.cnt["tm2"]
    ystg = [c.ab(64, 1), c.ab(65, 1)]
    pe_y = {}
    st_y = {}
    for ot in range(8):
        par = ot % 2
        k.dma("gpsimd", c.W2SB[:, par * 1024:par * 1024 + 1024],
              d["wbl"][ot].rearrange("d c o -> d (c o)"), f"yw{par}",
              waits=[(sPE, pe_y.get(ot - 2, 0))])
        ywc = k.cnt[f"yw{par}"]
        pe = None
        for h in range(2):
            for ci in range(8):
                pe = k.emit("tensor", lambda e, ci=ci, h=h, par=par: e.matmul(
                    c.ps[0 + h][:, :], c.W2SB[:, par * 1024 + ci * 128:par * 1024 + ci * 128 + 128],
                    xpf[:, ci * 1024 + h * 512:ci * 1024 + h * 512 + 512],
                    start=(ci == 0), stop=(ci == 7)),
                    waits=([(f"yw{par}", ywc), ("tm2", tm2),
                            (sDVE, st_y.get(ot - 1, 0))] if (h == 0 and ci == 0) else []),
                    inc=sPE)
        pe_y[ot] = pe
        dvs = None
        for h in range(2):
            dvs = k.emit("vector", lambda e, h=h, ot=ot: e.tensor_copy(
                ystg[ot % 2][:, h * 512:h * 512 + 512], c.ps[0 + h][:, :]),
                waits=[(sPE, pe), ("ys", k.cnt["ys"])] +
                      ([("xps", k.cnt["xps"])] if ot == 0 else []), inc=sDVE)
        st_y[ot] = dvs
        k.dma("sync", d["y"][ot * 128:(ot + 1) * 128, :], ystg[ot % 2], "ys",
              waits=[(sDVE, dvs)])
    ytm = c.ab(64, 8)
    for g in range(8):
        k.dmat("sync", ytm[:, g * 1024:(g + 1) * 1024],
               d["y"][:, g * 128:(g + 1) * 128], "tm3",
               waits=[("ys", k.cnt["ys"]), (sDVE, k.cnt[sDVE])] if g == 0 else [])
    tm3 = k.cnt["tm3"]

    gin = c.AF32[:, 0:4096]
    g2stg = c.ab(72, 4)
    for g in range(8):
        for i in range(4):
            for j in range(4):
                sc_ap = Hres[:, g * 16 + 4 * i + j:g * 16 + 4 * i + j + 1]
                if j == 0:
                    k.emit("vector", lambda e, g=g, i=i, j=j, sc_ap=sc_ap: e.tensor_scalar(
                        gin[:, i * 1024:(i + 1) * 1024],
                        xn1t[:, g * 4096 + j * 1024:g * 4096 + j * 1024 + 1024],
                        sc_ap, None, ALU.mult),
                        waits=[(sDVE, hres_done), ("tm", tmdone),
                               ("g2s", k.cnt["g2s"])] if i == 0 else [])
                else:
                    k.emit("vector", lambda e, g=g, i=i, j=j, sc_ap=sc_ap: e.scalar_tensor_tensor(
                        gin[:, i * 1024:(i + 1) * 1024],
                        xn1t[:, g * 4096 + j * 1024:g * 4096 + j * 1024 + 1024],
                        sc_ap, gin[:, i * 1024:(i + 1) * 1024], ALU.mult, ALU.add))
        for n in range(4):
            k.emit("vector", lambda e, g=g, n=n: e.scalar_tensor_tensor(
                gin[:, n * 1024:(n + 1) * 1024], ytm[:, g * 1024:(g + 1) * 1024],
                hpo[:, g * 4 + n:g * 4 + n + 1], gin[:, n * 1024:(n + 1) * 1024],
                ALU.mult, ALU.add),
                waits=[("tm3", tm3)] if (g == 0 and n == 0) else [])
        a = k.emit("scalar", lambda e, g=g: e.activation(g2stg[:, :], gin, AF.Silu),
                   waits=[(sDVE, k.cnt[sDVE])] +
                         ([(sPE, pe_y[7])] if g == 0 else []), inc=sACT)
        k.dma("sync", d["g2tm"][g * 128:(g + 1) * 128, :], g2stg[:, :], "g2s",
              waits=[(sACT, a)])

    g2c = [c.ab(76, 1), c.ab(77, 1)]
    for ci in range(DC):
        par = ci % 2
        k.dmat("sync", g2c[par], d["g2tm"][:, ci * 128:(ci + 1) * 128], "tm4",
               waits=([("g2s", k.cnt["g2s"])] if ci == 0 else []) +
                     [(sDVE, k.cnt[sDVE] if ci >= 2 else 0)])
        x1c = c.AF32[:, (par * 2) * 1024:(par * 2) * 1024 + 1024]
        s1c = c.AF32[:, (par * 2 + 1) * 1024:(par * 2 + 1) * 1024 + 1024]
        k.dma("sync", x1c, d["x1"][ci * 128:(ci + 1) * 128, :], "fx",
              waits=[(sDVE, k.cnt[sDVE] if ci >= 2 else 0)])
        k.dma("sync", s1c, d["s1"][ci * 128:(ci + 1) * 128, :], "fx")
        fxc = k.cnt["fx"]
        tmc = k.cnt["tm4"]
        dvs = k.emit("vector", lambda e, par=par, x1c=x1c, s1c=s1c: e.tensor_mul(
            s1c, s1c, g2c[par]), waits=[("fx", fxc), ("tm4", tmc), ("fo", k.cnt["fo"])], inc=sDVE)
        dvs = k.emit("vector", lambda e, x1c=x1c, s1c=s1c: e.tensor_add(
            x1c, x1c, s1c), inc=sDVE)
        k.dma("sync", d["out"][ci * 128:(ci + 1) * 128, :], x1c, "fo",
              waits=[(sDVE, dvs)])
    c.out_done = k.cnt["fo"]




def emit_p3(nc, k, stack, c, load_xn1=True, stage3=99, part="all"):
    d = c.d
    SINK = c.SINK
    LOGF = c.LOGF
    EYE = c.EYE
    TAU_I = 10.0
    xn1 = c.ab(32, 32)
    xn1t = c.ab(0, 32)
    ld3 = [
        k.dma("sync", c.EYE[:, :], d["eyef"], sIn),
        k.dma("sync", c.SM[:, 147:163], d["resb"], sIn),
        k.dma("sync", c.SM[:, 163:167], d["preb"], sIn),
        k.dma("sync", c.SM[:, 167:171], d["postb"], sIn),
        k.dma("sync", c.SM[:, 171:172], d["gatec"], sIn),
        k.dma("sync", c.SM[0:1, 172:173], d["eps2"], sIn),
    ]
    resb = c.SM[:, 147:163]
    preb = c.SM[:, 163:167]
    postb = c.SM[:, 167:171]
    gatec = c.SM[:, 171:172]
    eps2 = c.SM[0:1, 172:173]
    c3 = k.cnt[sIn]
    c.const_done = c3

    if load_xn1:
        for i in range(4):
            k.dma("gpsimd", sview(c.ab(32 + 8 * i, 8), 1024),
                  d["xn1"][i * 1024:(i + 1) * 1024, :].rearrange("(c d) t -> d c t", d=128),
                  "x")
    xn1done = k.cnt["x"]
    if part == "rest":
        emit_p3_rest(nc, k, stack, c, xn1done, stage3)
        return

    sqb = [c.W2SB[:, 0:512], c.W2SB[:, 512:1024]]
    pe_after = {}
    last = None
    for ci in range(DC):
        par = ci % 2
        a = None
        for h in range(2):
            waits = [("x", xn1done)]
            if ci >= 2 and h == 0:
                waits.append((sPE, pe_after[ci - 2]))
            if h == 1:
                waits.append((sPE, last))
            a = k.emit("scalar", lambda e, ci=ci, par=par, h=h: e.activation(
                sqb[par], xn1[:, ci * 1024 + h * 512:ci * 1024 + h * 512 + 512], AF.Square),
                waits=waits, inc=sACT)
            last = k.emit("tensor", lambda e, par=par, h=h, ci=ci: e.matmul(
                c.ps[6 + h][0:1, :], c.ones_col, sqb[par],
                start=(ci == 0), stop=(ci == DC - 1)),
                waits=[(sACT, a), (sIn, c.const_done)], inc=sPE)
        pe_after[ci] = last
    rr = c.ROWS[0:1, 0:1024]
    dvc = None
    for h in range(2):
        dvc = k.emit("vector", lambda e, h=h: e.tensor_copy(
            rr[:, h * 512:h * 512 + 512], c.ps[6 + h][0:1, :]),
            waits=[(sPE, last)], inc=sDVE)
    sc = k.emit("scalar", lambda e: e.activation(rr, rr, AF.Sqrt, scale=1.0 / D, bias=eps2),
                waits=[(sDVE, dvc), (sIn, c3)], inc=sACT)
    dv = k.emit("vector", lambda e: e.reciprocal(rr, rr), waits=[(sACT, sc)], inc=sDVE)
    k.wait("vector", sDVE, dv)
    rstd_dv = dv

    k.dma("gpsimd", sview(c.W2SB[:, 2048:2048 + DC * 24], 24), wview(d["wall"], 0, DC, 0, 24),
          "lw", waits=[(sPE, last)])
    lwc = k.cnt["lw"]
    pe = None
    for h in range(2):
        for ci in range(DC):
            pe = k.emit("tensor", lambda e, ci=ci, h=h: e.matmul(
                c.ps[0 + h][0:24, :], c.W2SB[:, 2048 + ci * 24:2048 + ci * 24 + 24],
                xn1[:, ci * 1024 + h * 512:ci * 1024 + h * 512 + 512],
                start=(ci == 0), stop=(ci == DC - 1)),
                waits=([("lw", lwc), ("x", xn1done)] if (h == 0 and ci == 0) else []), inc=sPE)
    dv = None
    for h in range(2):
        dv = k.emit("vector", lambda e, h=h: e.tensor_copy(
            LOGF[:, h * 512:h * 512 + 512], c.ps[0 + h][0:24, :]),
            waits=[(sPE, pe)], inc=sDVE)
    pe_tr = None
    for g in range(8):
        pe_tr = k.emit("tensor", lambda e, g=g: e.transpose(
            c.ps[2][:, g * 24:g * 24 + 24], LOGF[:, g * 128:(g + 1) * 128], EYE[0:24, 0:24]),
            waits=[(sDVE, dv)] if g == 0 else [], inc=sPE)
    ztmp = SINK[:, 512:704]
    dv_z = k.emit("vector", lambda e: e.tensor_copy(ztmp, c.ps[2][:, 0:192]),
                  waits=[(sPE, pe_tr)], inc=sDVE)
    pe_tr = None
    for g in range(8):
        pe_tr = k.emit("tensor", lambda e, g=g: e.transpose(
            c.ps[4][:, g:g + 1], rr[:, g * 128:(g + 1) * 128], EYE[0:1, 0:1]),
            waits=[(sDVE, rstd_dv), (sIn, c3)] if g == 0 else [], inc=sPE)
    rrms = SINK[:, 192:200]
    dv_rrms = k.emit("vector", lambda e: e.tensor_copy(rrms, c.ps[4][:, 0:8]),
                     waits=[(sPE, pe_tr)], inc=sDVE)

    Z = SINK[:, 0:128]
    E = SINK[:, 704:832]
    L1a = SINK[:, 832:864]
    L2a = SINK[:, 864:896]
    hp = SINK[:, 896:928]
    hpo = SINK[:, 928:960]
    hscr = SINK[:, 960:976]
    b10 = SINK[:, 976:992]
    Hres = SINK[:, 200:328]
    k.emit("vector", lambda e: e.tensor_scalar(b10, resb, TAU_I, None, ALU.mult),
           waits=[(sIn, c3)])
    for g in range(8):
        k.emit("vector", lambda e, g=g: e.tensor_scalar(
            Z[:, g * 16:(g + 1) * 16], ztmp[:, g * 24:g * 24 + 16],
            rrms[:, g:g + 1], TAU_I, ALU.mult, ALU.mult),
            waits=[(sDVE, dv_z), (sDVE, dv_rrms)] if g == 0 else [])
        k.emit("vector", lambda e, g=g: e.tensor_add(
            Z[:, g * 16:(g + 1) * 16], Z[:, g * 16:(g + 1) * 16], b10))
        k.emit("vector", lambda e, g=g: e.tensor_scalar(
            hp[:, g * 4:(g + 1) * 4], ztmp[:, g * 24 + 16:g * 24 + 20],
            rrms[:, g:g + 1], None, ALU.mult))
        k.emit("vector", lambda e, g=g: e.tensor_add(
            hp[:, g * 4:(g + 1) * 4], hp[:, g * 4:(g + 1) * 4], preb))
        k.emit("vector", lambda e, g=g: e.tensor_scalar(
            hpo[:, g * 4:(g + 1) * 4], ztmp[:, g * 24 + 20:g * 24 + 24],
            rrms[:, g:g + 1], None, ALU.mult))
        dvh = k.emit("vector", lambda e, g=g: e.tensor_add(
            hpo[:, g * 4:(g + 1) * 4], hpo[:, g * 4:(g + 1) * 4], postb))

    for name, hl in (("hp", hp), ("hpo", hpo)):
        a = k.emit("scalar", lambda e, hl=hl: e.activation(hl, hl, AF.Exp),
                   waits=[(sDVE, k.cnt[sDVE])], inc=sACT)
        dvs = k.emit("vector", lambda e, hl=hl: e.reduce_sum(
            hscr[:, 0:8], hl.rearrange("p (g n) -> p g n", n=4), axis=AX.X),
            waits=[(sACT, a)], inc=sDVE)
        dvs = k.emit("vector", lambda e: e.reciprocal(hscr[:, 0:8], hscr[:, 0:8]), inc=sDVE)
        k.wait("vector", sDVE, dvs)
        for g in range(8):
            k.emit("vector", lambda e, g=g, hl=hl: e.tensor_scalar(
                hl[:, g * 4:(g + 1) * 4], hl[:, g * 4:(g + 1) * 4],
                hscr[:, g:g + 1], None, ALU.mult))

    def sink_views(buf, fix_i=None, fix_j=None):
        v3 = buf.rearrange("p (g i j) -> p g i j", i=4, j=4)
        if fix_i is not None:
            return v3[:, :, fix_i, :]
        return v3[:, :, :, fix_j]

    def l3(buf, n=4):
        return buf.rearrange("p (g n) -> p g n", n=n)

    k.emit("vector", lambda e: e.memset(L2a, 0.0))
    for it in range(50):
        for i in range(4):
            k.emit("vector", lambda e, i=i: e.tensor_sub(
                sink_views(E, fix_i=i), sink_views(Z, fix_i=i), l3(L2a)))
        a = k.emit("scalar", lambda e: e.activation(E, E, AF.Exp),
                   waits=[(sDVE, k.cnt[sDVE])], inc=sACT)
        dvs = k.emit("vector", lambda e: e.reduce_sum(
            L1a, E.rearrange("p (gi j) -> p gi j", j=4), axis=AX.X),
            waits=[(sACT, a)], inc=sDVE)
        a = k.emit("scalar", lambda e: e.activation(L1a, L1a, AF.Ln),
                   waits=[(sDVE, dvs)], inc=sACT)
        k.wait("vector", sACT, a)
        for j in range(4):
            k.emit("vector", lambda e, j=j: e.tensor_sub(
                sink_views(E, fix_j=j), sink_views(Z, fix_j=j), l3(L1a)))
        a = k.emit("scalar", lambda e: e.activation(E, E, AF.Exp),
                   waits=[(sDVE, k.cnt[sDVE])], inc=sACT)
        dvs = k.emit("vector", lambda e: e.reduce_sum(
            l3(L2a), E.rearrange("p (g i j) -> p g j i", i=4, j=4), axis=AX.X),
            waits=[(sACT, a)], inc=sDVE)
        a = k.emit("scalar", lambda e: e.activation(L2a, L2a, AF.Ln),
                   waits=[(sDVE, dvs)], inc=sACT)
        k.wait("vector", sACT, a)
    for i in range(4):
        k.emit("vector", lambda e, i=i: e.tensor_sub(
            sink_views(Hres, fix_i=i), sink_views(Z, fix_i=i), l3(L2a)))
    for j in range(4):
        k.emit("vector", lambda e, j=j: e.tensor_sub(
            sink_views(Hres, fix_j=j), sink_views(Hres, fix_j=j), l3(L1a)))
    a = k.emit("scalar", lambda e: e.activation(Hres, Hres, AF.Exp),
               waits=[(sDVE, k.cnt[sDVE])], inc=sACT)
    gg = SINK[:, 992:993]
    om = SINK[:, 993:994]
    a2 = k.emit("scalar", lambda e: e.activation(gg, gatec, AF.Sigmoid),
                waits=[(sIn, c3)], inc=sACT)
    dvs = k.emit("vector", lambda e: e.tensor_scalar(om, gg, -1.0, 1.0, ALU.mult, ALU.add),
                 waits=[(sACT, a2)], inc=sDVE)
    dvs = k.emit("vector", lambda e: e.tensor_scalar(Hres, Hres, gg, None, ALU.mult),
                 waits=[(sACT, a)], inc=sDVE)
    for g in range(8):
        dvs = k.emit("vector", lambda e, g=g: e.tensor_scalar(
            Hres.rearrange("p (g i j) -> p g i j", i=4, j=4)[:, g, :, :]
            .rearrange("p i j -> p (i j)")[:, 0:16:5],
            Hres.rearrange("p (g i j) -> p g i j", i=4, j=4)[:, g, :, :]
            .rearrange("p i j -> p (i j)")[:, 0:16:5],
            om, None, ALU.add))
    hres_done = dvs
    c.p3s = dict(hres_done=hres_done)

    if stage3 <= 0:
        dvx = k.emit("vector", lambda e: e.tensor_copy(c.AF32[:, 0:1024], SINK[:, 0:1024]), inc=sDVE)
        k.dma("sync", d["out"][0:128, :], c.AF32[:, 0:1024], sOut, waits=[(sDVE, dvx)])
        return

    if part == "all":
        emit_p3_rest(nc, k, stack, c, xn1done, stage3)


def build_p3(stage3=99):
    nc = bass.Bass("TRN2", target_bir_lowering=False)
    c = Ctx()
    c.d = {}
    d = c.d
    dr = lambda name, shape, dt, kind: nc.dram_tensor(name, shape, dt, kind=kind).ap()
    d["xn1"] = dr("xn1", [D, T], BF16, "ExternalInput")
    d["x1"] = dr("x1", [D, T], F32, "ExternalInput")
    d["s1"] = dr("s1", [D, T], F32, "ExternalInput")
    d["wall"] = dr("wall", [D, 24], F32, "ExternalInput")
    d["wbl"] = dr("wbl", [8, 128, 8, 128], F32, "ExternalInput")
    d["resb"] = dr("resb", [128, 16], F32, "ExternalInput")
    d["preb"] = dr("preb", [128, 4], F32, "ExternalInput")
    d["postb"] = dr("postb", [128, 4], F32, "ExternalInput")
    d["gatec"] = dr("gatec", [128, 1], F32, "ExternalInput")
    d["eps2"] = dr("eps2", [1, 1], F32, "ExternalInput")
    d["eyef"] = dr("eyef", [24, 24], F32, "ExternalInput")
    d["ones_row"] = dr("ones_row", [1, 128], F32, "ExternalInput")
    d["ones_col"] = dr("ones_col", [128, 1], BF16, "ExternalInput")
    d["ones_col_f"] = dr("ones_col_f", [128, 1], F32, "ExternalInput")
    d["misc_f"] = dr("misc_f", [1, 2], F32, "ExternalInput")
    d["out"] = dr("out", [D, T], F32, "ExternalOutput")
    d["xpre"] = dr("xpre", [T, 1024], BF16, "Internal")
    d["y"] = dr("y", [1024, T], BF16, "Internal")
    d["g2tm"] = dr("g2tm", [T, D], BF16, "Internal")

    stack = ExitStack()
    with stack:
        k = K(nc, stack)
        build_common(nc, k, stack, c)
        emit_p3(nc, k, stack, c, load_xn1=True, stage3=stage3)
        k.wait("sync", "fo", k.cnt["fo"])
        k.wait("sync", sOut, k.cnt[sOut])
        k.finish()
    return nc


def p3_inputs(core, ins, xn1=None, x1=None, s1=None):
    t0 = core * T
    wall = np.concatenate([np.asarray(ins["L1_ffn_W_res"]).T,
                           np.asarray(ins["L1_ffn_W_pre"]).T,
                           np.asarray(ins["L1_ffn_W_post"]).T], axis=1)
    m = {
        "wall": np.ascontiguousarray(wall.astype(np.float32)),
        "wbl": np.ascontiguousarray(np.asarray(ins["L1_ffn_W_branch"])
                                    .reshape(8, 128, 8, 128).transpose(2, 1, 0, 3)),
        "resb": np.ascontiguousarray(np.broadcast_to(
            np.asarray(ins["L1_ffn_H_res_base"]).reshape(1, 16), (128, 16)).astype(np.float32)),
        "preb": np.ascontiguousarray(np.broadcast_to(
            np.asarray(ins["L1_ffn_H_pre_base"]).reshape(1, 4), (128, 4)).astype(np.float32)),
        "postb": np.ascontiguousarray(np.broadcast_to(
            np.asarray(ins["L1_ffn_H_post_base"]).reshape(1, 4), (128, 4)).astype(np.float32)),
        "gatec": np.ascontiguousarray(np.broadcast_to(
            np.asarray(ins["L1_ffn_gate"]).reshape(1, 1), (128, 1)).astype(np.float32)),
        "eps2": np.array([[1.1920929e-07]], np.float32),
        "eyef": np.eye(24, dtype=np.float32),
        "ones_row": np.ones((1, 128), np.float32),
        "ones_col": np.ones((128, 1), ml_dtypes.bfloat16),
        "ones_col_f": np.ones((128, 1), np.float32),
        "misc_f": np.array([[1.0, 1e-5]], np.float32),
    }
    if xn1 is not None:
        m["xn1"] = np.ascontiguousarray(xn1[t0:t0 + T].T.astype(ml_dtypes.bfloat16))
        m["x1"] = np.ascontiguousarray(x1[t0:t0 + T].T.astype(np.float32))
        m["s1"] = np.ascontiguousarray(s1[t0:t0 + T].T.astype(np.float32))
    return m


def build_p1(stage=99):
    nc = bass.Bass("TRN2", target_bir_lowering=False)
    c = Ctx()
    c.d = {}
    d = c.d
    dr = lambda name, shape, dt, kind: nc.dram_tensor(name, shape, dt, kind=kind).ap()
    d["x"] = dr("x", [D, T], F32, "ExternalInput")
    for nm, sh in (("w1l", [HCN, 128, DC, 128]), ("w3l", [HCN, 128, DC, 128]),
                   ("w2l", [DC, 128, HCN, 128]),
                   ("a1t", [D, LR]), ("b1t", [LR, HID]), ("a3t", [D, LR]),
                   ("b3t", [LR, HID]), ("a2t", [HID, LR]), ("b2t", [LR, D])):
        d[nm] = dr(nm, sh, F32, "ExternalInput")
    d["ones_row"] = dr("ones_row", [1, 128], F32, "ExternalInput")
    d["ones_col"] = dr("ones_col", [128, 1], BF16, "ExternalInput")
    d["ones_col_f"] = dr("ones_col_f", [128, 1], F32, "ExternalInput")
    d["misc_f"] = dr("misc_f", [1, 2], F32, "ExternalInput")
    d["x1"] = dr("x1", [D, T], F32, "ExternalOutput")
    d["xn1"] = dr("xn1", [D, T], BF16, "ExternalOutput")
    d["part"] = dr("part", [D, T], F32, "Internal")

    stack = ExitStack()
    with stack:
        k = K(nc, stack)
        build_common(nc, k, stack, c)
        emit_p1(nc, k, stack, c, stage)
        k.wait("sync", sOut, k.cnt[sOut])
        k.finish()
    return nc


def p1_inputs(core, ins):
    x = np.asarray(ins["L0_attn_x"])[0]
    sl = slice(core * T, (core + 1) * T)
    tr = lambda a: np.ascontiguousarray(np.asarray(a).T)
    def til(w, nt, nc_, no):
        w = np.asarray(w)
        return np.ascontiguousarray(w.reshape(nt, no, nc_, 128).transpose(0, 3, 2, 1))
    return {
        "x": np.ascontiguousarray(x[sl].T),
        "w1l": til(ins["L0_ffn_w1"], HCN, DC, 128),
        "w3l": til(ins["L0_ffn_w3"], HCN, DC, 128),
        "w2l": til(ins["L0_ffn_w2"], DC, HCN, 128),
        "a1t": tr(ins["L0_ffn_lora_A1"]), "b1t": tr(ins["L0_ffn_lora_B1"]),
        "a3t": tr(ins["L0_ffn_lora_A3"]), "b3t": tr(ins["L0_ffn_lora_B3"]),
        "a2t": tr(ins["L0_ffn_lora_A2"]), "b2t": tr(ins["L0_ffn_lora_B2"]),
        "ones_row": np.ones((1, 128), np.float32),
        "ones_col": np.ones((128, 1), ml_dtypes.bfloat16),
        "ones_col_f": np.ones((128, 1), np.float32),
        "misc_f": np.array([[1.0, 1e-5]], np.float32),
    }


def build_fused():
    nc = bass.Bass("TRN2", target_bir_lowering=False)
    c = Ctx()
    c.d = {}
    d = c.d
    dr = lambda name, shape, dt, kind: nc.dram_tensor(name, shape, dt, kind=kind).ap()
    d["x"] = dr("x", [D, T], F32, "ExternalInput")
    for nm, sh in (("w1l", [HCN, 128, DC, 128]), ("w3l", [HCN, 128, DC, 128]),
                   ("w2l", [DC, 128, HCN, 128]),
                   ("a1t", [D, LR]), ("b1t", [LR, HID]), ("a3t", [D, LR]),
                   ("b3t", [LR, HID]), ("a2t", [HID, LR]), ("b2t", [LR, D])):
        d[nm] = dr(nm, sh, F32, "ExternalInput")
    for nm, sh in (("wqal", [QRC, 128, DC, 128]), ("wqbl", [48, 128, QRC, 128]),
                   ("wkval", [4, 128, DC, 128]), ("wkvrl", [128, DC, 64]),
                   ("wknl", [NH, 128, KVRC, 128]), ("wvl", [8, 128, KVRC, 512]),
                   ("wol", [DC, 128, DC, 128])):
        d[nm] = dr(nm, sh, F32, "ExternalInput")
    d["gqa"] = dr("gqa", [128, QRC], F32, "ExternalInput")
    d["gkva"] = dr("gkva", [128, KVRC], F32, "ExternalInput")
    d["rt"] = dr("rt", [RD, RD], BF16, "ExternalInput")
    d["cos"] = dr("cos", [RD, T], F32, "ExternalInput")
    d["sin"] = dr("sin", [RD, T], F32, "ExternalInput")
    d["wall"] = dr("wall", [D, 24], F32, "ExternalInput")
    d["wbl"] = dr("wbl", [8, 128, 8, 128], F32, "ExternalInput")
    d["resb"] = dr("resb", [128, 16], F32, "ExternalInput")
    d["preb"] = dr("preb", [128, 4], F32, "ExternalInput")
    d["postb"] = dr("postb", [128, 4], F32, "ExternalInput")
    d["gatec"] = dr("gatec", [128, 1], F32, "ExternalInput")
    d["eps2"] = dr("eps2", [1, 1], F32, "ExternalInput")
    d["eyef"] = dr("eyef", [24, 24], F32, "ExternalInput")
    d["ones_row"] = dr("ones_row", [1, 128], F32, "ExternalInput")
    d["ones_col"] = dr("ones_col", [128, 1], BF16, "ExternalInput")
    d["ones_col_f"] = dr("ones_col_f", [128, 1], F32, "ExternalInput")
    d["misc_f"] = dr("misc_f", [1, 2], F32, "ExternalInput")
    d["out"] = dr("out", [D, T], F32, "ExternalOutput")
    for nm, sh, dt in (("x1", [D, T], F32), ("xn1", [D, T], BF16), ("part", [D, T], F32),
                       ("s1", [D, T], F32), ("q", [NH * QK, T], BF16),
                       ("knc", [NH * 128, T], BF16), ("vc", [T, NH * VD], BF16),
                       ("krc", [RD, T], BF16),
                       ("kng", [NCORES * NH * 128, T], BF16), ("vg", [SEQ, NH * VD], BF16),
                       ("krg", [NCORES * RD, T], BF16),
                       ("xpre", [T, 1024], BF16), ("y", [1024, T], BF16),
                       ("g2tm", [T, D], BF16)):
        d[nm] = dr(nm, sh, dt, "Internal")

    stack = ExitStack()
    with stack:
        k = K(nc, stack)
        build_common(nc, k, stack, c)
        emit_p1(nc, k, stack, c, 99)
        for qn_ in ("gpsimd", "sync"):
            k.wait(qn_, sPE, k.cnt[sPE])
            k.wait(qn_, sDVE, k.cnt[sDVE])
            k.wait(qn_, sACT, k.cnt[sACT])
            k.wait(qn_, sOut, k.cnt[sOut])
        c.overlap_hook = lambda: emit_p3(nc, k, stack, c, load_xn1=False,
                                         stage3=99, part="hmat")
        emit_p2(nc, k, stack, c, load_xn1=False, stage2=99)
        emit_p3(nc, k, stack, c, load_xn1=False, stage3=99, part="rest")
        k.wait("sync", "fo", k.cnt["fo"])
        k.wait("sync", sOut, k.cnt[sOut])
        k.finish()
    return nc


def fused_inputs(core, ins):
    m = p1_inputs(core, ins)
    m.update(p2_inputs(core, ins))
    m.update(p3_inputs(core, ins))
    return m


def kernel(**inputs):
    ins = inputs
    cores = list(range(NCORES))
    nc = build_fused()
    res = run_bass_kernel_spmd(nc, [fused_inputs(cc, ins) for cc in cores], core_ids=cores)
    out = np.concatenate([res.results[cc]["out"].T for cc in cores], axis=0)
    return out.reshape(1, SEQ, D).astype(np.float32)




# version 23
# speedup vs baseline: 1.0083x; 1.0083x over previous
import numpy as np
import ml_dtypes
from contextlib import ExitStack

import concourse.bass as bass
import concourse.mybir as mybir
from concourse.bass_utils import run_bass_kernel_spmd

F32 = mybir.dt.float32
BF16 = mybir.dt.bfloat16
AF = mybir.ActivationFunctionType
ALU = mybir.AluOpType
AX = mybir.AxisListType

NCORES = 8
T = 1024
D = 4096
DC = D // 128
HID = 11008
HCN = HID // 128
SEC_SIZES = [22, 22, 21, 21]
LR = 16
LSC = 2.0
NH = 32
QR = 1536
QRC = QR // 128
KVR = 512
KVRC = KVR // 128
RD = 64
ND = 128
VD = 128
QK = RD + ND
SEQ = NCORES * T
KTN = SEQ // 128

ENGS = ("sync", "tensor", "vector", "scalar", "gpsimd")
sIn, sW, sPE, sACT, sDVE, sPOOL, sOut, sCC = "in", "w", "pe", "act", "dve", "pool", "out", "cc"


class K:
    def __init__(self, nc, stack):
        self.nc = nc
        self.stack = stack
        self.q = {e: [] for e in ENGS}
        self.sems = {}
        import collections
        self.cnt = collections.defaultdict(int)

    def sem(self, name):
        if name not in self.sems:
            self.sems[name] = self.stack.enter_context(self.nc.semaphore(name))
            self.cnt[name] = 0
        return self.sems[name]

    ENGSEM = {"vector": sDVE, "scalar": sACT, "gpsimd": sPOOL}

    def emit(self, eng, fn, waits=(), inc=None, by=1):
        es = self.ENGSEM.get(eng)
        if es is not None:
            waits = list(waits) + [(es, self.cnt[es])]
            if inc is None:
                inc = es
        waits = tuple((n, v) for (n, v) in waits if v > 0)
        for n, _ in waits:
            self.sem(n)
        if inc is not None:
            self.sem(inc)

        def go(e, fn=fn, waits=waits, inc=inc, by=by):
            for (n, v) in waits:
                e.wait_ge(self.sems[n], v)
            ins = fn(e)
            if inc is not None:
                ins.then_inc(self.sems[inc], by)

        self.q[eng].append(go)
        if inc is not None:
            self.cnt[inc] += by
            return self.cnt[inc]
        return None

    def wait(self, eng, name, v):
        if v > 0:
            self.sem(name)
            self.q[eng].append(lambda e, n=name, v=v: e.wait_ge(self.sems[n], v))

    def dma(self, eng, out, in_, inc, waits=()):
        return self.emit(eng, lambda e: e.dma_start(out=out, in_=in_),
                         waits=waits, inc=inc, by=16)

    def dmat(self, eng, out, in_, inc, waits=()):
        return self.emit(eng, lambda e: e.dma_start_transpose(out=out, in_=in_),
                         waits=waits, inc=inc, by=16)

    def finish(self):
        block = self.stack.enter_context(self.nc.Block())
        for e in ENGS:
            def run(eng, e=e):
                for f in self.q[e]:
                    f(eng)
            getattr(block, e)(run)


def tload(w4, tile, c0=0, nch=None, flat_cols=None):
    if nch is None:
        return w4[tile]
    return w4[tile][:, c0:c0 + nch, :]


def wview(w, c0, nch, o0, no):
    return w[c0 * 128:(c0 + nch) * 128, o0:o0 + no].rearrange("(c d) o -> d c o", d=128)


def sview(sb, no):
    return sb.rearrange("d (c o) -> d c o", o=no)


class Ctx:
    overlap_hook = None


def build_common(nc, k, stack, c):
    ec = stack.enter_context
    c.AB = ec(nc.sbuf_tensor("AB", [128, 80 * 1024], BF16))
    c.AF32 = ec(nc.sbuf_tensor("AF32", [128, 4 * 1024], F32))
    c.ROWS = ec(nc.sbuf_tensor("ROWS", [1, 2 * 1024], F32))
    c.SM = ec(nc.sbuf_tensor("SM", [128, 176], F32))
    c.SMB = ec(nc.sbuf_tensor("SMB", [128, 160], BF16))
    c.L16 = ec(nc.sbuf_tensor("L16", [16, 3840], BF16))
    c.W2SB = ec(nc.sbuf_tensor("W2SB", [128, 2 * 2816], BF16))
    c.SINK = ec(nc.sbuf_tensor("SINK", [128, 1024], F32))
    c.EYE = ec(nc.sbuf_tensor("EYE", [24, 24], F32))
    c.LOGF = c.SINK[0:24, :]
    c.ps = [ec(nc.psum_tensor(f"ps{i}", [128, 512], F32)) for i in range(8)]

    c.ab = lambda c0, n: c.AB[:, c0 * 1024:(c0 + n) * 1024]
    c.af = lambda c0, n: c.AF32[:, c0 * 1024:(c0 + n) * 1024]

    c.ones_row = c.SM[0:1, 0:128]
    c.ones_col = c.SMB[:, 0:1]
    c.ones_col_f = c.SM[:, 130:131]
    c.eps_ln = c.SM[0:1, 129:130]

    c.ld_const = [
        k.dma("sync", c.SM[0:1, 0:128], c.d["ones_row"], sIn),
        k.dma("sync", c.SMB[:, 0:1], c.d["ones_col"], sIn),
        k.dma("sync", c.SM[0:1, 128:130], c.d["misc_f"], sIn),
        k.dma("sync", c.SM[:, 130:131], c.d["ones_col_f"], sIn),
    ]
    c.const_done = c.ld_const[-1]


def stat_rows(k, c, src_ab, nchunks, wait_fn):
    pe_after = {}
    last = None
    for ci in range(nchunks):
        par = ci % 2
        waits = list(wait_fn(ci))
        if ci >= 2:
            waits.append((sPE, pe_after[ci - 2]))
        a = k.emit("scalar", lambda e, ci=ci, par=par: e.activation(
            c.ab(64 + par, 1), src_ab[:, ci * 1024:(ci + 1) * 1024], AF.Square),
            waits=waits, inc=sACT)
        for h in range(2):
            k.emit("tensor", lambda e, ci=ci, h=h: e.matmul(
                c.ps[2 + h][0:1, :], c.ones_col,
                src_ab[:, ci * 1024 + h * 512: ci * 1024 + h * 512 + 512],
                start=(ci == 0), stop=(ci == nchunks - 1)),
                waits=list(wait_fn(ci)) + [(sIn, c.const_done)], inc=sPE)
        for h in range(2):
            last = k.emit("tensor", lambda e, par=par, h=h, ci=ci: e.matmul(
                c.ps[4 + h][0:1, :], c.ones_col,
                c.ab(64 + par, 1)[:, h * 512:h * 512 + 512],
                start=(ci == 0), stop=(ci == nchunks - 1)),
                waits=[(sACT, a)], inc=sPE)
            pe_after[ci] = last
    return last


def ln_apply(k, c, src_ab, dst_ab, nchunks, pe_stats, r0):
    rm = c.ROWS[0:1, 0:1024]
    ra = c.ROWS[0:1, 1024:2048]
    rt = c.SINK[0:1, 0:1024]
    for h in range(2):
        k.emit("vector", lambda e, h=h: e.tensor_copy(rm[:, h * 512:h * 512 + 512], c.ps[2 + h][0:1, :]),
               waits=[(sPE, pe_stats)], inc=sDVE)
    for h in range(2):
        k.emit("vector", lambda e, h=h: e.tensor_copy(rt[:, h * 512:h * 512 + 512], c.ps[4 + h][0:1, :]),
               inc=sDVE)
    k.emit("vector", lambda e: e.tensor_scalar_mul(rm, rm, 1.0 / D), inc=sDVE)
    k.emit("vector", lambda e: e.tensor_scalar_mul(rt, rt, 1.0 / D), inc=sDVE)
    k.emit("vector", lambda e: e.tensor_mul(ra, rm, rm), inc=sDVE)
    dv = k.emit("vector", lambda e: e.tensor_sub(rt, rt, ra), inc=sDVE)
    a1 = k.emit("scalar", lambda e: e.activation(ra, rt, AF.Sqrt, bias=c.eps_ln),
                waits=[(sDVE, dv), (sIn, c.const_done)], inc=sACT)
    dv = k.emit("vector", lambda e: e.reciprocal(ra, ra), waits=[(sACT, a1)], inc=sDVE)
    k.wait("vector", sDVE, dv)
    dv = k.emit("vector", lambda e: e.scalar_tensor_tensor(rm, rm, -1.0, ra, ALU.mult, ALU.mult),
                inc=sDVE)
    pe_rep = None
    for h in range(2):
        k.emit("tensor", lambda e, h=h: e.matmul(
            c.ps[0 + h][:, :], c.ones_row, ra[:, h * 512:h * 512 + 512], start=True, stop=True),
            waits=[(sDVE, dv)], inc=sPE)
    for h in range(2):
        pe_rep = k.emit("tensor", lambda e, h=h: e.matmul(
            c.ps[2 + h][:, :], c.ones_row, rm[:, h * 512:h * 512 + 512], start=True, stop=True),
            inc=sPE)
    scr = c.af(0, 1)
    last = None
    for ci in range(nchunks):
        for h in range(2):
            sl = slice(ci * 1024 + h * 512, ci * 1024 + h * 512 + 512)
            k.emit("vector", lambda e, sl=sl, h=h: e.tensor_mul(
                scr[:, 0:512], src_ab[:, sl], c.ps[0 + h][:, :]),
                waits=[(sPE, pe_rep)], inc=sDVE)
            last = k.emit("vector", lambda e, sl=sl, h=h: e.tensor_add(
                dst_ab[:, sl], scr[:, 0:512], c.ps[2 + h][:, :]),
                inc=sDVE)
    return last


def emit_p1(nc, k, stack, c, stage=99):
    d = c.d
    xbf = c.ab(0, 32)
    xn = c.ab(32, 32)
    gsec = c.ab(0, 22)
    w1b = [c.ab(64, 4), c.ab(68, 4)]
    w3b = [c.ab(72, 4), c.ab(76, 4)]
    w2b = [c.W2SB[:, 0:2816], c.W2SB[:, 2816:5632]]

    for i in range(4):
        k.dma("gpsimd", sview(c.ab(8 * i, 8), 1024),
              d["x"][i * 1024:(i + 1) * 1024, :].rearrange("(c d) t -> d c t", d=128), "x")
    xdone = k.cnt["x"]

    pe_stats = stat_rows(k, c, xbf, DC, lambda ci: [("x", xdone)])
    ln_done = ln_apply(k, c, xbf, xn, DC, pe_stats, 0)
    if stage <= 0:
        for i in range(4):
            k.dma("sync", d["xn1"][i * 1024:(i + 1) * 1024, :].rearrange("(c d) t -> d c t", d=128),
                  sview(c.ab(32 + 8 * i, 8), 1024), sOut, waits=[(sDVE, ln_done)])
        return

    xnA = {"a1t": c.L16[:, 0:1024], "a3t": c.L16[:, 1024:2048]}
    a_evac = {}
    for ni, nm in enumerate(("a1t", "a3t")):
        k.dma("gpsimd", sview(c.W2SB[:, ni * 512:(ni + 1) * 512], 16),
              wview(d[nm], 0, DC, 0, LR), "lw", waits=[(sPE, pe_stats)] if ni == 0 else [])
    lwc = k.cnt["lw"]
    prev_evac = 0
    for ni, nm in enumerate(("a1t", "a3t")):
        pe = None
        for ci in range(DC):
            for h in range(2):
                pe = k.emit("tensor", lambda e, ci=ci, h=h, ni=ni: e.matmul(
                    c.ps[6 + h][0:16, :], c.W2SB[:, ni * 512 + ci * 16:ni * 512 + ci * 16 + 16],
                    xn[:, ci * 1024 + h * 512:ci * 1024 + h * 512 + 512],
                    start=(ci == 0), stop=(ci == DC - 1)),
                    waits=[("lw", lwc), (sDVE, ln_done), (sACT, prev_evac)], inc=sPE)
        ae = None
        for h in range(2):
            ae = k.emit("scalar", lambda e, h=h, nm=nm: e.mul(
                xnA[nm][0:16, h * 512:h * 512 + 512], c.ps[6 + h][0:16, :], LSC),
                waits=[(sPE, pe)], inc=sACT)
        a_evac[nm] = ae
        prev_evac = ae

    hc_base = 0
    pe_lora2 = {}
    cons_hc = {}
    w2_cons = {}
    lora2_evac = None
    w2_rmw = {}
    part_f32 = c.af(1, 1)
    silu_scr = c.af(2, 1)
    x_f32 = c.af(3, 1)
    prev_part_user = {}
    pe_stat_mm = {}
    x1_store_last = None
    last_hc_pe = None

    for s, ssz in enumerate(SEC_SIZES):
        for j in range(ssz):
            hc = hc_base + j
            slot = hc % 2
            wsem = f"wh{slot}"
            gate = [(sPE, pe_lora2.get(hc - 2, 0))]
            if hc < 2:
                gate.append((sPE, pe_stats))
                gate.append((sACT, a_evac["a3t"]))
            k.dma("gpsimd", w1b[slot], d["w1l"][hc].rearrange("d c o -> d (c o)"),
                  wsem, waits=gate)
            k.dma("gpsimd", w3b[slot], d["w3l"][hc].rearrange("d c o -> d (c o)"), wsem)
            k.dma("gpsimd", c.L16[:, 3328 + slot * 256:3328 + slot * 256 + 128],
                  d["b1t"][:, hc * 128:hc * 128 + 128], wsem)
            k.dma("gpsimd", c.L16[:, 3328 + slot * 256 + 128:3328 + slot * 256 + 256],
                  d["b3t"][:, hc * 128:hc * 128 + 128], wsem)
            k.dma("gpsimd", c.SMB[:, 128 + slot * 16:128 + slot * 16 + 16],
                  d["a2t"][hc * 128:hc * 128 + 128, :], wsem)
            wdone = k.cnt[wsem]
            pe = None
            ag = asl = None
            cons = cons_hc.get(hc - 1, ((0, 0), (0, 0)))
            for h in range(2):
                for ci in range(DC):
                    k.emit("tensor", lambda e, ci=ci, h=h, slot=slot: e.matmul(
                        c.ps[0 + h][:, :], w1b[slot][:, ci * 128:ci * 128 + 128],
                        xn[:, ci * 1024 + h * 512:ci * 1024 + h * 512 + 512],
                        start=(ci == 0), stop=False),
                        waits=([(wsem, wdone), (sDVE, ln_done),
                                (sDVE, cons[h][0]), (sACT, cons[h][1])]
                               if ci == 0 else []), inc=sPE)
                k.emit("tensor", lambda e, h=h, slot=slot: e.matmul(
                    c.ps[0 + h][:, :], c.L16[0:16, 3328 + slot * 256:3328 + slot * 256 + 128],
                    c.L16[0:16, 0 + h * 512:0 + h * 512 + 512], start=False, stop=True),
                    waits=[(sACT, a_evac["a1t"])], inc=sPE)
                for ci in range(DC):
                    k.emit("tensor", lambda e, ci=ci, h=h, slot=slot: e.matmul(
                        c.ps[2 + h][:, :], w3b[slot][:, ci * 128:ci * 128 + 128],
                        xn[:, ci * 1024 + h * 512:ci * 1024 + h * 512 + 512],
                        start=(ci == 0), stop=False), inc=sPE)
                pe = k.emit("tensor", lambda e, h=h, slot=slot: e.matmul(
                    c.ps[2 + h][:, :], c.L16[0:16, 3328 + slot * 256 + 128:3328 + slot * 256 + 256],
                    c.L16[0:16, 1024 + h * 512:1024 + h * 512 + 512], start=False, stop=True),
                    waits=[(sACT, a_evac["a3t"])], inc=sPE)
                asl = k.emit("scalar", lambda e, h=h: e.activation(
                    silu_scr[:, h * 512:h * 512 + 512], c.ps[0 + h][:, :], AF.Silu),
                    waits=[(sPE, pe)], inc=sACT)
                ag = k.emit("vector", lambda e, h=h, j=j: e.tensor_mul(
                    gsec[:, j * 1024 + h * 512:j * 1024 + h * 512 + 512],
                    silu_scr[:, h * 512:h * 512 + 512], c.ps[2 + h][:, :]),
                    waits=[(sACT, asl)], inc=sDVE)
                if h == 0:
                    cons_h0 = (ag, asl)
            cons_hc[hc] = (cons_h0, (ag, asl))
            last_hc_pe = pe
            for h in range(2):
                pe_lora2[hc] = k.emit("tensor", lambda e, h=h, j=j, slot=slot, hc=hc: e.matmul(
                    c.ps[6 + h][0:16, :], c.SMB[:, 128 + slot * 16:128 + slot * 16 + 16],
                    gsec[:, j * 1024 + h * 512:j * 1024 + h * 512 + 512],
                    start=(hc == 0), stop=(hc == HCN - 1), skip_group_check=True),
                    waits=[(sDVE, ag)], inc=sPE)

        for ot in range(DC):
            slot = ot % 2
            w2sem = f"w2{slot}"
            gate = [(sDVE, w2_cons.get((s, ot - 2), (0, 0))[0]),
                    (sACT, w2_cons.get((s, ot - 2), (0, 0))[1])]
            if s == 0 and ot < 2:
                gate.append((sACT, a_evac["a3t"]))
            k.dma("gpsimd", w2b[slot][:, 0:ssz * 128],
                  d["w2l"][ot][:, hc_base:hc_base + ssz, :].rearrange("d c o -> d (c o)"),
                  w2sem, waits=gate)
            if s == len(SEC_SIZES) - 1:
                k.dma("gpsimd", c.L16[:, 3072 + slot * 128:3072 + slot * 128 + 128],
                      d["b2t"][:, ot * 128:(ot + 1) * 128], w2sem)
            w2done = k.cnt[w2sem]
            pe = None
            for h in range(2):
                for jj in range(ssz):
                    pe = k.emit("tensor", lambda e, jj=jj, h=h, slot=slot, ssz=ssz: e.matmul(
                        c.ps[4 + h][:, :], w2b[slot][:, jj * 128:jj * 128 + 128],
                        gsec[:, jj * 1024 + h * 512:jj * 1024 + h * 512 + 512],
                        start=(jj == 0), stop=(jj == ssz - 1)),
                        waits=([(w2sem, w2done), (sDVE, w2_cons.get((s, ot - 1), (0, 0))[0]),
                                (sACT, w2_cons.get((s, ot - 1), (0, 0))[1])]
                               if (h == 0 and jj == 0) else []), inc=sPE)
            if s < len(SEC_SIZES) - 1:
                cpb = [c.af(0, 1), c.SINK[:, 0:1024]][ot % 2]
                dvc2 = None
                for h in range(2):
                    dvc2 = k.emit("vector", lambda e, h=h, cpb=cpb: e.tensor_copy(
                        cpb[:, h * 512:h * 512 + 512], c.ps[4 + h][:, :]),
                        waits=[(sPE, pe)] + ([(sDVE, w2_rmw.get((s, ot - 2), 0)),
                                              (sOut, k.cnt[sOut] if s == 0 and ot >= 2 else 0)]
                                             if h == 0 else []),
                        inc=sDVE)
                dv = dvc2
                if s > 0:
                    pl = k.dma("sync", part_f32, d["part"][ot * 128:(ot + 1) * 128, :], "pp",
                               waits=[(sOut, k.cnt[sOut])])
                    dv = k.emit("vector", lambda e, cpb=cpb: e.tensor_add(
                        part_f32, part_f32, cpb),
                        waits=[(sDVE, dvc2), ("pp", k.cnt["pp"])], inc=sDVE)
                    k.dma("sync", d["part"][ot * 128:(ot + 1) * 128, :], part_f32, sOut,
                          waits=[(sDVE, dv)])
                else:
                    k.dma("sync", d["part"][ot * 128:(ot + 1) * 128, :], cpb, sOut,
                          waits=[(sDVE, dvc2)])
                w2_rmw[(s, ot)] = dv
                w2_cons[(s, ot)] = (dvc2, 0)
            else:
                if lora2_evac is None:
                    for h in range(2):
                        lora2_evac = k.emit("scalar", lambda e, h=h: e.mul(
                            c.L16[:, 2048 + h * 512:2048 + h * 512 + 512],
                            c.ps[6 + h][0:16, :], LSC),
                            waits=[(sPE, pe_lora2[HCN - 1])], inc=sACT)
                for h in range(2):
                    pe = k.emit("tensor", lambda e, h=h, slot=slot: e.matmul(
                        c.ps[4 + h][:, :], c.L16[0:16, 3072 + slot * 128:3072 + slot * 128 + 128],
                        c.L16[0:16, 2048 + h * 512:2048 + h * 512 + 512],
                        start=False, stop=True, skip_group_check=True),
                        waits=[(sACT, lora2_evac)], inc=sPE)
                pl = k.dma("sync", part_f32, d["part"][ot * 128:(ot + 1) * 128, :], "pp",
                           waits=[(sACT, prev_part_user.get(ot - 1, 0)),
                                  (sOut, k.cnt[sOut])])
                xl = k.dma("sync", x_f32, d["x"][ot * 128:(ot + 1) * 128, :], "px",
                           waits=[(sDVE, w2_cons.get((s, ot - 1), (0, 0))[0])])
                dv = None
                for h in range(2):
                    hs = slice(h * 512, h * 512 + 512)
                    dv = k.emit("vector", lambda e, h=h, hs=hs: e.tensor_add(
                        part_f32[:, hs], part_f32[:, hs], c.ps[4 + h][:, :]),
                        waits=[(sPE, pe)] + ([("pp", k.cnt["pp"])] if h == 0 else []), inc=sDVE)
                ag1 = k.emit("scalar", lambda e: e.activation(
                    silu_scr, part_f32, AF.Silu), waits=[(sDVE, dv)], inc=sACT)
                prev_part_user[ot] = ag1
                dv = k.emit("vector", lambda e, ot=ot: e.tensor_mul(
                    silu_scr, xn[:, ot * 1024:(ot + 1) * 1024], silu_scr),
                    waits=[(sACT, ag1)], inc=sDVE)
                dv = k.emit("vector", lambda e: e.tensor_add(
                    silu_scr, silu_scr, x_f32),
                    waits=[("px", k.cnt["px"])], inc=sDVE)
                x1_store_last = k.dma("sync", d["x1"][ot * 128:(ot + 1) * 128, :],
                                      silu_scr, sOut, waits=[(sDVE, dv)])
                par2 = ot % 2
                bq = k.emit("vector", lambda e, par2=par2: e.tensor_copy(
                    c.ab(64 + par2, 1), silu_scr),
                    waits=[(sPE, pe_stat_mm.get(ot - 2, 0))], inc=sDVE)
                sqa = k.emit("scalar", lambda e, par2=par2: e.activation(
                    c.ab(66 + par2, 1), c.ab(64 + par2, 1), AF.Square),
                    waits=[(sDVE, bq)] + ([(sACT, lora2_evac)] if ot == 0 else []),
                    inc=sACT)
                pstat = None
                for h in range(2):
                    k.emit("tensor", lambda e, h=h, par2=par2, ot=ot: e.matmul(
                        c.ps[2 + h][0:1, :], c.ones_col,
                        c.ab(64 + par2, 1)[:, h * 512:h * 512 + 512],
                        start=(ot == 0), stop=(ot == DC - 1)),
                        waits=[(sDVE, bq), (sIn, c.const_done)], inc=sPE)
                for h in range(2):
                    pstat = k.emit("tensor", lambda e, h=h, par2=par2, ot=ot: e.matmul(
                        c.ps[6 + h][0:1, :], c.ones_col,
                        c.ab(66 + par2, 1)[:, h * 512:h * 512 + 512],
                        start=(ot == 0), stop=(ot == DC - 1)),
                        waits=[(sACT, sqa)], inc=sPE)
                pe_stat_mm[ot] = pstat
                w2_cons[(s, ot)] = (dv, ag1)
        hc_base += ssz
        if stage <= 1 + s:
            return

    if stage <= 4:
        return
    xn1 = c.ab(32, 32)
    rm = c.ROWS[0:1, 0:1024]
    ra = c.ROWS[0:1, 1024:2048]
    rt = c.SINK[0:1, 0:1024]
    pe_stats1 = k.cnt[sPE]
    for h in range(2):
        k.emit("vector", lambda e, h=h: e.tensor_copy(rm[:, h * 512:h * 512 + 512],
                                                      c.ps[2 + h][0:1, :]),
               waits=[(sPE, pe_stats1)], inc=sDVE)
    for h in range(2):
        k.emit("vector", lambda e, h=h: e.tensor_copy(rt[:, h * 512:h * 512 + 512],
                                                      c.ps[6 + h][0:1, :]), inc=sDVE)
    k.emit("vector", lambda e: e.tensor_scalar_mul(rm, rm, 1.0 / D), inc=sDVE)
    k.emit("vector", lambda e: e.tensor_scalar_mul(rt, rt, 1.0 / D), inc=sDVE)
    k.emit("vector", lambda e: e.tensor_mul(ra, rm, rm), inc=sDVE)
    dv = k.emit("vector", lambda e: e.tensor_sub(rt, rt, ra), inc=sDVE)
    a1 = k.emit("scalar", lambda e: e.activation(ra, rt, AF.Sqrt, bias=c.eps_ln),
                waits=[(sDVE, dv), (sIn, c.const_done)], inc=sACT)
    dv = k.emit("vector", lambda e: e.reciprocal(ra, ra), waits=[(sACT, a1)], inc=sDVE)
    k.wait("vector", sDVE, dv)
    dv = k.emit("vector", lambda e: e.scalar_tensor_tensor(rm, rm, -1.0, ra, ALU.mult, ALU.mult),
                inc=sDVE)
    pe_rep = None
    for h in range(2):
        k.emit("tensor", lambda e, h=h: e.matmul(
            c.ps[0 + h][:, :], c.ones_row, ra[:, h * 512:h * 512 + 512], start=True, stop=True),
            waits=[(sDVE, dv)], inc=sPE)
    for h in range(2):
        pe_rep = k.emit("tensor", lambda e, h=h: e.matmul(
            c.ps[2 + h][:, :], c.ones_row, rm[:, h * 512:h * 512 + 512], start=True, stop=True),
            inc=sPE)
    x1c = [c.af(1, 1), c.af(3, 1)]
    ln1_done = None
    apl = {}
    for ci in range(DC):
        par = ci % 2
        k.dma("sync", x1c[par], d["x1"][ci * 128:(ci + 1) * 128, :], "x1r",
              waits=([(sOut, k.cnt[sOut])] if ci < 2 else
                     [(sDVE, apl.get(ci - 2, 0))]))
        x1r = k.cnt["x1r"]
        for h in range(2):
            sl = slice(ci * 1024 + h * 512, ci * 1024 + h * 512 + 512)
            k.emit("vector", lambda e, par=par, h=h: e.tensor_mul(
                c.af(0, 1)[:, 0:512], x1c[par][:, h * 512:h * 512 + 512], c.ps[0 + h][:, :]),
                waits=[(sPE, pe_rep), ("x1r", x1r)], inc=sDVE)
            ln1_done = k.emit("vector", lambda e, sl=sl, h=h: e.tensor_add(
                xn1[:, sl], c.af(0, 1)[:, 0:512], c.ps[2 + h][:, :]), inc=sDVE)
        apl[ci] = ln1_done
    c.apl = apl
    for i in range(4):
        k.dma("sync", d["xn1"][i * 1024:(i + 1) * 1024, :]
              .rearrange("(c d) t -> d c t", d=128),
              sview(c.ab(32 + 8 * i, 8), 1024), sOut,
              waits=[(sDVE, ln1_done)])
    c.ln1_done = ln1_done
    c.xn1 = xn1
def emit_p2(nc, k, stack, c, load_xn1=True, core=None, stage2=99, part="all"):
    d = c.d
    xn1 = c.ab(32, 32)
    gqa = c.SM[:, 131:143]
    gkva = c.SM[:, 143:147]
    Rt = c.SMB[0:64, 16:80]
    ld2 = [
        k.dma("sync", c.SM[:, 131:143], d["gqa"], sIn),
        k.dma("sync", c.SM[:, 143:147], d["gkva"], sIn),
        k.dma("sync", c.SMB[0:64, 16:80], d["rt"], sIn),
        k.dma("sync", c.AF32[0:64, 1024:2048], d["cos"], sIn),
        k.dma("sync", c.AF32[0:64, 2048:3072], d["sin"], sIn),
    ]
    cos_sb = c.AF32[0:64, 1024:2048]
    sin_sb = c.AF32[0:64, 2048:3072]
    p2c = k.cnt[sIn]
    c.const_done = p2c

    if load_xn1:
        for i in range(4):
            k.dma("sync", sview(c.ab(32 + 8 * i, 8), 1024),
                  d["xn1"][i * 1024:(i + 1) * 1024, :].rearrange("(c d) t -> d c t", d=128),
                  "x")
    xn1done = k.cnt["x"]
    if stage2 <= -5:
        k.wait("gpsimd", "x", xn1done)
        k.wait("sync", sIn, k.cnt[sIn])
        return

    wslt = [c.ab(64, 4), c.ab(68, 4)]
    sqscr = c.af(3, 1)
    def rms_rows(pe_stat, denom, r0):
        rr = c.ROWS[0:1, r0:r0 + 1024]
        dvc = None
        for h in range(2):
            dvc = k.emit("vector", lambda e, h=h: e.tensor_copy(rr[:, h * 512:h * 512 + 512],
                                                                c.ps[6 + h][0:1, :]),
                         waits=[(sPE, pe_stat)], inc=sDVE)
        sc = k.emit("scalar", lambda e: e.activation(rr, rr, AF.Sqrt, scale=1.0 / denom),
                    waits=[(sDVE, dvc)], inc=sACT)
        dv = k.emit("vector", lambda e: e.tensor_scalar_add(rr, rr, 1e-6), waits=[(sACT, sc)], inc=sDVE)
        dv = k.emit("vector", lambda e: e.reciprocal(rr, rr), inc=sDVE)
        k.wait("vector", sDVE, dv)
        pe_rep = None
        for h in range(2):
            pe_rep = k.emit("tensor", lambda e, h=h: e.matmul(
                c.ps[0 + h][:, :], c.ones_row, rr[:, h * 512:h * 512 + 512],
                start=True, stop=True), waits=[(sDVE, dv)], inc=sPE)
        return pe_rep

    kcn = c.ab(4, 8)[:, 0:4 * 1024]
    krb = c.ab(8, 1)[0:64, :]
    kev = {}
    pe_kv = {}
    for ot in range(5):
        par = ot % 2
        wsem = f"wh{par}"
        nr = 128 if ot < 4 else 64
        k.dma("gpsimd", wslt[par][:, 0:DC * nr],
              (d["wkval"][ot].rearrange("d c o -> d (c o)") if ot < 4
               else d["wkvrl"].rearrange("d c o -> d (c o)")), wsem,
              waits=[(sPE, pe_kv.get(ot - 2, 0))])
        wdone = k.cnt[wsem]
        apl = getattr(c, "apl", {})
        pe = None
        for h in range(2):
            for ci in range(DC):
                pe = k.emit("tensor", lambda e, ci=ci, h=h, par=par, nr=nr: e.matmul(
                    c.ps[0 + h][0:nr, :], wslt[par][:, ci * nr:ci * nr + nr],
                    xn1[:, ci * 1024 + h * 512:ci * 1024 + h * 512 + 512],
                    start=(ci == 0), stop=(ci == DC - 1)),
                    waits=([(wsem, wdone), (sACT, kev.get(ot - 1, (0, 0))[0] or 0),
                            (sDVE, kev.get(ot - 1, (0, 0))[1]),
                            (sDVE, apl.get(0, 0) if ot == 0 else 0)]
                           if (h == 0 and ci == 0) else
                           ([(sDVE, apl.get(ci, 0))] if (h == 0 and ot == 0) else [])),
                    inc=sPE)
        pe_kv[ot] = pe
        aq = dq = None
        sqb = c.ab(24 + par, 1)
        for h in range(2):
            if ot < 4:
                dq = k.emit("vector", lambda e, h=h, ot=ot: e.tensor_copy(
                    kcn[:, ot * 1024 + h * 512:ot * 1024 + h * 512 + 512], c.ps[0 + h][:, :]),
                    waits=[(sPE, pe)], inc=sDVE)
                aq = k.emit("scalar", lambda e, h=h, ot=ot, sqb=sqb: e.activation(
                    sqb[:, h * 512:h * 512 + 512],
                    kcn[:, ot * 1024 + h * 512:ot * 1024 + h * 512 + 512], AF.Square),
                    waits=[(sDVE, dq)], inc=sACT)
            else:
                dq = k.emit("vector", lambda e, h=h: e.tensor_copy(
                    krb[:, h * 512:h * 512 + 512], c.ps[0 + h][0:64, :]),
                    waits=[(sPE, pe)], inc=sDVE)
        if ot < 4:
            for h in range(2):
                k.emit("tensor", lambda e, h=h, ot=ot, sqb=sqb: e.matmul(
                    c.ps[6 + h][0:1, :], c.ones_col, sqb[:, h * 512:h * 512 + 512],
                    start=(ot == 0), stop=(ot == 3)),
                    waits=[(sACT, aq)], inc=sPE)
        kev[ot] = (aq if aq else 0, dq)
    pe_kstat = k.cnt[sPE]
    pe_rep = rms_rows(pe_kstat, KVR, 1024)
    dv_kcn = None
    for ot in range(4):
        for h in range(2):
            sl = slice(ot * 1024 + h * 512, ot * 1024 + h * 512 + 512)
            k.emit("vector", lambda e, sl=sl, h=h: e.tensor_mul(
                sqscr[:, h * 512:h * 512 + 512], kcn[:, sl], c.ps[0 + h][:, :]),
                waits=[(sPE, pe_rep)], inc=sDVE)
            dv_kcn = k.emit("vector", lambda e, sl=sl, ot=ot, h=h: e.tensor_scalar(
                kcn[:, sl], sqscr[:, h * 512:h * 512 + 512], gkva[:, ot:ot + 1], None, ALU.mult),
                inc=sDVE)

    pe_rot = None
    for h in range(2):
        pe_rot = k.emit("tensor", lambda e, h=h: e.matmul(
            c.ps[2][0:64, :] if h == 0 else c.ps[3][0:64, :], Rt,
            krb[:, h * 512:h * 512 + 512], start=True, stop=True),
            waits=[(sDVE, kev[4][1]), (sIn, p2c)], inc=sPE)
    dv = None
    for h in range(2):
        hs = slice(h * 512, h * 512 + 512)
        k.emit("vector", lambda e, h=h, hs=hs: e.tensor_mul(
            c.af(3, 1)[0:64, hs], c.ps[2 + h][0:64, :], sin_sb[:, hs]),
            waits=[(sPE, pe_rot)], inc=sDVE)
        k.emit("vector", lambda e, h=h, hs=hs: e.tensor_mul(
            c.af(0, 1)[0:64, hs], krb[:, hs], cos_sb[:, hs]), inc=sDVE)
        dv = k.emit("vector", lambda e, h=h, hs=hs: e.tensor_add(
            c.ab(9, 1)[0:64, hs], c.af(0, 1)[0:64, hs], c.af(3, 1)[0:64, hs]), inc=sDVE)
    k.dma("sync", d["krc"], c.ab(9, 1)[0:64, :], "ct0", waits=[(sDVE, dv)])

    if stage2 <= -1:
        return
    stg2 = [c.ab(10, 1), c.ab(11, 1)]
    pe_kn = {}
    st_kn = {}
    for hh in range(NH):
        par = hh % 2
        wsem = f"wh{par}"
        k.dma("gpsimd", wslt[par][:, 0:KVRC * 128],
              d["wknl"][hh].rearrange("d c o -> d (c o)"), wsem,
              waits=[(sPE, pe_kn.get(hh - 2, pe_rot if hh < 2 else 0))])
        wdone = k.cnt[wsem]
        pe = None
        for h in range(2):
            for ci in range(KVRC):
                pe = k.emit("tensor", lambda e, ci=ci, h=h, par=par: e.matmul(
                    c.ps[0 + h][:, :], wslt[par][:, ci * 128:ci * 128 + 128],
                    kcn[:, ci * 1024 + h * 512:ci * 1024 + h * 512 + 512],
                    start=(ci == 0), stop=(ci == KVRC - 1)),
                    waits=([(wsem, wdone), (sDVE, dv_kcn),
                            (sDVE, st_kn.get(hh - 1, (0, 0))[1])]
                           if (h == 0 and ci == 0) else []), inc=sPE)
        pe_kn[hh] = pe
        dv = None
        for h in range(2):
            dv = k.emit("vector", lambda e, h=h, par=par: e.tensor_copy(
                stg2[par][:, h * 512:h * 512 + 512], c.ps[0 + h][:, :]),
                waits=[(sPE, pe), (f"ct{par}", k.cnt[f"ct{par}"])], inc=sDVE)
        sid = k.dma("sync", d["knc"][hh * 128:(hh + 1) * 128, :], stg2[par], f"ct{par}",
                    waits=[(sDVE, dv)])
        st_kn[hh] = (sid, dv)

    pe_v = {}
    st_v = {}
    for hg in range(8):
        par = hg % 2
        wsem = f"wh{par}"
        k.dma("gpsimd", wslt[par][:, 0:4 * 512],
              d["wvl"][hg].rearrange("d c o -> d (c o)"), wsem,
              waits=[(sPE, pe_v.get((hg - 2, 7), pe_kn.get(31, 0) if hg < 2 else 0))])
        wdone = k.cnt[wsem]
        for ts in range(8):
            pe = None
            for ci in range(KVRC):
                pe = k.emit("tensor", lambda e, ci=ci, par=par, ts=ts: e.matmul(
                    c.ps[2 + ts % 2][:, :],
                    kcn[:, ci * 1024 + ts * 128:ci * 1024 + ts * 128 + 128],
                    wslt[par][:, ci * 512:(ci + 1) * 512],
                    start=(ci == 0), stop=(ci == KVRC - 1)),
                    waits=([(wsem, wdone), (sDVE, st_v.get((hg, ts - 2),
                            st_v.get((hg - 1, 6 + ts % 2), (0, 0)))[1])]
                           if ci == 0 else []), inc=sPE)
            pe_v[(hg, ts)] = pe
            dv = k.emit("vector", lambda e, par=par, ts=ts: e.tensor_copy(
                stg2[ts % 2][:, 0:512], c.ps[2 + ts % 2][:, :]),
                waits=[(sPE, pe), (f"ct{ts % 2}", k.cnt[f"ct{ts % 2}"])], inc=sDVE)
            sid = k.dma("sync", d["vc"][ts * 128:(ts + 1) * 128, hg * 512:(hg + 1) * 512],
                        stg2[ts % 2][:, 0:512], f"ct{ts % 2}", waits=[(sDVE, dv)])
            st_v[(hg, ts)] = (sid, dv)

    k.wait("gpsimd", "ct0", k.cnt["ct0"])
    k.wait("gpsimd", "ct1", k.cnt["ct1"])
    rg = [list(range(NCORES))]
    k.emit("gpsimd", lambda e: e.collective_compute(
        "AllGather", ALU.bypass, replica_groups=rg,
        ins=[d["knc"].opt()], outs=[d["kng"].opt()]), inc="cc")
    k.emit("gpsimd", lambda e: e.collective_compute(
        "AllGather", ALU.bypass, replica_groups=rg,
        ins=[d["vc"].opt()], outs=[d["vg"].opt()]), inc="cc")
    cc_done = k.emit("gpsimd", lambda e: e.collective_compute(
        "AllGather", ALU.bypass, replica_groups=rg,
        ins=[d["krc"].opt()], outs=[d["krg"].opt()]), inc="cc")
    if c.overlap_hook is not None:
        c.overlap_hook()

    qc = c.ab(0, 12)
    wslt = [c.ab(64, 4), c.ab(68, 4)]
    sqscr = c.af(3, 1)
    evq = {}
    pe_q = {}
    dve_hmat = k.cnt[sDVE]
    for ot in range(QRC):
        par = ot % 2
        wsem = f"wh{par}"
        gate = [(sPE, pe_q.get(ot - 2, pe_v.get((7, 7), 0) if ot < 2 else 0))]
        k.dma("gpsimd", wslt[par], d["wqal"][ot].rearrange("d c o -> d (c o)"),
              wsem, waits=gate)
        wdone = k.cnt[wsem]
        pe = None
        for h in range(2):
            for ci in range(DC):
                pe = k.emit("tensor", lambda e, ci=ci, h=h, par=par: e.matmul(
                    c.ps[0 + h][:, :], wslt[par][:, ci * 128:ci * 128 + 128],
                    xn1[:, ci * 1024 + h * 512:ci * 1024 + h * 512 + 512],
                    start=(ci == 0), stop=(ci == DC - 1)),
                    waits=([(wsem, wdone), ("x", xn1done),
                            (sDVE, dve_hmat if ot == 0 else 0),
                            (sACT, evq.get(ot - 1, (0, 0))[0]),
                            (sDVE, evq.get(ot - 1, (0, 0))[1])]
                           if (h == 0 and ci == 0) else []), inc=sPE)
        pe_q[ot] = pe
        aq = dq = None
        sqb = c.W2SB[:, 1024:2048]
        for h in range(2):
            dq = k.emit("vector", lambda e, h=h, ot=ot: e.tensor_copy(
                qc[:, ot * 1024 + h * 512:ot * 1024 + h * 512 + 512], c.ps[0 + h][:, :]),
                waits=[(sPE, pe)], inc=sDVE)
            aq = k.emit("scalar", lambda e, h=h, ot=ot, sqb=sqb: e.activation(
                sqb[:, h * 512:h * 512 + 512],
                qc[:, ot * 1024 + h * 512:ot * 1024 + h * 512 + 512], AF.Square),
                waits=[(sDVE, dq)], inc=sACT)
        for h in range(2):
            k.emit("tensor", lambda e, h=h, ot=ot, sqb=sqb: e.matmul(
                c.ps[6 + h][0:1, :], c.ones_col, sqb[:, h * 512:h * 512 + 512],
                start=(ot == 0), stop=(ot == QRC - 1)),
                waits=[(sACT, aq), (sIn, c.const_done)], inc=sPE)
        evq[ot] = (aq, dq)
    pe_qstat = k.cnt[sPE]
    if stage2 <= -4:
        k.wait("tensor", sPE, pe_qstat)
        return

    pe_rep = rms_rows(pe_qstat, QR, 0)
    qcn = c.ab(12, 12)
    dv_qcn = None
    for ot in range(QRC):
        for h in range(2):
            sl = slice(ot * 1024 + h * 512, ot * 1024 + h * 512 + 512)
            k.emit("vector", lambda e, sl=sl, h=h: e.tensor_mul(
                sqscr[:, h * 512:h * 512 + 512], qc[:, sl], c.ps[0 + h][:, :]),
                waits=[(sPE, pe_rep), (sIn, p2c)], inc=sDVE)
            dv_qcn = k.emit("vector", lambda e, sl=sl, ot=ot, h=h: e.tensor_scalar(
                qcn[:, sl], sqscr[:, h * 512:h * 512 + 512], gqa[:, ot:ot + 1], None, ALU.mult),
                inc=sDVE)

    if stage2 <= -3:
        return
    stg = [c.ab(0, 1), c.ab(1, 1)]
    pe_qb = {}
    st_q = {}
    for ot in range(48):
        par = ot % 2
        wsem = f"wh{par}"
        k.dma("gpsimd", wslt[par][:, 0:QRC * 128],
              d["wqbl"][ot].rearrange("d c o -> d (c o)"), wsem,
              waits=[(sPE, pe_qb.get(ot - 2, pe_rep if ot < 2 else 0))])
        wdone = k.cnt[wsem]
        pe = None
        for h in range(2):
            for ci in range(QRC):
                pe = k.emit("tensor", lambda e, ci=ci, h=h, par=par: e.matmul(
                    c.ps[2 + h][:, :], wslt[par][:, ci * 128:ci * 128 + 128],
                    qcn[:, ci * 1024 + h * 512:ci * 1024 + h * 512 + 512],
                    start=(ci == 0), stop=(ci == QRC - 1)),
                    waits=([(wsem, wdone), (sDVE, dv_qcn),
                            (sDVE, st_q.get(ot - 1, (0, 0))[1])]
                           if (h == 0 and ci == 0) else []), inc=sPE)
        pe_qb[ot] = pe
        dv = None
        for h in range(2):
            dv = k.emit("vector", lambda e, h=h, par=par: e.tensor_copy(
                stg[par][:, h * 512:h * 512 + 512], c.ps[2 + h][:, :]),
                waits=[(sPE, pe), ("sq", k.cnt["sq"])], inc=sDVE)
        sq_id = k.dma("sync", d["q"][ot * 128:(ot + 1) * 128, :], stg[par], "sq",
                      waits=[(sDVE, dv)])
        st_q[ot] = (sq_id, dv)
    q_done = k.cnt["sq"]

    if stage2 <= -2:
        return
    if stage2 <= 0:
        return

    c.p2s = dict(q_done=q_done, dvq_const=p2c)
    if part == "proj_stop":
        return
    kh = [c.ab(0, 8), c.ab(8, 8)]
    vh = [c.ab(16, 8), c.ab(24, 8)]
    krh = c.ab(32, 8)[0:64, :]
    qn = [c.W2SB[:, 0:1024], c.W2SB[:, 1024:2048]]
    exps = [c.ab(42, 1), c.ab(43, 1), c.ab(41, 1)]
    ao = c.ab(44, 32)
    qraw = [c.ab(76, 1)[0:64, :], c.ab(77, 1)[0:64, :]]
    qrop = [c.ab(78, 1)[0:64, :], c.ab(79, 1)[0:64, :]]
    lacc = c.af(0, 1)
    rl = c.af(3, 1)
    kng4 = d["kng"].rearrange("(rr hh dd) t -> hh dd rr t", hh=NH, dd=128)
    vg4 = d["vg"].rearrange("(kt kk) (hh dv) -> hh kk kt dv", kk=128, hh=NH)

    k.wait("sync", "cc", cc_done)
    krh_ld = k.dma("sync", sview(krh, 1024), d["krg"].rearrange("(rr dd) t -> dd rr t", dd=64), "krh")
    pe_av = {}
    cons_head = {}
    norm_done = {}
    pe_s_last = {}
    rope_done = {}
    pe_l_done = {}

    def emit_rope(hh, first_waits=()):
        par = hh % 2
        pe_rot = None
        for h in range(2):
            pe_rot = k.emit("tensor", lambda e, h=h, par=par: e.matmul(
                c.ps[6][0:64, :] if h == 0 else c.ps[7][0:64, :], Rt,
                qraw[par][:, h * 512:h * 512 + 512], start=True, stop=True),
                waits=((list(first_waits) + [(sIn, p2c)]) if h == 0 else []), inc=sPE)
        dvq = None
        for h in range(2):
            hs = slice(h * 512, h * 512 + 512)
            psb = c.ps[6] if h == 0 else c.ps[7]
            k.emit("vector", lambda e, hs=hs, psb=psb: e.tensor_mul(
                rl[0:64, hs], psb[0:64, :], sin_sb[:, hs]),
                waits=([(sPE, pe_rot)] + ([(sPE, pe_s_last.get(hh - 2, 0))] if h == 0 else [])),
                inc=sDVE)
            k.emit("vector", lambda e, hs=hs, par=par, psb=psb: e.tensor_mul(
                psb[0:64, :], qraw[par][:, hs], cos_sb[:, hs]), inc=sDVE)
            dvq = k.emit("vector", lambda e, hs=hs, par=par, psb=psb: e.tensor_add(
                qrop[par][:, hs], rl[0:64, hs], psb[0:64, :]), inc=sDVE)
        rope_done[hh] = dvq
        return dvq

    def emit_qload(hh, gate_scores):
        par = hh % 2
        asem = f"at{par}"
        qw = [(sPE, gate_scores)] if gate_scores else []
        if hh == 0:
            qw.append(("sq", q_done))
        k.dma("sync", qn[par], d["q"][hh * QK:hh * QK + 128, :], asem, waits=qw)
        k.dma("sync", qraw[par], d["q"][hh * QK + 128:hh * QK + QK, :], asem)
        return k.cnt[asem]

    q0_done = emit_qload(0, 0)
    for hh in range(NH):
        par = hh % 2
        asem = f"at{par}"
        gate = [(sPE, pe_av.get(hh - 2, 0)), (sDVE, cons_head.get(hh - 2, 0))]
        if hh < 2:
            gate.append(("sq", q_done))
        k.dma("sync", sview(kh[par], 1024), kng4[hh], asem, waits=gate)
        k.dma("sync", vh[par][:, :].rearrange("d (kt dv) -> d kt dv", dv=128), vg4[hh], asem)
        adone = k.cnt[asem]
        if getattr(c, "pending_tail", None) is not None:
            c.pending_tail()
            pe_l_done[hh - 1] = k.cnt[sPE]
            c.pending_tail = None
        if hh == 0:
            dvq = emit_rope(0, first_waits=[(asem, q0_done), (sDVE, k.cnt[sDVE])])
        else:
            dvq = rope_done[hh]
        exp_done = {}
        pool_done = {}
        pe_s = {}
        pe_avk = {}
        ace = k.cnt[sACT]

        def emit_scores(kt):
            sp = kt % 2
            pe = None
            for h in range(2):
                k.emit("tensor", lambda e, h=h, kt=kt, sp=sp, par=par: e.matmul(
                    c.ps[2 * sp + h][:, :], kh[par][:, kt * 128:(kt + 1) * 128],
                    qn[par][:, h * 512:h * 512 + 512], start=True, stop=False),
                    waits=(([(sACT, exp_done.get(kt - 2, ace if kt < 2 else 0)),
                             (sDVE, dvq)]) if h == 0 else []), inc=sPE)
            for h in range(2):
                pe = k.emit("tensor", lambda e, h=h, kt=kt, sp=sp, par=par: e.matmul(
                    c.ps[2 * sp + h][:, :], krh[:, kt * 128:(kt + 1) * 128],
                    qrop[par][:, h * 512:h * 512 + 512], start=False, stop=True),
                    waits=([("krh", krh_ld)] if (hh == 0 and kt == 0 and h == 0) else []),
                    inc=sPE)
            pe_s[kt] = pe
            pe_s_last[hh] = pe

        def emit_exp_pool(kt):
            ae = None
            for h in range(2):
                ae = k.emit("scalar", lambda e, h=h, kt=kt: e.activation(
                    exps[kt % 3][:, h * 512:h * 512 + 512], c.ps[2 * (kt % 2) + h][:, :],
                    AF.Exp, scale=float(QK) ** -0.5),
                    waits=[(sPE, pe_s[kt])] + ([(sPE, pe_avk.get(kt - 3, 0)),
                                                (sPOOL, pool_done.get(kt - 3, (0, 0))[0]),
                                                (sDVE, pool_done.get(kt - 3, (0, 0))[1])]
                           if h == 0 else []),
                    inc=sACT)
            exp_done[kt] = ae
            if kt == 0:
                pd = k.emit("gpsimd", lambda e: e.tensor_copy(lacc[:, 0:512], exps[0][:, 0:512]),
                            waits=[(sACT, ae), (sDVE, dvq),
                                   (sPE, pe_l_done.get(hh - 1, 0))], inc=sPOOL)
                dd_ = k.emit("vector", lambda e: e.tensor_copy(lacc[:, 512:1024], exps[0][:, 512:1024]),
                             waits=[(sACT, ae), (sPE, pe_l_done.get(hh - 1, 0))], inc=sDVE)
            else:
                pd = k.emit("gpsimd", lambda e, kt=kt: e.tensor_add(
                    lacc[:, 0:512], lacc[:, 0:512], exps[kt % 3][:, 0:512]),
                    waits=[(sACT, ae)], inc=sPOOL)
                dd_ = k.emit("vector", lambda e, kt=kt: e.tensor_add(
                    lacc[:, 512:1024], lacc[:, 512:1024], exps[kt % 3][:, 512:1024]),
                    waits=[(sACT, ae)], inc=sDVE)
            pool_done[kt] = (pd, dd_)

        def emit_av(kt):
            pe2 = None
            for h in range(2):
                pe2 = k.emit("tensor", lambda e, h=h, kt=kt, par=par: e.matmul(
                    c.ps[4 + h][:, :], vh[par][:, kt * 128:(kt + 1) * 128],
                    exps[kt % 3][:, h * 512:h * 512 + 512],
                    start=(kt == 0), stop=(kt == KTN - 1)),
                    waits=([(sACT, exp_done[kt])] if h == 0 else []) +
                          ([(sDVE, cons_head.get(hh - 1, 0))] if (kt == 0 and h == 0) else []),
                    inc=sPE)
            pe_avk[kt] = pe2

        emit_scores(0)
        emit_exp_pool(0)
        emit_scores(1)
        emit_exp_pool(1)
        if hh + 1 < NH:
            nq = emit_qload(hh + 1, pe_s_last.get(hh - 1, 0))
            emit_rope(hh + 1, first_waits=[(f"at{(hh + 1) % 2}", nq),
                                           (sDVE, norm_done.get(hh - 1, 0))])
        for kt in range(2, KTN):
            emit_scores(kt)
            emit_exp_pool(kt)
            emit_av(kt - 2)
            if kt == 7 and getattr(c, "pending_tail2", None) is not None:
                c.pending_tail2()
                c.pending_tail2 = None
        emit_av(KTN - 2)
        emit_av(KTN - 1)
        pe_av[hh] = pe_avk[KTN - 1]
        evo = None
        for h in range(2):
            evo = k.emit("vector", lambda e, h=h, hh=hh: e.tensor_copy(
                ao[:, hh * 1024 + h * 512:hh * 1024 + h * 512 + 512],
                c.ps[4 + h][:, :]),
                waits=([(sPE, pe_av[hh])] if h == 0 else []), inc=sDVE)
        cons_head[hh] = evo

        def head_tail(hh=hh, pool_last=pool_done[KTN - 1]):
            pe_l = None
            for h in range(2):
                pe_l = k.emit("tensor", lambda e, h=h: e.matmul(
                    c.ps[6 + h][0:1, :], c.ones_col_f, lacc[:, h * 512:h * 512 + 512],
                    start=True, stop=True),
                    waits=[(sPOOL, pool_last[0]), (sDVE, pool_last[1])]
                          + ([(sDVE, norm_done.get(hh - 1, 0)),
                              (sDVE, rope_done.get(hh + 1, 0))] if h == 0 else []),
                    inc=sPE)
            rr = c.ROWS[0:1, 0:1024]
            dv = None
            for h in range(2):
                dv = k.emit("vector", lambda e, h=h: e.tensor_copy(
                    rr[:, h * 512:h * 512 + 512], c.ps[6 + h][0:1, :]),
                    waits=[(sPE, pe_l)], inc=sDVE)
            dv = k.emit("vector", lambda e: e.reciprocal(rr, rr), inc=sDVE)
            k.wait("vector", sDVE, dv)

            def head_tail2(hh=hh, dv=dv):
                pe_rl = None
                for h in range(2):
                    pe_rl = k.emit("tensor", lambda e, h=h: e.matmul(
                        c.ps[6 + h][:, :], c.ones_row, rr[:, h * 512:h * 512 + 512],
                        start=True, stop=True), waits=[(sDVE, dv)], inc=sPE)
                for h in range(2):
                    norm_done[hh] = k.emit("vector", lambda e, h=h, hh=hh: e.tensor_mul(
                        ao[:, hh * 1024 + h * 512:hh * 1024 + h * 512 + 512],
                        ao[:, hh * 1024 + h * 512:hh * 1024 + h * 512 + 512],
                        c.ps[6 + h][:, :]),
                        waits=[(sPE, pe_rl)], inc=sDVE)
            c.pending_tail2 = head_tail2
        c.pending_tail = head_tail

    if stage2 <= 1:
        for i in range(4):
            k.dma("sync", d["s1"][i * 1024:(i + 1) * 1024, :]
                  .rearrange("(c d) t -> d c t", d=128),
                  sview(c.ab(44 + 8 * i, 8), 1024), sOut,
                  waits=[(sDVE, cons_head[NH - 1])])
        return

    if getattr(c, "pending_tail", None) is not None:
        c.pending_tail()
        c.pending_tail = None
    if getattr(c, "pending_tail2", None) is not None:
        c.pending_tail2()
        c.pending_tail2 = None

    woslt = [c.ab(32, 4), c.ab(36, 4)]
    sstg = [c.af(1, 1), c.af(2, 1)]
    pe_wo = {}
    st_s1 = {}
    for ot in range(DC):
        par = ot % 2
        wsem = f"wh{par}"
        k.dma("gpsimd", woslt[par], d["wol"][ot].rearrange("d c o -> d (c o)"),
              wsem, waits=[(sPE, pe_wo.get(ot - 2, pe_av.get(NH - 1, 0) if ot < 2 else 0))])
        wdone = k.cnt[wsem]
        pe = None
        for h in range(2):
            for ci in range(DC):
                pe = k.emit("tensor", lambda e, ci=ci, h=h, par=par: e.matmul(
                    c.ps[0 + h][:, :], woslt[par][:, ci * 128:ci * 128 + 128],
                    ao[:, ci * 1024 + h * 512:ci * 1024 + h * 512 + 512],
                    start=(ci == 0), stop=(ci == DC - 1)),
                    waits=([(wsem, wdone), (sDVE, norm_done[NH - 1]),
                            (sACT, k.cnt[sACT] if ot == 0 else 0),
                            (sDVE, st_s1.get(ot - 1, (0, 0))[1])]
                           if (h == 0 and ci == 0) else []), inc=sPE)
        pe_wo[ot] = pe
        dv = None
        for h in range(2):
            dv = k.emit("vector", lambda e, h=h, par=par: e.tensor_copy(
                sstg[par][:, h * 512:h * 512 + 512], c.ps[0 + h][:, :]),
                waits=[(sPE, pe), ("so", k.cnt["so"])], inc=sDVE)
        sid = k.dma("sync", d["s1"][ot * 128:(ot + 1) * 128, :], sstg[par], "so",
                    waits=[(sDVE, dv)])
        st_s1[ot] = (sid, dv)
    c.s1_done = k.cnt["so"]
    c.att_done_pe = pe_av.get(NH - 1, 0)
    c.att_done_dve = norm_done.get(NH - 1, 0)
    c.wo_done_pe = pe_wo.get(DC - 1, 0)


def build_p2(stage2=99):
    nc = bass.Bass("TRN2", target_bir_lowering=False)
    c = Ctx()
    c.d = {}
    d = c.d
    dr = lambda name, shape, dt, kind: nc.dram_tensor(name, shape, dt, kind=kind).ap()
    d["xn1"] = dr("xn1", [D, T], BF16, "ExternalInput")
    for nm, sh in (("wqal", [QRC, 128, DC, 128]), ("wqbl", [48, 128, QRC, 128]),
                   ("wkval", [4, 128, DC, 128]), ("wkvrl", [128, DC, 64]),
                   ("wknl", [NH, 128, KVRC, 128]), ("wvl", [8, 128, KVRC, 512]),
                   ("wol", [DC, 128, DC, 128])):
        d[nm] = dr(nm, sh, F32, "ExternalInput")
    d["gqa"] = dr("gqa", [128, QRC], F32, "ExternalInput")
    d["gkva"] = dr("gkva", [128, KVRC], F32, "ExternalInput")
    d["rt"] = dr("rt", [RD, RD], BF16, "ExternalInput")
    d["cos"] = dr("cos", [RD, T], F32, "ExternalInput")
    d["sin"] = dr("sin", [RD, T], F32, "ExternalInput")
    d["ones_row"] = dr("ones_row", [1, 128], F32, "ExternalInput")
    d["ones_col"] = dr("ones_col", [128, 1], BF16, "ExternalInput")
    d["ones_col_f"] = dr("ones_col_f", [128, 1], F32, "ExternalInput")
    d["misc_f"] = dr("misc_f", [1, 2], F32, "ExternalInput")
    d["s1"] = dr("s1", [D, T], F32, "ExternalOutput")
    d["q"] = dr("q", [NH * QK, T], BF16, "Internal")
    d["knc"] = dr("knc", [NH * 128, T], BF16, "Internal")
    d["vc"] = dr("vc", [T, NH * VD], BF16, "Internal")
    d["krc"] = dr("krc", [RD, T], BF16, "Internal")
    d["kng"] = dr("kng", [NCORES * NH * 128, T], BF16, "Internal")
    d["vg"] = dr("vg", [SEQ, NH * VD], BF16, "Internal")
    d["krg"] = dr("krg", [NCORES * RD, T], BF16, "Internal")

    stack = ExitStack()
    with stack:
        k = K(nc, stack)
        build_common(nc, k, stack, c)
        emit_p2(nc, k, stack, c, load_xn1=True, stage2=stage2)
        k.wait("sync", "so", k.cnt["so"])
        k.wait("sync", sOut, k.cnt[sOut])
        k.finish()
    return nc


def p2_inputs(core, ins, xn1=None):
    tr = lambda a: np.ascontiguousarray(np.asarray(a).T)
    t0 = core * T
    inv = (10000.0 ** (-(np.arange(0, RD, 2, dtype=np.float64) / RD)))
    freqs = np.outer(np.arange(t0, t0 + T, dtype=np.float64), inv)
    emb = np.concatenate([freqs, freqs], 1)
    def til2(w, nt, nc_, no):
        w = np.asarray(w)
        return np.ascontiguousarray(w.reshape(nc_, 128, nt, no).transpose(2, 1, 0, 3))
    wkvb = np.asarray(ins["L1_attn_w_kvb"]).reshape(KVR, NH, 2, 128)
    wkn = np.ascontiguousarray(wkvb[:, :, 0, :].reshape(KVR, NH * 128))
    wv = np.ascontiguousarray(wkvb[:, :, 1, :].reshape(KVR, NH * 128))
    wkva = np.asarray(ins["L1_attn_w_kva"])
    m = {
        "wqal": til2(ins["L1_attn_w_qa"], QRC, DC, 128),
        "wqbl": til2(ins["L1_attn_w_qb"], 48, QRC, 128),
        "wkval": til2(wkva[:, 0:KVR], 4, DC, 128),
        "wkvrl": np.ascontiguousarray(wkva[:, KVR:].reshape(DC, 128, RD).transpose(1, 0, 2)),
        "wknl": til2(wkn, NH, KVRC, 128),
        "wvl": til2(wv, 8, KVRC, 512),
        "wol": til2(ins["L1_attn_w_o"], DC, DC, 128),
        "gqa": np.ascontiguousarray(np.asarray(ins["L1_attn_g_qa"]).reshape(QRC, 128).T.astype(np.float32)),
        "gkva": np.ascontiguousarray(np.asarray(ins["L1_attn_g_kva"]).reshape(KVRC, 128).T.astype(np.float32)),
        "rt": _make_rt(),
        "cos": np.ascontiguousarray(np.cos(emb).T.astype(np.float32)),
        "sin": np.ascontiguousarray(np.sin(emb).T.astype(np.float32)),
        "ones_row": np.ones((1, 128), np.float32),
        "ones_col": np.ones((128, 1), ml_dtypes.bfloat16),
        "ones_col_f": np.ones((128, 1), np.float32),
        "misc_f": np.array([[1.0, 1e-5]], np.float32),
    }
    if xn1 is not None:
        m["xn1"] = np.ascontiguousarray(xn1[t0:t0 + T].T.astype(ml_dtypes.bfloat16))
    return m


def _make_rt():
    rt = np.zeros((RD, RD), np.float32)
    half = RD // 2
    for j in range(half):
        rt[j, j + half] = 1.0
    for j in range(half, RD):
        rt[j, j - half] = -1.0
    return rt.astype(ml_dtypes.bfloat16)


def emit_p3_rest(nc, k, stack, c, xn1done, stage3=99):
    d = c.d
    SINK = c.SINK
    xn1t = c.ab(0, 32)
    hp = SINK[:, 896:928]
    hpo = SINK[:, 928:960]
    Hres = SINK[:, 200:328]
    hres_done = c.p3s["hres_done"]
    for g in range(8):
        k.dmat("sync", xn1t[:, g * 4096:(g + 1) * 4096],
               d["xn1"][:, g * 128:(g + 1) * 128], "tm",
               waits=([("x", xn1done), (sPE, getattr(c, "att_done_pe", 0)),
                       (sDVE, getattr(c, "att_done_dve", 0))] if g == 0 else []))
    tmdone = k.cnt["tm"]

    xp = c.AF32[:, 0:1024]
    stg = [c.ab(64, 1), c.ab(65, 1)]
    for g in range(8):
        par = g % 2
        for n in range(4):
            if n == 0:
                k.emit("vector", lambda e, g=g, n=n: e.tensor_scalar(
                    xp, xn1t[:, g * 4096 + n * 1024:g * 4096 + n * 1024 + 1024],
                    hp[:, g * 4 + n:g * 4 + n + 1], None, ALU.mult),
                    waits=[("tm", tmdone), ("xps", k.cnt["xps"])])
            else:
                k.emit("vector", lambda e, g=g, n=n: e.scalar_tensor_tensor(
                    xp, xn1t[:, g * 4096 + n * 1024:g * 4096 + n * 1024 + 1024],
                    hp[:, g * 4 + n:g * 4 + n + 1], xp, ALU.mult, ALU.add))
        dvs = k.emit("vector", lambda e, par=par: e.tensor_copy(stg[par], xp),
                     waits=[(sPE, getattr(c, "wo_done_pe", 0)) if g == 0 else (sPE, 0)],
                     inc=sDVE)
        k.dma("sync", d["xpre"][g * 128:(g + 1) * 128, :], stg[par], "xps",
              waits=[(sDVE, dvs)])

    xpf = c.ab(72, 8)
    for cc2 in range(8):
        k.dmat("sync", xpf[:, cc2 * 1024:(cc2 + 1) * 1024],
               d["xpre"][:, cc2 * 128:(cc2 + 1) * 128], "tm2",
               waits=[("xps", k.cnt["xps"])] if cc2 == 0 else [])
    tm2 = k.cnt["tm2"]
    ystg = [c.ab(64, 1), c.ab(65, 1)]
    pe_y = {}
    st_y = {}
    for ot in range(8):
        par = ot % 2
        k.dma("gpsimd", c.W2SB[:, par * 1024:par * 1024 + 1024],
              d["wbl"][ot].rearrange("d c o -> d (c o)"), f"yw{par}",
              waits=[(sPE, pe_y.get(ot - 2, 0))])
        ywc = k.cnt[f"yw{par}"]
        pe = None
        for h in range(2):
            for ci in range(8):
                pe = k.emit("tensor", lambda e, ci=ci, h=h, par=par: e.matmul(
                    c.ps[0 + h][:, :], c.W2SB[:, par * 1024 + ci * 128:par * 1024 + ci * 128 + 128],
                    xpf[:, ci * 1024 + h * 512:ci * 1024 + h * 512 + 512],
                    start=(ci == 0), stop=(ci == 7)),
                    waits=([(f"yw{par}", ywc), ("tm2", tm2),
                            (sDVE, st_y.get(ot - 1, 0))] if (h == 0 and ci == 0) else []),
                    inc=sPE)
        pe_y[ot] = pe
        dvs = None
        for h in range(2):
            dvs = k.emit("vector", lambda e, h=h, ot=ot: e.tensor_copy(
                ystg[ot % 2][:, h * 512:h * 512 + 512], c.ps[0 + h][:, :]),
                waits=[(sPE, pe), ("ys", k.cnt["ys"])] +
                      ([("xps", k.cnt["xps"])] if ot == 0 else []), inc=sDVE)
        st_y[ot] = dvs
        k.dma("sync", d["y"][ot * 128:(ot + 1) * 128, :], ystg[ot % 2], "ys",
              waits=[(sDVE, dvs)])
    ytm = c.ab(64, 8)
    for g in range(8):
        k.dmat("sync", ytm[:, g * 1024:(g + 1) * 1024],
               d["y"][:, g * 128:(g + 1) * 128], "tm3",
               waits=[("ys", k.cnt["ys"]), (sDVE, k.cnt[sDVE])] if g == 0 else [])
    tm3 = k.cnt["tm3"]

    gin = c.AF32[:, 0:4096]
    g2stg = c.ab(72, 4)
    for g in range(8):
        for i in range(4):
            for j in range(4):
                sc_ap = Hres[:, g * 16 + 4 * i + j:g * 16 + 4 * i + j + 1]
                if j == 0:
                    k.emit("vector", lambda e, g=g, i=i, j=j, sc_ap=sc_ap: e.tensor_scalar(
                        gin[:, i * 1024:(i + 1) * 1024],
                        xn1t[:, g * 4096 + j * 1024:g * 4096 + j * 1024 + 1024],
                        sc_ap, None, ALU.mult),
                        waits=[(sDVE, hres_done), ("tm", tmdone),
                               ("g2s", k.cnt["g2s"])] if i == 0 else [])
                else:
                    k.emit("vector", lambda e, g=g, i=i, j=j, sc_ap=sc_ap: e.scalar_tensor_tensor(
                        gin[:, i * 1024:(i + 1) * 1024],
                        xn1t[:, g * 4096 + j * 1024:g * 4096 + j * 1024 + 1024],
                        sc_ap, gin[:, i * 1024:(i + 1) * 1024], ALU.mult, ALU.add))
        for n in range(4):
            k.emit("vector", lambda e, g=g, n=n: e.scalar_tensor_tensor(
                gin[:, n * 1024:(n + 1) * 1024], ytm[:, g * 1024:(g + 1) * 1024],
                hpo[:, g * 4 + n:g * 4 + n + 1], gin[:, n * 1024:(n + 1) * 1024],
                ALU.mult, ALU.add),
                waits=[("tm3", tm3)] if (g == 0 and n == 0) else [])
        a = k.emit("scalar", lambda e, g=g: e.activation(g2stg[:, :], gin, AF.Silu),
                   waits=[(sDVE, k.cnt[sDVE])] +
                         ([(sPE, pe_y[7])] if g == 0 else []), inc=sACT)
        k.dma("sync", d["g2tm"][g * 128:(g + 1) * 128, :], g2stg[:, :], "g2s",
              waits=[(sACT, a)])

    g2c = [c.ab(76, 1), c.ab(77, 1)]
    for ci in range(DC):
        par = ci % 2
        k.dmat("sync", g2c[par], d["g2tm"][:, ci * 128:(ci + 1) * 128], "tm4",
               waits=([("g2s", k.cnt["g2s"])] if ci == 0 else []) +
                     [(sDVE, k.cnt[sDVE] if ci >= 2 else 0)])
        x1c = c.AF32[:, (par * 2) * 1024:(par * 2) * 1024 + 1024]
        s1c = c.AF32[:, (par * 2 + 1) * 1024:(par * 2 + 1) * 1024 + 1024]
        k.dma("sync", x1c, d["x1"][ci * 128:(ci + 1) * 128, :], "fx",
              waits=[(sDVE, k.cnt[sDVE] if ci >= 2 else 0)])
        k.dma("sync", s1c, d["s1"][ci * 128:(ci + 1) * 128, :], "fx")
        fxc = k.cnt["fx"]
        tmc = k.cnt["tm4"]
        dvs = k.emit("vector", lambda e, par=par, x1c=x1c, s1c=s1c: e.tensor_mul(
            s1c, s1c, g2c[par]), waits=[("fx", fxc), ("tm4", tmc), ("fo", k.cnt["fo"])], inc=sDVE)
        dvs = k.emit("vector", lambda e, x1c=x1c, s1c=s1c: e.tensor_add(
            x1c, x1c, s1c), inc=sDVE)
        k.dma("sync", d["out"][ci * 128:(ci + 1) * 128, :], x1c, "fo",
              waits=[(sDVE, dvs)])
    c.out_done = k.cnt["fo"]




def emit_p3(nc, k, stack, c, load_xn1=True, stage3=99, part="all"):
    d = c.d
    SINK = c.SINK
    LOGF = c.LOGF
    EYE = c.EYE
    TAU_I = 10.0
    xn1 = c.ab(32, 32)
    xn1t = c.ab(0, 32)
    ld3 = [
        k.dma("sync", c.EYE[:, :], d["eyef"], sIn),
        k.dma("sync", c.SM[:, 147:163], d["resb"], sIn),
        k.dma("sync", c.SM[:, 163:167], d["preb"], sIn),
        k.dma("sync", c.SM[:, 167:171], d["postb"], sIn),
        k.dma("sync", c.SM[:, 171:172], d["gatec"], sIn),
        k.dma("sync", c.SM[0:1, 172:173], d["eps2"], sIn),
    ]
    resb = c.SM[:, 147:163]
    preb = c.SM[:, 163:167]
    postb = c.SM[:, 167:171]
    gatec = c.SM[:, 171:172]
    eps2 = c.SM[0:1, 172:173]
    c3 = k.cnt[sIn]
    c.const_done = c3

    if load_xn1:
        for i in range(4):
            k.dma("gpsimd", sview(c.ab(32 + 8 * i, 8), 1024),
                  d["xn1"][i * 1024:(i + 1) * 1024, :].rearrange("(c d) t -> d c t", d=128),
                  "x")
    xn1done = k.cnt["x"]
    if part == "rest":
        emit_p3_rest(nc, k, stack, c, xn1done, stage3)
        return

    sqb = [c.W2SB[:, 0:512], c.W2SB[:, 512:1024]]
    pe_after = {}
    last = None
    for ci in range(DC):
        par = ci % 2
        a = None
        for h in range(2):
            waits = [("x", xn1done)]
            if ci >= 2 and h == 0:
                waits.append((sPE, pe_after[ci - 2]))
            if h == 1:
                waits.append((sPE, last))
            a = k.emit("scalar", lambda e, ci=ci, par=par, h=h: e.activation(
                sqb[par], xn1[:, ci * 1024 + h * 512:ci * 1024 + h * 512 + 512], AF.Square),
                waits=waits, inc=sACT)
            last = k.emit("tensor", lambda e, par=par, h=h, ci=ci: e.matmul(
                c.ps[6 + h][0:1, :], c.ones_col, sqb[par],
                start=(ci == 0), stop=(ci == DC - 1)),
                waits=[(sACT, a), (sIn, c.const_done)], inc=sPE)
        pe_after[ci] = last
    rr = c.ROWS[0:1, 0:1024]
    dvc = None
    for h in range(2):
        dvc = k.emit("vector", lambda e, h=h: e.tensor_copy(
            rr[:, h * 512:h * 512 + 512], c.ps[6 + h][0:1, :]),
            waits=[(sPE, last)], inc=sDVE)
    sc = k.emit("scalar", lambda e: e.activation(rr, rr, AF.Sqrt, scale=1.0 / D, bias=eps2),
                waits=[(sDVE, dvc), (sIn, c3)], inc=sACT)
    dv = k.emit("vector", lambda e: e.reciprocal(rr, rr), waits=[(sACT, sc)], inc=sDVE)
    k.wait("vector", sDVE, dv)
    rstd_dv = dv

    k.dma("gpsimd", sview(c.W2SB[:, 2048:2048 + DC * 24], 24), wview(d["wall"], 0, DC, 0, 24),
          "lw", waits=[(sPE, last)])
    lwc = k.cnt["lw"]
    pe = None
    for h in range(2):
        for ci in range(DC):
            pe = k.emit("tensor", lambda e, ci=ci, h=h: e.matmul(
                c.ps[0 + h][0:24, :], c.W2SB[:, 2048 + ci * 24:2048 + ci * 24 + 24],
                xn1[:, ci * 1024 + h * 512:ci * 1024 + h * 512 + 512],
                start=(ci == 0), stop=(ci == DC - 1)),
                waits=([("lw", lwc), ("x", xn1done)] if (h == 0 and ci == 0) else []), inc=sPE)
    dv = None
    for h in range(2):
        dv = k.emit("vector", lambda e, h=h: e.tensor_copy(
            LOGF[:, h * 512:h * 512 + 512], c.ps[0 + h][0:24, :]),
            waits=[(sPE, pe)], inc=sDVE)
    pe_tr = None
    for g in range(8):
        pe_tr = k.emit("tensor", lambda e, g=g: e.transpose(
            c.ps[2][:, g * 24:g * 24 + 24], LOGF[:, g * 128:(g + 1) * 128], EYE[0:24, 0:24]),
            waits=[(sDVE, dv)] if g == 0 else [], inc=sPE)
    ztmp = SINK[:, 512:704]
    dv_z = k.emit("vector", lambda e: e.tensor_copy(ztmp, c.ps[2][:, 0:192]),
                  waits=[(sPE, pe_tr)], inc=sDVE)
    pe_tr = None
    for g in range(8):
        pe_tr = k.emit("tensor", lambda e, g=g: e.transpose(
            c.ps[4][:, g:g + 1], rr[:, g * 128:(g + 1) * 128], EYE[0:1, 0:1]),
            waits=[(sDVE, rstd_dv), (sIn, c3)] if g == 0 else [], inc=sPE)
    rrms = SINK[:, 192:200]
    dv_rrms = k.emit("vector", lambda e: e.tensor_copy(rrms, c.ps[4][:, 0:8]),
                     waits=[(sPE, pe_tr)], inc=sDVE)

    Z = SINK[:, 0:128]
    E = SINK[:, 704:832]
    L1a = SINK[:, 832:864]
    L2a = SINK[:, 864:896]
    hp = SINK[:, 896:928]
    hpo = SINK[:, 928:960]
    hscr = SINK[:, 960:976]
    b10 = SINK[:, 976:992]
    Hres = SINK[:, 200:328]
    k.emit("vector", lambda e: e.tensor_scalar(b10, resb, TAU_I, None, ALU.mult),
           waits=[(sIn, c3)])
    for g in range(8):
        k.emit("vector", lambda e, g=g: e.tensor_scalar(
            Z[:, g * 16:(g + 1) * 16], ztmp[:, g * 24:g * 24 + 16],
            rrms[:, g:g + 1], TAU_I, ALU.mult, ALU.mult),
            waits=[(sDVE, dv_z), (sDVE, dv_rrms)] if g == 0 else [])
        k.emit("vector", lambda e, g=g: e.tensor_add(
            Z[:, g * 16:(g + 1) * 16], Z[:, g * 16:(g + 1) * 16], b10))
        k.emit("vector", lambda e, g=g: e.tensor_scalar(
            hp[:, g * 4:(g + 1) * 4], ztmp[:, g * 24 + 16:g * 24 + 20],
            rrms[:, g:g + 1], None, ALU.mult))
        k.emit("vector", lambda e, g=g: e.tensor_add(
            hp[:, g * 4:(g + 1) * 4], hp[:, g * 4:(g + 1) * 4], preb))
        k.emit("vector", lambda e, g=g: e.tensor_scalar(
            hpo[:, g * 4:(g + 1) * 4], ztmp[:, g * 24 + 20:g * 24 + 24],
            rrms[:, g:g + 1], None, ALU.mult))
        dvh = k.emit("vector", lambda e, g=g: e.tensor_add(
            hpo[:, g * 4:(g + 1) * 4], hpo[:, g * 4:(g + 1) * 4], postb))

    for name, hl in (("hp", hp), ("hpo", hpo)):
        a = k.emit("scalar", lambda e, hl=hl: e.activation(hl, hl, AF.Exp),
                   waits=[(sDVE, k.cnt[sDVE])], inc=sACT)
        dvs = k.emit("vector", lambda e, hl=hl: e.reduce_sum(
            hscr[:, 0:8], hl.rearrange("p (g n) -> p g n", n=4), axis=AX.X),
            waits=[(sACT, a)], inc=sDVE)
        dvs = k.emit("vector", lambda e: e.reciprocal(hscr[:, 0:8], hscr[:, 0:8]), inc=sDVE)
        k.wait("vector", sDVE, dvs)
        for g in range(8):
            k.emit("vector", lambda e, g=g, hl=hl: e.tensor_scalar(
                hl[:, g * 4:(g + 1) * 4], hl[:, g * 4:(g + 1) * 4],
                hscr[:, g:g + 1], None, ALU.mult))

    def sink_views(buf, fix_i=None, fix_j=None):
        v3 = buf.rearrange("p (g i j) -> p g i j", i=4, j=4)
        if fix_i is not None:
            return v3[:, :, fix_i, :]
        return v3[:, :, :, fix_j]

    def l3(buf, n=4):
        return buf.rearrange("p (g n) -> p g n", n=n)

    k.emit("vector", lambda e: e.memset(L2a, 0.0))
    for it in range(50):
        for i in range(4):
            k.emit("vector", lambda e, i=i: e.tensor_sub(
                sink_views(E, fix_i=i), sink_views(Z, fix_i=i), l3(L2a)))
        a = k.emit("scalar", lambda e: e.activation(E, E, AF.Exp),
                   waits=[(sDVE, k.cnt[sDVE])], inc=sACT)
        dvs = k.emit("vector", lambda e: e.reduce_sum(
            L1a, E.rearrange("p (gi j) -> p gi j", j=4), axis=AX.X),
            waits=[(sACT, a)], inc=sDVE)
        a = k.emit("scalar", lambda e: e.activation(L1a, L1a, AF.Ln),
                   waits=[(sDVE, dvs)], inc=sACT)
        k.wait("vector", sACT, a)
        for j in range(4):
            k.emit("vector", lambda e, j=j: e.tensor_sub(
                sink_views(E, fix_j=j), sink_views(Z, fix_j=j), l3(L1a)))
        a = k.emit("scalar", lambda e: e.activation(E, E, AF.Exp),
                   waits=[(sDVE, k.cnt[sDVE])], inc=sACT)
        dvs = k.emit("vector", lambda e: e.reduce_sum(
            l3(L2a), E.rearrange("p (g i j) -> p g j i", i=4, j=4), axis=AX.X),
            waits=[(sACT, a)], inc=sDVE)
        a = k.emit("scalar", lambda e: e.activation(L2a, L2a, AF.Ln),
                   waits=[(sDVE, dvs)], inc=sACT)
        k.wait("vector", sACT, a)
    for i in range(4):
        k.emit("vector", lambda e, i=i: e.tensor_sub(
            sink_views(Hres, fix_i=i), sink_views(Z, fix_i=i), l3(L2a)))
    for j in range(4):
        k.emit("vector", lambda e, j=j: e.tensor_sub(
            sink_views(Hres, fix_j=j), sink_views(Hres, fix_j=j), l3(L1a)))
    a = k.emit("scalar", lambda e: e.activation(Hres, Hres, AF.Exp),
               waits=[(sDVE, k.cnt[sDVE])], inc=sACT)
    gg = SINK[:, 992:993]
    om = SINK[:, 993:994]
    a2 = k.emit("scalar", lambda e: e.activation(gg, gatec, AF.Sigmoid),
                waits=[(sIn, c3)], inc=sACT)
    dvs = k.emit("vector", lambda e: e.tensor_scalar(om, gg, -1.0, 1.0, ALU.mult, ALU.add),
                 waits=[(sACT, a2)], inc=sDVE)
    dvs = k.emit("vector", lambda e: e.tensor_scalar(Hres, Hres, gg, None, ALU.mult),
                 waits=[(sACT, a)], inc=sDVE)
    for g in range(8):
        dvs = k.emit("vector", lambda e, g=g: e.tensor_scalar(
            Hres.rearrange("p (g i j) -> p g i j", i=4, j=4)[:, g, :, :]
            .rearrange("p i j -> p (i j)")[:, 0:16:5],
            Hres.rearrange("p (g i j) -> p g i j", i=4, j=4)[:, g, :, :]
            .rearrange("p i j -> p (i j)")[:, 0:16:5],
            om, None, ALU.add))
    hres_done = dvs
    c.p3s = dict(hres_done=hres_done)

    if stage3 <= 0:
        dvx = k.emit("vector", lambda e: e.tensor_copy(c.AF32[:, 0:1024], SINK[:, 0:1024]), inc=sDVE)
        k.dma("sync", d["out"][0:128, :], c.AF32[:, 0:1024], sOut, waits=[(sDVE, dvx)])
        return

    if part == "all":
        emit_p3_rest(nc, k, stack, c, xn1done, stage3)


def build_p3(stage3=99):
    nc = bass.Bass("TRN2", target_bir_lowering=False)
    c = Ctx()
    c.d = {}
    d = c.d
    dr = lambda name, shape, dt, kind: nc.dram_tensor(name, shape, dt, kind=kind).ap()
    d["xn1"] = dr("xn1", [D, T], BF16, "ExternalInput")
    d["x1"] = dr("x1", [D, T], F32, "ExternalInput")
    d["s1"] = dr("s1", [D, T], F32, "ExternalInput")
    d["wall"] = dr("wall", [D, 24], F32, "ExternalInput")
    d["wbl"] = dr("wbl", [8, 128, 8, 128], F32, "ExternalInput")
    d["resb"] = dr("resb", [128, 16], F32, "ExternalInput")
    d["preb"] = dr("preb", [128, 4], F32, "ExternalInput")
    d["postb"] = dr("postb", [128, 4], F32, "ExternalInput")
    d["gatec"] = dr("gatec", [128, 1], F32, "ExternalInput")
    d["eps2"] = dr("eps2", [1, 1], F32, "ExternalInput")
    d["eyef"] = dr("eyef", [24, 24], F32, "ExternalInput")
    d["ones_row"] = dr("ones_row", [1, 128], F32, "ExternalInput")
    d["ones_col"] = dr("ones_col", [128, 1], BF16, "ExternalInput")
    d["ones_col_f"] = dr("ones_col_f", [128, 1], F32, "ExternalInput")
    d["misc_f"] = dr("misc_f", [1, 2], F32, "ExternalInput")
    d["out"] = dr("out", [D, T], F32, "ExternalOutput")
    d["xpre"] = dr("xpre", [T, 1024], BF16, "Internal")
    d["y"] = dr("y", [1024, T], BF16, "Internal")
    d["g2tm"] = dr("g2tm", [T, D], BF16, "Internal")

    stack = ExitStack()
    with stack:
        k = K(nc, stack)
        build_common(nc, k, stack, c)
        emit_p3(nc, k, stack, c, load_xn1=True, stage3=stage3)
        k.wait("sync", "fo", k.cnt["fo"])
        k.wait("sync", sOut, k.cnt[sOut])
        k.finish()
    return nc


def p3_inputs(core, ins, xn1=None, x1=None, s1=None):
    t0 = core * T
    wall = np.concatenate([np.asarray(ins["L1_ffn_W_res"]).T,
                           np.asarray(ins["L1_ffn_W_pre"]).T,
                           np.asarray(ins["L1_ffn_W_post"]).T], axis=1)
    m = {
        "wall": np.ascontiguousarray(wall.astype(np.float32)),
        "wbl": np.ascontiguousarray(np.asarray(ins["L1_ffn_W_branch"])
                                    .reshape(8, 128, 8, 128).transpose(2, 1, 0, 3)),
        "resb": np.ascontiguousarray(np.broadcast_to(
            np.asarray(ins["L1_ffn_H_res_base"]).reshape(1, 16), (128, 16)).astype(np.float32)),
        "preb": np.ascontiguousarray(np.broadcast_to(
            np.asarray(ins["L1_ffn_H_pre_base"]).reshape(1, 4), (128, 4)).astype(np.float32)),
        "postb": np.ascontiguousarray(np.broadcast_to(
            np.asarray(ins["L1_ffn_H_post_base"]).reshape(1, 4), (128, 4)).astype(np.float32)),
        "gatec": np.ascontiguousarray(np.broadcast_to(
            np.asarray(ins["L1_ffn_gate"]).reshape(1, 1), (128, 1)).astype(np.float32)),
        "eps2": np.array([[1.1920929e-07]], np.float32),
        "eyef": np.eye(24, dtype=np.float32),
        "ones_row": np.ones((1, 128), np.float32),
        "ones_col": np.ones((128, 1), ml_dtypes.bfloat16),
        "ones_col_f": np.ones((128, 1), np.float32),
        "misc_f": np.array([[1.0, 1e-5]], np.float32),
    }
    if xn1 is not None:
        m["xn1"] = np.ascontiguousarray(xn1[t0:t0 + T].T.astype(ml_dtypes.bfloat16))
        m["x1"] = np.ascontiguousarray(x1[t0:t0 + T].T.astype(np.float32))
        m["s1"] = np.ascontiguousarray(s1[t0:t0 + T].T.astype(np.float32))
    return m


def build_p1(stage=99):
    nc = bass.Bass("TRN2", target_bir_lowering=False)
    c = Ctx()
    c.d = {}
    d = c.d
    dr = lambda name, shape, dt, kind: nc.dram_tensor(name, shape, dt, kind=kind).ap()
    d["x"] = dr("x", [D, T], F32, "ExternalInput")
    for nm, sh in (("w1l", [HCN, 128, DC, 128]), ("w3l", [HCN, 128, DC, 128]),
                   ("w2l", [DC, 128, HCN, 128]),
                   ("a1t", [D, LR]), ("b1t", [LR, HID]), ("a3t", [D, LR]),
                   ("b3t", [LR, HID]), ("a2t", [HID, LR]), ("b2t", [LR, D])):
        d[nm] = dr(nm, sh, F32, "ExternalInput")
    d["ones_row"] = dr("ones_row", [1, 128], F32, "ExternalInput")
    d["ones_col"] = dr("ones_col", [128, 1], BF16, "ExternalInput")
    d["ones_col_f"] = dr("ones_col_f", [128, 1], F32, "ExternalInput")
    d["misc_f"] = dr("misc_f", [1, 2], F32, "ExternalInput")
    d["x1"] = dr("x1", [D, T], F32, "ExternalOutput")
    d["xn1"] = dr("xn1", [D, T], BF16, "ExternalOutput")
    d["part"] = dr("part", [D, T], F32, "Internal")

    stack = ExitStack()
    with stack:
        k = K(nc, stack)
        build_common(nc, k, stack, c)
        emit_p1(nc, k, stack, c, stage)
        k.wait("sync", sOut, k.cnt[sOut])
        k.finish()
    return nc


def p1_inputs(core, ins):
    x = np.asarray(ins["L0_attn_x"])[0]
    sl = slice(core * T, (core + 1) * T)
    tr = lambda a: np.ascontiguousarray(np.asarray(a).T)
    def til(w, nt, nc_, no):
        w = np.asarray(w)
        return np.ascontiguousarray(w.reshape(nt, no, nc_, 128).transpose(0, 3, 2, 1))
    return {
        "x": np.ascontiguousarray(x[sl].T),
        "w1l": til(ins["L0_ffn_w1"], HCN, DC, 128),
        "w3l": til(ins["L0_ffn_w3"], HCN, DC, 128),
        "w2l": til(ins["L0_ffn_w2"], DC, HCN, 128),
        "a1t": tr(ins["L0_ffn_lora_A1"]), "b1t": tr(ins["L0_ffn_lora_B1"]),
        "a3t": tr(ins["L0_ffn_lora_A3"]), "b3t": tr(ins["L0_ffn_lora_B3"]),
        "a2t": tr(ins["L0_ffn_lora_A2"]), "b2t": tr(ins["L0_ffn_lora_B2"]),
        "ones_row": np.ones((1, 128), np.float32),
        "ones_col": np.ones((128, 1), ml_dtypes.bfloat16),
        "ones_col_f": np.ones((128, 1), np.float32),
        "misc_f": np.array([[1.0, 1e-5]], np.float32),
    }


def build_fused():
    nc = bass.Bass("TRN2", target_bir_lowering=False)
    c = Ctx()
    c.d = {}
    d = c.d
    dr = lambda name, shape, dt, kind: nc.dram_tensor(name, shape, dt, kind=kind).ap()
    d["x"] = dr("x", [D, T], F32, "ExternalInput")
    for nm, sh in (("w1l", [HCN, 128, DC, 128]), ("w3l", [HCN, 128, DC, 128]),
                   ("w2l", [DC, 128, HCN, 128]),
                   ("a1t", [D, LR]), ("b1t", [LR, HID]), ("a3t", [D, LR]),
                   ("b3t", [LR, HID]), ("a2t", [HID, LR]), ("b2t", [LR, D])):
        d[nm] = dr(nm, sh, F32, "ExternalInput")
    for nm, sh in (("wqal", [QRC, 128, DC, 128]), ("wqbl", [48, 128, QRC, 128]),
                   ("wkval", [4, 128, DC, 128]), ("wkvrl", [128, DC, 64]),
                   ("wknl", [NH, 128, KVRC, 128]), ("wvl", [8, 128, KVRC, 512]),
                   ("wol", [DC, 128, DC, 128])):
        d[nm] = dr(nm, sh, F32, "ExternalInput")
    d["gqa"] = dr("gqa", [128, QRC], F32, "ExternalInput")
    d["gkva"] = dr("gkva", [128, KVRC], F32, "ExternalInput")
    d["rt"] = dr("rt", [RD, RD], BF16, "ExternalInput")
    d["cos"] = dr("cos", [RD, T], F32, "ExternalInput")
    d["sin"] = dr("sin", [RD, T], F32, "ExternalInput")
    d["wall"] = dr("wall", [D, 24], F32, "ExternalInput")
    d["wbl"] = dr("wbl", [8, 128, 8, 128], F32, "ExternalInput")
    d["resb"] = dr("resb", [128, 16], F32, "ExternalInput")
    d["preb"] = dr("preb", [128, 4], F32, "ExternalInput")
    d["postb"] = dr("postb", [128, 4], F32, "ExternalInput")
    d["gatec"] = dr("gatec", [128, 1], F32, "ExternalInput")
    d["eps2"] = dr("eps2", [1, 1], F32, "ExternalInput")
    d["eyef"] = dr("eyef", [24, 24], F32, "ExternalInput")
    d["ones_row"] = dr("ones_row", [1, 128], F32, "ExternalInput")
    d["ones_col"] = dr("ones_col", [128, 1], BF16, "ExternalInput")
    d["ones_col_f"] = dr("ones_col_f", [128, 1], F32, "ExternalInput")
    d["misc_f"] = dr("misc_f", [1, 2], F32, "ExternalInput")
    d["out"] = dr("out", [D, T], F32, "ExternalOutput")
    for nm, sh, dt in (("x1", [D, T], F32), ("xn1", [D, T], BF16), ("part", [D, T], F32),
                       ("s1", [D, T], F32), ("q", [NH * QK, T], BF16),
                       ("knc", [NH * 128, T], BF16), ("vc", [T, NH * VD], BF16),
                       ("krc", [RD, T], BF16),
                       ("kng", [NCORES * NH * 128, T], BF16), ("vg", [SEQ, NH * VD], BF16),
                       ("krg", [NCORES * RD, T], BF16),
                       ("xpre", [T, 1024], BF16), ("y", [1024, T], BF16),
                       ("g2tm", [T, D], BF16)):
        d[nm] = dr(nm, sh, dt, "Internal")

    stack = ExitStack()
    with stack:
        k = K(nc, stack)
        build_common(nc, k, stack, c)
        emit_p1(nc, k, stack, c, 99)
        for qn_ in ("gpsimd", "sync"):
            k.wait(qn_, sPE, k.cnt[sPE])
            k.wait(qn_, sDVE, k.cnt[sDVE])
            k.wait(qn_, sACT, k.cnt[sACT])
            k.wait(qn_, sOut, k.cnt[sOut])
        c.overlap_hook = lambda: emit_p3(nc, k, stack, c, load_xn1=False,
                                         stage3=99, part="hmat")
        emit_p2(nc, k, stack, c, load_xn1=False, stage2=99)
        emit_p3(nc, k, stack, c, load_xn1=False, stage3=99, part="rest")
        k.wait("sync", "fo", k.cnt["fo"])
        k.wait("sync", sOut, k.cnt[sOut])
        k.finish()
    return nc


def fused_inputs(core, ins):
    m = p1_inputs(core, ins)
    m.update(p2_inputs(core, ins))
    m.update(p3_inputs(core, ins))
    return m


def kernel(**inputs):
    ins = inputs
    cores = list(range(NCORES))
    nc = build_fused()
    res = run_bass_kernel_spmd(nc, [fused_inputs(cc, ins) for cc in cores], core_ids=cores)
    out = np.concatenate([res.results[cc]["out"].T for cc in cores], axis=0)
    return out.reshape(1, SEQ, D).astype(np.float32)




# version 26
# speedup vs baseline: 1.0281x; 1.0196x over previous
import numpy as np
import ml_dtypes
from contextlib import ExitStack

import concourse.bass as bass
import concourse.mybir as mybir
from concourse.bass_utils import run_bass_kernel_spmd

F32 = mybir.dt.float32
BF16 = mybir.dt.bfloat16
AF = mybir.ActivationFunctionType
ALU = mybir.AluOpType
AX = mybir.AxisListType

NCORES = 8
T = 1024
D = 4096
DC = D // 128
HID = 11008
HCN = HID // 128
SEC_SIZES = [22, 22, 21, 21]
LR = 16
LSC = 2.0
NH = 32
QR = 1536
QRC = QR // 128
KVR = 512
KVRC = KVR // 128
RD = 64
ND = 128
VD = 128
QK = RD + ND
SEQ = NCORES * T
KTN = SEQ // 128

ENGS = ("sync", "tensor", "vector", "scalar", "gpsimd")
sIn, sW, sPE, sACT, sDVE, sPOOL, sOut, sCC = "in", "w", "pe", "act", "dve", "pool", "out", "cc"


class K:
    def __init__(self, nc, stack):
        self.nc = nc
        self.stack = stack
        self.q = {e: [] for e in ENGS}
        self.sems = {}
        import collections
        self.cnt = collections.defaultdict(int)

    def sem(self, name):
        if name not in self.sems:
            self.sems[name] = self.stack.enter_context(self.nc.semaphore(name))
            self.cnt[name] = 0
        return self.sems[name]

    ENGSEM = {"vector": sDVE, "scalar": sACT, "gpsimd": sPOOL}

    def emit(self, eng, fn, waits=(), inc=None, by=1):
        es = self.ENGSEM.get(eng)
        if es is not None:
            waits = list(waits) + [(es, self.cnt[es])]
            if inc is None:
                inc = es
        waits = tuple((n, v) for (n, v) in waits if v > 0)
        for n, _ in waits:
            self.sem(n)
        if inc is not None:
            self.sem(inc)

        def go(e, fn=fn, waits=waits, inc=inc, by=by):
            for (n, v) in waits:
                e.wait_ge(self.sems[n], v)
            ins = fn(e)
            if inc is not None:
                ins.then_inc(self.sems[inc], by)

        self.q[eng].append(go)
        if inc is not None:
            self.cnt[inc] += by
            return self.cnt[inc]
        return None

    def wait(self, eng, name, v):
        if v > 0:
            self.sem(name)
            self.q[eng].append(lambda e, n=name, v=v: e.wait_ge(self.sems[n], v))

    def dma(self, eng, out, in_, inc, waits=()):
        return self.emit(eng, lambda e: e.dma_start(out=out, in_=in_),
                         waits=waits, inc=inc, by=16)

    def dmat(self, eng, out, in_, inc, waits=()):
        return self.emit(eng, lambda e: e.dma_start_transpose(out=out, in_=in_),
                         waits=waits, inc=inc, by=16)

    def finish(self):
        block = self.stack.enter_context(self.nc.Block())
        for e in ENGS:
            def run(eng, e=e):
                for f in self.q[e]:
                    f(eng)
            getattr(block, e)(run)


def tload(w4, tile, c0=0, nch=None, flat_cols=None):
    if nch is None:
        return w4[tile]
    return w4[tile][:, c0:c0 + nch, :]


def wview(w, c0, nch, o0, no):
    return w[c0 * 128:(c0 + nch) * 128, o0:o0 + no].rearrange("(c d) o -> d c o", d=128)


def sview(sb, no):
    return sb.rearrange("d (c o) -> d c o", o=no)


class Ctx:
    overlap_hook = None


def build_common(nc, k, stack, c):
    ec = stack.enter_context
    c.AB = ec(nc.sbuf_tensor("AB", [128, 80 * 1024], BF16))
    c.AF32 = ec(nc.sbuf_tensor("AF32", [128, 4 * 1024], F32))
    c.ROWS = ec(nc.sbuf_tensor("ROWS", [1, 2 * 1024], F32))
    c.SM = ec(nc.sbuf_tensor("SM", [128, 176], F32))
    c.SMB = ec(nc.sbuf_tensor("SMB", [128, 160], BF16))
    c.L16 = ec(nc.sbuf_tensor("L16", [16, 3840], BF16))
    c.W2SB = ec(nc.sbuf_tensor("W2SB", [128, 2 * 2816], BF16))
    c.SINK = ec(nc.sbuf_tensor("SINK", [128, 1024], F32))
    c.EYE = ec(nc.sbuf_tensor("EYE", [24, 24], F32))
    c.LOGF = c.SINK[0:24, :]
    c.ps = [ec(nc.psum_tensor(f"ps{i}", [128, 512], F32)) for i in range(8)]

    c.ab = lambda c0, n: c.AB[:, c0 * 1024:(c0 + n) * 1024]
    c.af = lambda c0, n: c.AF32[:, c0 * 1024:(c0 + n) * 1024]

    c.ones_row = c.SM[0:1, 0:128]
    c.ones_col = c.SMB[:, 0:1]
    c.ones_col_f = c.SM[:, 130:131]
    c.eps_ln = c.SM[0:1, 129:130]

    c.ld_const = [
        k.dma("sync", c.SM[0:1, 0:128], c.d["ones_row"], sIn),
        k.dma("sync", c.SMB[:, 0:1], c.d["ones_col"], sIn),
        k.dma("sync", c.SM[0:1, 128:130], c.d["misc_f"], sIn),
        k.dma("sync", c.SM[:, 130:131], c.d["ones_col_f"], sIn),
    ]
    c.const_done = c.ld_const[-1]


def stat_rows(k, c, src_ab, nchunks, wait_fn):
    pe_after = {}
    last = None
    for ci in range(nchunks):
        par = ci % 2
        waits = list(wait_fn(ci))
        if ci >= 2:
            waits.append((sPE, pe_after[ci - 2]))
        a = k.emit("scalar", lambda e, ci=ci, par=par: e.activation(
            c.ab(64 + par, 1), src_ab[:, ci * 1024:(ci + 1) * 1024], AF.Square),
            waits=waits, inc=sACT)
        for h in range(2):
            k.emit("tensor", lambda e, ci=ci, h=h: e.matmul(
                c.ps[2 + h][0:1, :], c.ones_col,
                src_ab[:, ci * 1024 + h * 512: ci * 1024 + h * 512 + 512],
                start=(ci == 0), stop=(ci == nchunks - 1)),
                waits=list(wait_fn(ci)) + [(sIn, c.const_done)], inc=sPE)
        for h in range(2):
            last = k.emit("tensor", lambda e, par=par, h=h, ci=ci: e.matmul(
                c.ps[4 + h][0:1, :], c.ones_col,
                c.ab(64 + par, 1)[:, h * 512:h * 512 + 512],
                start=(ci == 0), stop=(ci == nchunks - 1)),
                waits=[(sACT, a)], inc=sPE)
            pe_after[ci] = last
    return last


def ln_apply(k, c, src_ab, dst_ab, nchunks, pe_stats, r0):
    rm = c.ROWS[0:1, 0:1024]
    ra = c.ROWS[0:1, 1024:2048]
    rt = c.SINK[0:1, 0:1024]
    for h in range(2):
        k.emit("vector", lambda e, h=h: e.tensor_copy(rm[:, h * 512:h * 512 + 512], c.ps[2 + h][0:1, :]),
               waits=[(sPE, pe_stats)], inc=sDVE)
    for h in range(2):
        k.emit("vector", lambda e, h=h: e.tensor_copy(rt[:, h * 512:h * 512 + 512], c.ps[4 + h][0:1, :]),
               inc=sDVE)
    k.emit("vector", lambda e: e.tensor_scalar_mul(rm, rm, 1.0 / D), inc=sDVE)
    k.emit("vector", lambda e: e.tensor_scalar_mul(rt, rt, 1.0 / D), inc=sDVE)
    k.emit("vector", lambda e: e.tensor_mul(ra, rm, rm), inc=sDVE)
    dv = k.emit("vector", lambda e: e.tensor_sub(rt, rt, ra), inc=sDVE)
    a1 = k.emit("scalar", lambda e: e.activation(ra, rt, AF.Sqrt, bias=c.eps_ln),
                waits=[(sDVE, dv), (sIn, c.const_done)], inc=sACT)
    dv = k.emit("vector", lambda e: e.reciprocal(ra, ra), waits=[(sACT, a1)], inc=sDVE)
    k.wait("vector", sDVE, dv)
    dv = k.emit("vector", lambda e: e.scalar_tensor_tensor(rm, rm, -1.0, ra, ALU.mult, ALU.mult),
                inc=sDVE)
    pe_rep = None
    for h in range(2):
        k.emit("tensor", lambda e, h=h: e.matmul(
            c.ps[0 + h][:, :], c.ones_row, ra[:, h * 512:h * 512 + 512], start=True, stop=True),
            waits=[(sDVE, dv)], inc=sPE)
    for h in range(2):
        pe_rep = k.emit("tensor", lambda e, h=h: e.matmul(
            c.ps[2 + h][:, :], c.ones_row, rm[:, h * 512:h * 512 + 512], start=True, stop=True),
            inc=sPE)
    scr = c.af(0, 1)
    last = None
    for ci in range(nchunks):
        for h in range(2):
            sl = slice(ci * 1024 + h * 512, ci * 1024 + h * 512 + 512)
            k.emit("vector", lambda e, sl=sl, h=h: e.tensor_mul(
                scr[:, 0:512], src_ab[:, sl], c.ps[0 + h][:, :]),
                waits=[(sPE, pe_rep)], inc=sDVE)
            last = k.emit("vector", lambda e, sl=sl, h=h: e.tensor_add(
                dst_ab[:, sl], scr[:, 0:512], c.ps[2 + h][:, :]),
                inc=sDVE)
    return last


def emit_p1(nc, k, stack, c, stage=99):
    d = c.d
    xbf = c.ab(0, 32)
    xn = c.ab(32, 32)
    gsec = c.ab(0, 22)
    w1b = [c.ab(64, 4), c.ab(68, 4)]
    w3b = [c.ab(72, 4), c.ab(76, 4)]
    w2b = [c.W2SB[:, 0:2816], c.W2SB[:, 2816:5632]]

    for i in range(4):
        k.dma("gpsimd", sview(c.ab(8 * i, 8), 1024),
              d["x"][i * 1024:(i + 1) * 1024, :].rearrange("(c d) t -> d c t", d=128), "x")
    xdone = k.cnt["x"]

    pe_stats = stat_rows(k, c, xbf, DC, lambda ci: [("x", xdone)])
    ln_done = ln_apply(k, c, xbf, xn, DC, pe_stats, 0)
    if stage <= 0:
        for i in range(4):
            k.dma("sync", d["xn1"][i * 1024:(i + 1) * 1024, :].rearrange("(c d) t -> d c t", d=128),
                  sview(c.ab(32 + 8 * i, 8), 1024), sOut, waits=[(sDVE, ln_done)])
        return

    xnA = {"a1t": c.L16[:, 0:1024], "a3t": c.L16[:, 1024:2048]}
    a_evac = {}
    for ni, nm in enumerate(("a1t", "a3t")):
        k.dma("gpsimd", sview(c.W2SB[:, ni * 512:(ni + 1) * 512], 16),
              wview(d[nm], 0, DC, 0, LR), "lw", waits=[(sPE, pe_stats)] if ni == 0 else [])
    lwc = k.cnt["lw"]
    prev_evac = 0
    for ni, nm in enumerate(("a1t", "a3t")):
        pe = None
        for ci in range(DC):
            for h in range(2):
                pe = k.emit("tensor", lambda e, ci=ci, h=h, ni=ni: e.matmul(
                    c.ps[6 + h][0:16, :], c.W2SB[:, ni * 512 + ci * 16:ni * 512 + ci * 16 + 16],
                    xn[:, ci * 1024 + h * 512:ci * 1024 + h * 512 + 512],
                    start=(ci == 0), stop=(ci == DC - 1)),
                    waits=[("lw", lwc), (sDVE, ln_done), (sACT, prev_evac)], inc=sPE)
        ae = None
        for h in range(2):
            ae = k.emit("scalar", lambda e, h=h, nm=nm: e.mul(
                xnA[nm][0:16, h * 512:h * 512 + 512], c.ps[6 + h][0:16, :], LSC),
                waits=[(sPE, pe)], inc=sACT)
        a_evac[nm] = ae
        prev_evac = ae

    hc_base = 0
    pe_lora2 = {}
    cons_hc = {}
    w2_cons = {}
    lora2_evac = None
    w2_rmw = {}
    part_f32 = c.af(1, 1)
    silu_scr = c.af(2, 1)
    x_f32 = c.af(3, 1)
    prev_part_user = {}
    pe_stat_mm = {}
    x1_store_last = None
    last_hc_pe = None

    for s, ssz in enumerate(SEC_SIZES):
        for j in range(ssz):
            hc = hc_base + j
            slot = hc % 2
            wsem = f"wh{slot}"
            gate = [(sPE, pe_lora2.get(hc - 2, 0))]
            if hc < 2:
                gate.append((sPE, pe_stats))
                gate.append((sACT, a_evac["a3t"]))
            k.dma("gpsimd", w1b[slot], d["w1l"][hc].rearrange("d c o -> d (c o)"),
                  wsem, waits=gate)
            k.dma("gpsimd", w3b[slot], d["w3l"][hc].rearrange("d c o -> d (c o)"), wsem)
            k.dma("gpsimd", c.L16[:, 3328 + slot * 256:3328 + slot * 256 + 128],
                  d["b1t"][:, hc * 128:hc * 128 + 128], wsem)
            k.dma("gpsimd", c.L16[:, 3328 + slot * 256 + 128:3328 + slot * 256 + 256],
                  d["b3t"][:, hc * 128:hc * 128 + 128], wsem)
            k.dma("gpsimd", c.SMB[:, 128 + slot * 16:128 + slot * 16 + 16],
                  d["a2t"][hc * 128:hc * 128 + 128, :], wsem)
            wdone = k.cnt[wsem]
            pe = None
            ag = asl = None
            cons = cons_hc.get(hc - 1, ((0, 0), (0, 0)))
            for h in range(2):
                for ci in range(DC):
                    k.emit("tensor", lambda e, ci=ci, h=h, slot=slot: e.matmul(
                        c.ps[0 + h][:, :], w1b[slot][:, ci * 128:ci * 128 + 128],
                        xn[:, ci * 1024 + h * 512:ci * 1024 + h * 512 + 512],
                        start=(ci == 0), stop=False),
                        waits=([(wsem, wdone), (sDVE, ln_done),
                                (sDVE, cons[h][0]), (sACT, cons[h][1])]
                               if ci == 0 else []), inc=sPE)
                k.emit("tensor", lambda e, h=h, slot=slot: e.matmul(
                    c.ps[0 + h][:, :], c.L16[0:16, 3328 + slot * 256:3328 + slot * 256 + 128],
                    c.L16[0:16, 0 + h * 512:0 + h * 512 + 512], start=False, stop=True),
                    waits=[(sACT, a_evac["a1t"])], inc=sPE)
                for ci in range(DC):
                    k.emit("tensor", lambda e, ci=ci, h=h, slot=slot: e.matmul(
                        c.ps[2 + h][:, :], w3b[slot][:, ci * 128:ci * 128 + 128],
                        xn[:, ci * 1024 + h * 512:ci * 1024 + h * 512 + 512],
                        start=(ci == 0), stop=False), inc=sPE)
                pe = k.emit("tensor", lambda e, h=h, slot=slot: e.matmul(
                    c.ps[2 + h][:, :], c.L16[0:16, 3328 + slot * 256 + 128:3328 + slot * 256 + 256],
                    c.L16[0:16, 1024 + h * 512:1024 + h * 512 + 512], start=False, stop=True),
                    waits=[(sACT, a_evac["a3t"])], inc=sPE)
                asl = k.emit("scalar", lambda e, h=h: e.activation(
                    silu_scr[:, h * 512:h * 512 + 512], c.ps[0 + h][:, :], AF.Silu),
                    waits=[(sPE, pe)], inc=sACT)
                ag = k.emit("vector", lambda e, h=h, j=j: e.tensor_mul(
                    gsec[:, j * 1024 + h * 512:j * 1024 + h * 512 + 512],
                    silu_scr[:, h * 512:h * 512 + 512], c.ps[2 + h][:, :]),
                    waits=[(sACT, asl)], inc=sDVE)
                if h == 0:
                    cons_h0 = (ag, asl)
            cons_hc[hc] = (cons_h0, (ag, asl))
            last_hc_pe = pe
            for h in range(2):
                pe_lora2[hc] = k.emit("tensor", lambda e, h=h, j=j, slot=slot, hc=hc: e.matmul(
                    c.ps[6 + h][0:16, :], c.SMB[:, 128 + slot * 16:128 + slot * 16 + 16],
                    gsec[:, j * 1024 + h * 512:j * 1024 + h * 512 + 512],
                    start=(hc == 0), stop=(hc == HCN - 1), skip_group_check=True),
                    waits=[(sDVE, ag)], inc=sPE)

        for ot in range(DC):
            slot = ot % 2
            w2sem = f"w2{slot}"
            gate = [(sDVE, w2_cons.get((s, ot - 2), (0, 0))[0]),
                    (sACT, w2_cons.get((s, ot - 2), (0, 0))[1])]
            if s == 0 and ot < 2:
                gate.append((sACT, a_evac["a3t"]))
            k.dma("gpsimd", w2b[slot][:, 0:ssz * 128],
                  d["w2l"][ot][:, hc_base:hc_base + ssz, :].rearrange("d c o -> d (c o)"),
                  w2sem, waits=gate)
            if s == len(SEC_SIZES) - 1:
                k.dma("gpsimd", c.L16[:, 3072 + slot * 128:3072 + slot * 128 + 128],
                      d["b2t"][:, ot * 128:(ot + 1) * 128], w2sem)
            w2done = k.cnt[w2sem]
            pe = None
            for h in range(2):
                for jj in range(ssz):
                    pe = k.emit("tensor", lambda e, jj=jj, h=h, slot=slot, ssz=ssz: e.matmul(
                        c.ps[4 + h][:, :], w2b[slot][:, jj * 128:jj * 128 + 128],
                        gsec[:, jj * 1024 + h * 512:jj * 1024 + h * 512 + 512],
                        start=(jj == 0), stop=(jj == ssz - 1)),
                        waits=([(w2sem, w2done), (sDVE, w2_cons.get((s, ot - 1), (0, 0))[0]),
                                (sACT, w2_cons.get((s, ot - 1), (0, 0))[1])]
                               if (h == 0 and jj == 0) else []), inc=sPE)
            if s < len(SEC_SIZES) - 1:
                cpb = [c.af(0, 1), c.SINK[:, 0:1024]][ot % 2]
                dvc2 = None
                for h in range(2):
                    dvc2 = k.emit("vector", lambda e, h=h, cpb=cpb: e.tensor_copy(
                        cpb[:, h * 512:h * 512 + 512], c.ps[4 + h][:, :]),
                        waits=[(sPE, pe)] + ([(sDVE, w2_rmw.get((s, ot - 2), 0)),
                                              (sOut, k.cnt[sOut] if s == 0 and ot >= 2 else 0)]
                                             if h == 0 else []),
                        inc=sDVE)
                dv = dvc2
                if s > 0:
                    pl = k.dma("sync", part_f32, d["part"][ot * 128:(ot + 1) * 128, :], "pp",
                               waits=[(sOut, k.cnt[sOut])])
                    dv = k.emit("vector", lambda e, cpb=cpb: e.tensor_add(
                        part_f32, part_f32, cpb),
                        waits=[(sDVE, dvc2), ("pp", k.cnt["pp"])], inc=sDVE)
                    k.dma("sync", d["part"][ot * 128:(ot + 1) * 128, :], part_f32, sOut,
                          waits=[(sDVE, dv)])
                else:
                    k.dma("sync", d["part"][ot * 128:(ot + 1) * 128, :], cpb, sOut,
                          waits=[(sDVE, dvc2)])
                w2_rmw[(s, ot)] = dv
                w2_cons[(s, ot)] = (dvc2, 0)
            else:
                if lora2_evac is None:
                    for h in range(2):
                        lora2_evac = k.emit("scalar", lambda e, h=h: e.mul(
                            c.L16[:, 2048 + h * 512:2048 + h * 512 + 512],
                            c.ps[6 + h][0:16, :], LSC),
                            waits=[(sPE, pe_lora2[HCN - 1])], inc=sACT)
                for h in range(2):
                    pe = k.emit("tensor", lambda e, h=h, slot=slot: e.matmul(
                        c.ps[4 + h][:, :], c.L16[0:16, 3072 + slot * 128:3072 + slot * 128 + 128],
                        c.L16[0:16, 2048 + h * 512:2048 + h * 512 + 512],
                        start=False, stop=True, skip_group_check=True),
                        waits=[(sACT, lora2_evac)], inc=sPE)
                pl = k.dma("sync", part_f32, d["part"][ot * 128:(ot + 1) * 128, :], "pp",
                           waits=[(sACT, prev_part_user.get(ot - 1, 0)),
                                  (sOut, k.cnt[sOut])])
                xl = k.dma("sync", x_f32, d["x"][ot * 128:(ot + 1) * 128, :], "px",
                           waits=[(sDVE, w2_cons.get((s, ot - 1), (0, 0))[0])])
                dv = None
                for h in range(2):
                    hs = slice(h * 512, h * 512 + 512)
                    dv = k.emit("vector", lambda e, h=h, hs=hs: e.tensor_add(
                        part_f32[:, hs], part_f32[:, hs], c.ps[4 + h][:, :]),
                        waits=[(sPE, pe)] + ([("pp", k.cnt["pp"])] if h == 0 else []), inc=sDVE)
                ag1 = k.emit("scalar", lambda e: e.activation(
                    silu_scr, part_f32, AF.Silu), waits=[(sDVE, dv)], inc=sACT)
                prev_part_user[ot] = ag1
                dv = k.emit("vector", lambda e, ot=ot: e.tensor_mul(
                    silu_scr, xn[:, ot * 1024:(ot + 1) * 1024], silu_scr),
                    waits=[(sACT, ag1)], inc=sDVE)
                dv = k.emit("vector", lambda e: e.tensor_add(
                    silu_scr, silu_scr, x_f32),
                    waits=[("px", k.cnt["px"])], inc=sDVE)
                x1_store_last = k.dma("sync", d["x1"][ot * 128:(ot + 1) * 128, :],
                                      silu_scr, sOut, waits=[(sDVE, dv)])
                par2 = ot % 2
                bq = k.emit("vector", lambda e, par2=par2: e.tensor_copy(
                    c.ab(64 + par2, 1), silu_scr),
                    waits=[(sPE, pe_stat_mm.get(ot - 2, 0))], inc=sDVE)
                sqa = k.emit("scalar", lambda e, par2=par2: e.activation(
                    c.ab(66 + par2, 1), c.ab(64 + par2, 1), AF.Square),
                    waits=[(sDVE, bq)] + ([(sACT, lora2_evac)] if ot == 0 else []),
                    inc=sACT)
                pstat = None
                for h in range(2):
                    k.emit("tensor", lambda e, h=h, par2=par2, ot=ot: e.matmul(
                        c.ps[2 + h][0:1, :], c.ones_col,
                        c.ab(64 + par2, 1)[:, h * 512:h * 512 + 512],
                        start=(ot == 0), stop=(ot == DC - 1)),
                        waits=[(sDVE, bq), (sIn, c.const_done)], inc=sPE)
                for h in range(2):
                    pstat = k.emit("tensor", lambda e, h=h, par2=par2, ot=ot: e.matmul(
                        c.ps[6 + h][0:1, :], c.ones_col,
                        c.ab(66 + par2, 1)[:, h * 512:h * 512 + 512],
                        start=(ot == 0), stop=(ot == DC - 1)),
                        waits=[(sACT, sqa)], inc=sPE)
                pe_stat_mm[ot] = pstat
                w2_cons[(s, ot)] = (dv, ag1)
        hc_base += ssz
        if stage <= 1 + s:
            return

    if stage <= 4:
        return
    xn1 = c.ab(32, 32)
    rm = c.ROWS[0:1, 0:1024]
    ra = c.ROWS[0:1, 1024:2048]
    rt = c.SINK[0:1, 0:1024]
    pe_stats1 = k.cnt[sPE]
    for h in range(2):
        k.emit("vector", lambda e, h=h: e.tensor_copy(rm[:, h * 512:h * 512 + 512],
                                                      c.ps[2 + h][0:1, :]),
               waits=[(sPE, pe_stats1)], inc=sDVE)
    for h in range(2):
        k.emit("vector", lambda e, h=h: e.tensor_copy(rt[:, h * 512:h * 512 + 512],
                                                      c.ps[6 + h][0:1, :]), inc=sDVE)
    k.emit("vector", lambda e: e.tensor_scalar_mul(rm, rm, 1.0 / D), inc=sDVE)
    k.emit("vector", lambda e: e.tensor_scalar_mul(rt, rt, 1.0 / D), inc=sDVE)
    k.emit("vector", lambda e: e.tensor_mul(ra, rm, rm), inc=sDVE)
    dv = k.emit("vector", lambda e: e.tensor_sub(rt, rt, ra), inc=sDVE)
    a1 = k.emit("scalar", lambda e: e.activation(ra, rt, AF.Sqrt, bias=c.eps_ln),
                waits=[(sDVE, dv), (sIn, c.const_done)], inc=sACT)
    dv = k.emit("vector", lambda e: e.reciprocal(ra, ra), waits=[(sACT, a1)], inc=sDVE)
    k.wait("vector", sDVE, dv)
    dv = k.emit("vector", lambda e: e.scalar_tensor_tensor(rm, rm, -1.0, ra, ALU.mult, ALU.mult),
                inc=sDVE)
    pe_rep = None
    for h in range(2):
        k.emit("tensor", lambda e, h=h: e.matmul(
            c.ps[0 + h][:, :], c.ones_row, ra[:, h * 512:h * 512 + 512], start=True, stop=True),
            waits=[(sDVE, dv)], inc=sPE)
    for h in range(2):
        pe_rep = k.emit("tensor", lambda e, h=h: e.matmul(
            c.ps[2 + h][:, :], c.ones_row, rm[:, h * 512:h * 512 + 512], start=True, stop=True),
            inc=sPE)
    x1c = [c.af(1, 1), c.af(3, 1)]
    ln1_done = None
    apl = {}
    for ci in range(DC):
        par = ci % 2
        k.dma("sync", x1c[par], d["x1"][ci * 128:(ci + 1) * 128, :], "x1r",
              waits=([(sOut, k.cnt[sOut])] if ci < 2 else
                     [(sDVE, apl.get(ci - 2, 0))]))
        x1r = k.cnt["x1r"]
        for h in range(2):
            sl = slice(ci * 1024 + h * 512, ci * 1024 + h * 512 + 512)
            k.emit("vector", lambda e, par=par, h=h: e.tensor_mul(
                c.af(0, 1)[:, 0:512], x1c[par][:, h * 512:h * 512 + 512], c.ps[0 + h][:, :]),
                waits=[(sPE, pe_rep), ("x1r", x1r)], inc=sDVE)
            ln1_done = k.emit("vector", lambda e, sl=sl, h=h: e.tensor_add(
                xn1[:, sl], c.af(0, 1)[:, 0:512], c.ps[2 + h][:, :]), inc=sDVE)
        apl[ci] = ln1_done
    c.apl = apl
    for i in range(4):
        k.dma("sync", d["xn1"][i * 1024:(i + 1) * 1024, :]
              .rearrange("(c d) t -> d c t", d=128),
              sview(c.ab(32 + 8 * i, 8), 1024), sOut,
              waits=[(sDVE, ln1_done)])
    c.ln1_done = ln1_done
    c.xn1 = xn1
def emit_p2(nc, k, stack, c, load_xn1=True, core=None, stage2=99, part="all"):
    d = c.d
    xn1 = c.ab(32, 32)
    gqa = c.SM[:, 131:143]
    gkva = c.SM[:, 143:147]
    Rt = c.SMB[0:64, 16:80]
    ld2 = [
        k.dma("sync", c.SM[:, 131:143], d["gqa"], sIn),
        k.dma("sync", c.SM[:, 143:147], d["gkva"], sIn),
        k.dma("sync", c.SMB[0:64, 16:80], d["rt"], sIn),
        k.dma("sync", c.AF32[0:64, 1024:2048], d["cos"], sIn),
        k.dma("sync", c.AF32[0:64, 2048:3072], d["sin"], sIn),
    ]
    cos_sb = c.AF32[0:64, 1024:2048]
    sin_sb = c.AF32[0:64, 2048:3072]
    p2c = k.cnt[sIn]
    c.const_done = p2c

    if load_xn1:
        for i in range(4):
            k.dma("sync", sview(c.ab(32 + 8 * i, 8), 1024),
                  d["xn1"][i * 1024:(i + 1) * 1024, :].rearrange("(c d) t -> d c t", d=128),
                  "x")
    xn1done = k.cnt["x"]
    if stage2 <= -5:
        k.wait("gpsimd", "x", xn1done)
        k.wait("sync", sIn, k.cnt[sIn])
        return

    wslt = [c.ab(64, 4), c.ab(68, 4)]
    sqscr = c.af(3, 1)
    def rms_rows(pe_stat, denom, r0):
        rr = c.ROWS[0:1, r0:r0 + 1024]
        dvc = None
        for h in range(2):
            dvc = k.emit("vector", lambda e, h=h: e.tensor_copy(rr[:, h * 512:h * 512 + 512],
                                                                c.ps[6 + h][0:1, :]),
                         waits=[(sPE, pe_stat)], inc=sDVE)
        sc = k.emit("scalar", lambda e: e.activation(rr, rr, AF.Sqrt, scale=1.0 / denom),
                    waits=[(sDVE, dvc)], inc=sACT)
        dv = k.emit("vector", lambda e: e.tensor_scalar_add(rr, rr, 1e-6), waits=[(sACT, sc)], inc=sDVE)
        dv = k.emit("vector", lambda e: e.reciprocal(rr, rr), inc=sDVE)
        k.wait("vector", sDVE, dv)
        pe_rep = None
        for h in range(2):
            pe_rep = k.emit("tensor", lambda e, h=h: e.matmul(
                c.ps[0 + h][:, :], c.ones_row, rr[:, h * 512:h * 512 + 512],
                start=True, stop=True), waits=[(sDVE, dv)], inc=sPE)
        return pe_rep

    kcn = c.ab(4, 8)[:, 0:4 * 1024]
    krb = c.ab(8, 1)[0:64, :]
    kev = {}
    pe_kv = {}
    for ot in range(5):
        par = ot % 2
        wsem = f"wh{par}"
        nr = 128 if ot < 4 else 64
        k.dma("gpsimd", wslt[par][:, 0:DC * nr],
              (d["wkval"][ot].rearrange("d c o -> d (c o)") if ot < 4
               else d["wkvrl"].rearrange("d c o -> d (c o)")), wsem,
              waits=[(sPE, pe_kv.get(ot - 2, 0))])
        wdone = k.cnt[wsem]
        apl = getattr(c, "apl", {})
        pe = None
        for h in range(2):
            for ci in range(DC):
                pe = k.emit("tensor", lambda e, ci=ci, h=h, par=par, nr=nr: e.matmul(
                    c.ps[0 + h][0:nr, :], wslt[par][:, ci * nr:ci * nr + nr],
                    xn1[:, ci * 1024 + h * 512:ci * 1024 + h * 512 + 512],
                    start=(ci == 0), stop=(ci == DC - 1)),
                    waits=([(wsem, wdone), (sACT, kev.get(ot - 1, (0, 0))[0] or 0),
                            (sDVE, kev.get(ot - 1, (0, 0))[1]),
                            (sDVE, apl.get(0, 0) if ot == 0 else 0)]
                           if (h == 0 and ci == 0) else
                           ([(sDVE, apl.get(ci, 0))] if (h == 0 and ot == 0) else [])),
                    inc=sPE)
        pe_kv[ot] = pe
        aq = dq = None
        sqb = c.ab(24 + par, 1)
        for h in range(2):
            if ot < 4:
                dq = k.emit("vector", lambda e, h=h, ot=ot: e.tensor_copy(
                    kcn[:, ot * 1024 + h * 512:ot * 1024 + h * 512 + 512], c.ps[0 + h][:, :]),
                    waits=[(sPE, pe)], inc=sDVE)
                aq = k.emit("scalar", lambda e, h=h, ot=ot, sqb=sqb: e.activation(
                    sqb[:, h * 512:h * 512 + 512],
                    kcn[:, ot * 1024 + h * 512:ot * 1024 + h * 512 + 512], AF.Square),
                    waits=[(sDVE, dq)], inc=sACT)
            else:
                dq = k.emit("vector", lambda e, h=h: e.tensor_copy(
                    krb[:, h * 512:h * 512 + 512], c.ps[0 + h][0:64, :]),
                    waits=[(sPE, pe)], inc=sDVE)
        if ot < 4:
            for h in range(2):
                k.emit("tensor", lambda e, h=h, ot=ot, sqb=sqb: e.matmul(
                    c.ps[6 + h][0:1, :], c.ones_col, sqb[:, h * 512:h * 512 + 512],
                    start=(ot == 0), stop=(ot == 3)),
                    waits=[(sACT, aq)], inc=sPE)
        kev[ot] = (aq if aq else 0, dq)
    pe_kstat = k.cnt[sPE]
    pe_rep = rms_rows(pe_kstat, KVR, 1024)
    dv_kcn = None
    for ot in range(4):
        for h in range(2):
            sl = slice(ot * 1024 + h * 512, ot * 1024 + h * 512 + 512)
            k.emit("vector", lambda e, sl=sl, h=h: e.tensor_mul(
                sqscr[:, h * 512:h * 512 + 512], kcn[:, sl], c.ps[0 + h][:, :]),
                waits=[(sPE, pe_rep)], inc=sDVE)
            dv_kcn = k.emit("vector", lambda e, sl=sl, ot=ot, h=h: e.tensor_scalar(
                kcn[:, sl], sqscr[:, h * 512:h * 512 + 512], gkva[:, ot:ot + 1], None, ALU.mult),
                inc=sDVE)

    pe_rot = None
    for h in range(2):
        pe_rot = k.emit("tensor", lambda e, h=h: e.matmul(
            c.ps[2][0:64, :] if h == 0 else c.ps[3][0:64, :], Rt,
            krb[:, h * 512:h * 512 + 512], start=True, stop=True),
            waits=[(sDVE, kev[4][1]), (sIn, p2c)], inc=sPE)
    dv = None
    for h in range(2):
        hs = slice(h * 512, h * 512 + 512)
        k.emit("vector", lambda e, h=h, hs=hs: e.tensor_mul(
            c.af(3, 1)[0:64, hs], c.ps[2 + h][0:64, :], sin_sb[:, hs]),
            waits=[(sPE, pe_rot)], inc=sDVE)
        k.emit("vector", lambda e, h=h, hs=hs: e.tensor_mul(
            c.af(0, 1)[0:64, hs], krb[:, hs], cos_sb[:, hs]), inc=sDVE)
        dv = k.emit("vector", lambda e, h=h, hs=hs: e.tensor_add(
            c.ab(9, 1)[0:64, hs], c.af(0, 1)[0:64, hs], c.af(3, 1)[0:64, hs]), inc=sDVE)
    k.dma("sync", d["krc"], c.ab(9, 1)[0:64, :], "ct0", waits=[(sDVE, dv)])

    if stage2 <= -1:
        return
    stg2 = [c.ab(10, 1), c.ab(11, 1)]
    pe_kn = {}
    st_kn = {}
    for hh in range(NH):
        par = hh % 2
        wsem = f"wh{par}"
        k.dma("gpsimd", wslt[par][:, 0:KVRC * 128],
              d["wknl"][hh].rearrange("d c o -> d (c o)"), wsem,
              waits=[(sPE, pe_kn.get(hh - 2, pe_rot if hh < 2 else 0))])
        wdone = k.cnt[wsem]
        pe = None
        for h in range(2):
            for ci in range(KVRC):
                pe = k.emit("tensor", lambda e, ci=ci, h=h, par=par: e.matmul(
                    c.ps[0 + h][:, :], wslt[par][:, ci * 128:ci * 128 + 128],
                    kcn[:, ci * 1024 + h * 512:ci * 1024 + h * 512 + 512],
                    start=(ci == 0), stop=(ci == KVRC - 1)),
                    waits=([(wsem, wdone), (sDVE, dv_kcn),
                            (sDVE, st_kn.get(hh - 1, (0, 0))[1])]
                           if (h == 0 and ci == 0) else []), inc=sPE)
        pe_kn[hh] = pe
        dv = None
        for h in range(2):
            dv = k.emit("vector", lambda e, h=h, par=par: e.tensor_copy(
                stg2[par][:, h * 512:h * 512 + 512], c.ps[0 + h][:, :]),
                waits=[(sPE, pe), (f"ct{par}", k.cnt[f"ct{par}"])], inc=sDVE)
        sid = k.dma("sync", d["knc"][hh * 128:(hh + 1) * 128, :], stg2[par], f"ct{par}",
                    waits=[(sDVE, dv)])
        st_kn[hh] = (sid, dv)

    pe_v = {}
    st_v = {}
    for hg in range(8):
        par = hg % 2
        wsem = f"wh{par}"
        k.dma("gpsimd", wslt[par][:, 0:4 * 512],
              d["wvl"][hg].rearrange("d c o -> d (c o)"), wsem,
              waits=[(sPE, pe_v.get((hg - 2, 7), pe_kn.get(31, 0) if hg < 2 else 0))])
        wdone = k.cnt[wsem]
        for ts in range(8):
            pe = None
            for ci in range(KVRC):
                pe = k.emit("tensor", lambda e, ci=ci, par=par, ts=ts: e.matmul(
                    c.ps[2 + ts % 2][:, :],
                    kcn[:, ci * 1024 + ts * 128:ci * 1024 + ts * 128 + 128],
                    wslt[par][:, ci * 512:(ci + 1) * 512],
                    start=(ci == 0), stop=(ci == KVRC - 1)),
                    waits=([(wsem, wdone), (sDVE, st_v.get((hg, ts - 2),
                            st_v.get((hg - 1, 6 + ts % 2), (0, 0)))[1])]
                           if ci == 0 else []), inc=sPE)
            pe_v[(hg, ts)] = pe
            dv = k.emit("vector", lambda e, par=par, ts=ts: e.tensor_copy(
                stg2[ts % 2][:, 0:512], c.ps[2 + ts % 2][:, :]),
                waits=[(sPE, pe), (f"ct{ts % 2}", k.cnt[f"ct{ts % 2}"])], inc=sDVE)
            sid = k.dma("sync", d["vc"][ts * 128:(ts + 1) * 128, hg * 512:(hg + 1) * 512],
                        stg2[ts % 2][:, 0:512], f"ct{ts % 2}", waits=[(sDVE, dv)])
            st_v[(hg, ts)] = (sid, dv)

    k.wait("gpsimd", "ct0", k.cnt["ct0"])
    k.wait("gpsimd", "ct1", k.cnt["ct1"])
    rg = [list(range(NCORES))]
    k.emit("gpsimd", lambda e: e.collective_compute(
        "AllGather", ALU.bypass, replica_groups=rg,
        ins=[d["knc"].opt()], outs=[d["kng"].opt()]), inc="cc")
    k.emit("gpsimd", lambda e: e.collective_compute(
        "AllGather", ALU.bypass, replica_groups=rg,
        ins=[d["vc"].opt()], outs=[d["vg"].opt()]), inc="cc")
    cc_done = k.emit("gpsimd", lambda e: e.collective_compute(
        "AllGather", ALU.bypass, replica_groups=rg,
        ins=[d["krc"].opt()], outs=[d["krg"].opt()]), inc="cc")
    if c.overlap_hook is not None:
        c.overlap_hook()

    qc = c.ab(0, 12)
    wslt = [c.ab(64, 4), c.ab(68, 4)]
    sqscr = c.af(3, 1)
    evq = {}
    pe_q = {}
    dve_hmat = k.cnt[sDVE]
    for ot in range(QRC):
        par = ot % 2
        wsem = f"wh{par}"
        gate = [(sPE, pe_q.get(ot - 2, pe_v.get((7, 7), 0) if ot < 2 else 0))]
        k.dma("gpsimd", wslt[par], d["wqal"][ot].rearrange("d c o -> d (c o)"),
              wsem, waits=gate)
        wdone = k.cnt[wsem]
        pe = None
        for h in range(2):
            for ci in range(DC):
                pe = k.emit("tensor", lambda e, ci=ci, h=h, par=par: e.matmul(
                    c.ps[0 + h][:, :], wslt[par][:, ci * 128:ci * 128 + 128],
                    xn1[:, ci * 1024 + h * 512:ci * 1024 + h * 512 + 512],
                    start=(ci == 0), stop=(ci == DC - 1)),
                    waits=([(wsem, wdone), ("x", xn1done),
                            (sDVE, dve_hmat if ot == 0 else 0),
                            (sACT, evq.get(ot - 1, (0, 0))[0]),
                            (sDVE, evq.get(ot - 1, (0, 0))[1])]
                           if (h == 0 and ci == 0) else []), inc=sPE)
        pe_q[ot] = pe
        aq = dq = None
        sqb = c.W2SB[:, 1024:2048]
        for h in range(2):
            dq = k.emit("vector", lambda e, h=h, ot=ot: e.tensor_copy(
                qc[:, ot * 1024 + h * 512:ot * 1024 + h * 512 + 512], c.ps[0 + h][:, :]),
                waits=[(sPE, pe)], inc=sDVE)
            aq = k.emit("scalar", lambda e, h=h, ot=ot, sqb=sqb: e.activation(
                sqb[:, h * 512:h * 512 + 512],
                qc[:, ot * 1024 + h * 512:ot * 1024 + h * 512 + 512], AF.Square),
                waits=[(sDVE, dq)], inc=sACT)
        for h in range(2):
            k.emit("tensor", lambda e, h=h, ot=ot, sqb=sqb: e.matmul(
                c.ps[6 + h][0:1, :], c.ones_col, sqb[:, h * 512:h * 512 + 512],
                start=(ot == 0), stop=(ot == QRC - 1)),
                waits=[(sACT, aq), (sIn, c.const_done)], inc=sPE)
        evq[ot] = (aq, dq)
    pe_qstat = k.cnt[sPE]
    if stage2 <= -4:
        k.wait("tensor", sPE, pe_qstat)
        return

    pe_rep = rms_rows(pe_qstat, QR, 0)
    qcn = c.ab(12, 12)
    dv_qcn = None
    for ot in range(QRC):
        for h in range(2):
            sl = slice(ot * 1024 + h * 512, ot * 1024 + h * 512 + 512)
            k.emit("vector", lambda e, sl=sl, h=h: e.tensor_mul(
                sqscr[:, h * 512:h * 512 + 512], qc[:, sl], c.ps[0 + h][:, :]),
                waits=[(sPE, pe_rep), (sIn, p2c)], inc=sDVE)
            dv_qcn = k.emit("vector", lambda e, sl=sl, ot=ot, h=h: e.tensor_scalar(
                qcn[:, sl], sqscr[:, h * 512:h * 512 + 512], gqa[:, ot:ot + 1], None, ALU.mult),
                inc=sDVE)

    if stage2 <= -3:
        return
    stg = [c.ab(0, 1), c.ab(1, 1)]
    pe_qb = {}
    st_q = {}
    for ot in range(48):
        par = ot % 2
        wsem = f"wh{par}"
        k.dma("gpsimd", wslt[par][:, 0:QRC * 128],
              d["wqbl"][ot].rearrange("d c o -> d (c o)"), wsem,
              waits=[(sPE, pe_qb.get(ot - 2, pe_rep if ot < 2 else 0))])
        wdone = k.cnt[wsem]
        pe = None
        for h in range(2):
            for ci in range(QRC):
                pe = k.emit("tensor", lambda e, ci=ci, h=h, par=par: e.matmul(
                    c.ps[2 + h][:, :], wslt[par][:, ci * 128:ci * 128 + 128],
                    qcn[:, ci * 1024 + h * 512:ci * 1024 + h * 512 + 512],
                    start=(ci == 0), stop=(ci == QRC - 1)),
                    waits=([(wsem, wdone), (sDVE, dv_qcn),
                            (sDVE, st_q.get(ot - 1, (0, 0))[1])]
                           if (h == 0 and ci == 0) else []), inc=sPE)
        pe_qb[ot] = pe
        dv = None
        for h in range(2):
            dv = k.emit("vector", lambda e, h=h, par=par: e.tensor_copy(
                stg[par][:, h * 512:h * 512 + 512], c.ps[2 + h][:, :]),
                waits=[(sPE, pe), ("sq", k.cnt["sq"])], inc=sDVE)
        sq_id = k.dma("sync", d["q"][ot * 128:(ot + 1) * 128, :], stg[par], "sq",
                      waits=[(sDVE, dv)])
        st_q[ot] = (sq_id, dv)
    q_done = k.cnt["sq"]

    if stage2 <= -2:
        return
    if stage2 <= 0:
        return

    c.p2s = dict(q_done=q_done, dvq_const=p2c)
    if part == "proj_stop":
        return
    kh = [c.ab(0, 8), c.ab(8, 8)]
    vh = [c.ab(16, 8), c.ab(24, 8)]
    krh = c.ab(32, 8)[0:64, :]
    qn = [c.W2SB[:, 0:1024], c.W2SB[:, 1024:2048]]
    exps = [c.ab(42, 1), c.ab(43, 1), c.ab(41, 1)]
    ao = c.ab(44, 32)
    qraw = [c.ab(76, 1)[0:64, :], c.ab(77, 1)[0:64, :]]
    qrop = [c.ab(78, 1)[0:64, :], c.ab(79, 1)[0:64, :]]
    lacc = c.af(0, 1)
    rl = c.af(3, 1)
    kng4 = d["kng"].rearrange("(rr hh dd) t -> hh dd rr t", hh=NH, dd=128)
    vg4 = d["vg"].rearrange("(kt kk) (hh dv) -> hh kk kt dv", kk=128, hh=NH)

    k.wait("sync", "cc", cc_done)
    krh_ld = k.dma("sync", sview(krh, 1024), d["krg"].rearrange("(rr dd) t -> dd rr t", dd=64), "krh")
    pe_av = {}
    cons_head = {}
    norm_done = {}
    pe_s_last = {}
    rope_done = {}
    pe_l_done = {}

    def emit_rope(hh, first_waits=()):
        par = hh % 2
        pe_rot = None
        for h in range(2):
            pe_rot = k.emit("tensor", lambda e, h=h, par=par: e.matmul(
                c.ps[6][0:64, :] if h == 0 else c.ps[7][0:64, :], Rt,
                qraw[par][:, h * 512:h * 512 + 512], start=True, stop=True),
                waits=((list(first_waits) + [(sIn, p2c)]) if h == 0 else []), inc=sPE)
        dvq = None
        for h in range(2):
            hs = slice(h * 512, h * 512 + 512)
            psb = c.ps[6] if h == 0 else c.ps[7]
            k.emit("vector", lambda e, hs=hs, psb=psb: e.tensor_mul(
                rl[0:64, hs], psb[0:64, :], sin_sb[:, hs]),
                waits=([(sPE, pe_rot)] + ([(sPE, pe_s_last.get(hh - 2, 0))] if h == 0 else [])),
                inc=sDVE)
            k.emit("vector", lambda e, hs=hs, par=par, psb=psb: e.tensor_mul(
                psb[0:64, :], qraw[par][:, hs], cos_sb[:, hs]), inc=sDVE)
            dvq = k.emit("vector", lambda e, hs=hs, par=par, psb=psb: e.tensor_add(
                qrop[par][:, hs], rl[0:64, hs], psb[0:64, :]), inc=sDVE)
        rope_done[hh] = dvq
        return dvq

    def emit_qload(hh, gate_scores):
        par = hh % 2
        asem = f"at{par}"
        qw = [(sPE, gate_scores)] if gate_scores else []
        if hh == 0:
            qw.append(("sq", q_done))
        k.dma("sync", qn[par], d["q"][hh * QK:hh * QK + 128, :], asem, waits=qw)
        k.dma("sync", qraw[par], d["q"][hh * QK + 128:hh * QK + QK, :], asem)
        return k.cnt[asem]

    q0_done = emit_qload(0, 0)
    for hh in range(NH):
        par = hh % 2
        asem = f"at{par}"
        gate = [(sPE, pe_av.get(hh - 2, 0)), (sDVE, cons_head.get(hh - 2, 0))]
        if hh < 2:
            gate.append(("sq", q_done))
        k.dma("sync", sview(kh[par], 1024), kng4[hh], asem, waits=gate)
        k.dma("sync", vh[par][:, :].rearrange("d (kt dv) -> d kt dv", dv=128), vg4[hh], asem)
        adone = k.cnt[asem]
        if getattr(c, "pending_tail", None) is not None:
            c.pending_tail()
            pe_l_done[hh - 1] = k.cnt[sPE]
            c.pending_tail = None
        if hh == 0:
            dvq = emit_rope(0, first_waits=[(asem, q0_done), (sDVE, k.cnt[sDVE])])
        else:
            dvq = rope_done[hh]
        exp_done = {}
        pool_done = {}
        pe_s = {}
        pe_avk = {}
        ace = k.cnt[sACT]

        def emit_scores(kt):
            sp = kt % 2
            pe = None
            for h in range(2):
                k.emit("tensor", lambda e, h=h, kt=kt, sp=sp, par=par: e.matmul(
                    c.ps[2 * sp + h][:, :], kh[par][:, kt * 128:(kt + 1) * 128],
                    qn[par][:, h * 512:h * 512 + 512], start=True, stop=False),
                    waits=(([(sACT, exp_done.get(kt - 2, ace if kt < 2 else 0)),
                             (sDVE, dvq)]) if h == 0 else []), inc=sPE)
            for h in range(2):
                pe = k.emit("tensor", lambda e, h=h, kt=kt, sp=sp, par=par: e.matmul(
                    c.ps[2 * sp + h][:, :], krh[:, kt * 128:(kt + 1) * 128],
                    qrop[par][:, h * 512:h * 512 + 512], start=False, stop=True),
                    waits=([("krh", krh_ld)] if (hh == 0 and kt == 0 and h == 0) else []),
                    inc=sPE)
            pe_s[kt] = pe
            pe_s_last[hh] = pe

        def emit_exp_pool(kt):
            ae = None
            for h in range(2):
                ae = k.emit("scalar", lambda e, h=h, kt=kt: e.activation(
                    exps[kt % 3][:, h * 512:h * 512 + 512], c.ps[2 * (kt % 2) + h][:, :],
                    AF.Exp, scale=float(QK) ** -0.5),
                    waits=[(sPE, pe_s[kt])] + ([(sPE, pe_avk.get(kt - 3, 0)),
                                                (sPOOL, pool_done.get(kt - 3, (0, 0))[0]),
                                                (sDVE, pool_done.get(kt - 3, (0, 0))[1])]
                           if h == 0 else []),
                    inc=sACT)
            exp_done[kt] = ae
            if kt == 0:
                pd = k.emit("gpsimd", lambda e: e.tensor_copy(lacc[:, 0:512], exps[0][:, 0:512]),
                            waits=[(sACT, ae), (sDVE, dvq),
                                   (sPE, pe_l_done.get(hh - 1, 0))], inc=sPOOL)
                dd_ = k.emit("vector", lambda e: e.tensor_copy(lacc[:, 512:1024], exps[0][:, 512:1024]),
                             waits=[(sACT, ae), (sPE, pe_l_done.get(hh - 1, 0))], inc=sDVE)
            else:
                pd = k.emit("gpsimd", lambda e, kt=kt: e.tensor_add(
                    lacc[:, 0:512], lacc[:, 0:512], exps[kt % 3][:, 0:512]),
                    waits=[(sACT, ae)], inc=sPOOL)
                dd_ = k.emit("vector", lambda e, kt=kt: e.tensor_add(
                    lacc[:, 512:1024], lacc[:, 512:1024], exps[kt % 3][:, 512:1024]),
                    waits=[(sACT, ae)], inc=sDVE)
            pool_done[kt] = (pd, dd_)

        def emit_av(kt):
            pe2 = None
            for h in range(2):
                pe2 = k.emit("tensor", lambda e, h=h, kt=kt, par=par: e.matmul(
                    c.ps[4 + h][:, :], vh[par][:, kt * 128:(kt + 1) * 128],
                    exps[kt % 3][:, h * 512:h * 512 + 512],
                    start=(kt == 0), stop=(kt == KTN - 1)),
                    waits=([(sACT, exp_done[kt])] if h == 0 else []) +
                          ([(sDVE, cons_head.get(hh - 1, 0))] if (kt == 0 and h == 0) else []),
                    inc=sPE)
            pe_avk[kt] = pe2

        emit_scores(0)
        emit_exp_pool(0)
        emit_scores(1)
        emit_exp_pool(1)
        if hh + 1 < NH:
            nq = emit_qload(hh + 1, pe_s_last.get(hh - 1, 0))
            emit_rope(hh + 1, first_waits=[(f"at{(hh + 1) % 2}", nq),
                                           (sDVE, norm_done.get(hh - 1, 0))])
        for kt in range(2, KTN):
            emit_scores(kt)
            emit_exp_pool(kt)
            emit_av(kt - 2)
            if kt == 7 and getattr(c, "pending_tail2", None) is not None:
                c.pending_tail2()
                c.pending_tail2 = None
        emit_av(KTN - 2)
        emit_av(KTN - 1)
        pe_av[hh] = pe_avk[KTN - 1]
        evo = None
        for h in range(2):
            evo = k.emit("vector", lambda e, h=h, hh=hh: e.tensor_copy(
                ao[:, hh * 1024 + h * 512:hh * 1024 + h * 512 + 512],
                c.ps[4 + h][:, :]),
                waits=([(sPE, pe_av[hh])] if h == 0 else []), inc=sDVE)
        cons_head[hh] = evo

        def head_tail(hh=hh, pool_last=pool_done[KTN - 1]):
            pe_l = None
            for h in range(2):
                pe_l = k.emit("tensor", lambda e, h=h: e.matmul(
                    c.ps[6 + h][0:1, :], c.ones_col_f, lacc[:, h * 512:h * 512 + 512],
                    start=True, stop=True),
                    waits=[(sPOOL, pool_last[0]), (sDVE, pool_last[1])]
                          + ([(sDVE, norm_done.get(hh - 1, 0)),
                              (sDVE, rope_done.get(hh + 1, 0))] if h == 0 else []),
                    inc=sPE)
            rr = c.ROWS[0:1, 0:1024]
            dv = None
            for h in range(2):
                dv = k.emit("vector", lambda e, h=h: e.tensor_copy(
                    rr[:, h * 512:h * 512 + 512], c.ps[6 + h][0:1, :]),
                    waits=[(sPE, pe_l)], inc=sDVE)
            dv = k.emit("vector", lambda e: e.reciprocal(rr, rr), inc=sDVE)
            k.wait("vector", sDVE, dv)

            def head_tail2(hh=hh, dv=dv):
                pe_rl = None
                for h in range(2):
                    pe_rl = k.emit("tensor", lambda e, h=h: e.matmul(
                        c.ps[6 + h][:, :], c.ones_row, rr[:, h * 512:h * 512 + 512],
                        start=True, stop=True), waits=[(sDVE, dv)], inc=sPE)
                for h in range(2):
                    norm_done[hh] = k.emit("vector", lambda e, h=h, hh=hh: e.tensor_mul(
                        ao[:, hh * 1024 + h * 512:hh * 1024 + h * 512 + 512],
                        ao[:, hh * 1024 + h * 512:hh * 1024 + h * 512 + 512],
                        c.ps[6 + h][:, :]),
                        waits=[(sPE, pe_rl)], inc=sDVE)
            c.pending_tail2 = head_tail2
        c.pending_tail = head_tail

    if stage2 <= 1:
        for i in range(4):
            k.dma("sync", d["s1"][i * 1024:(i + 1) * 1024, :]
                  .rearrange("(c d) t -> d c t", d=128),
                  sview(c.ab(44 + 8 * i, 8), 1024), sOut,
                  waits=[(sDVE, cons_head[NH - 1])])
        return

    if getattr(c, "pending_tail", None) is not None:
        c.pending_tail()
        c.pending_tail = None
    if getattr(c, "pending_tail2", None) is not None:
        c.pending_tail2()
        c.pending_tail2 = None

    woslt = [c.ab(32, 4), c.ab(36, 4)]
    sstg = [c.af(1, 1), c.af(2, 1)]
    pe_wo = {}
    st_s1 = {}
    for ot in range(DC):
        par = ot % 2
        wsem = f"wh{par}"
        k.dma("gpsimd", woslt[par], d["wol"][ot].rearrange("d c o -> d (c o)"),
              wsem, waits=[(sPE, pe_wo.get(ot - 2, pe_av.get(NH - 1, 0) if ot < 2 else 0))])
        wdone = k.cnt[wsem]
        pe = None
        for h in range(2):
            for ci in range(DC):
                pe = k.emit("tensor", lambda e, ci=ci, h=h, par=par: e.matmul(
                    c.ps[0 + h][:, :], woslt[par][:, ci * 128:ci * 128 + 128],
                    ao[:, ci * 1024 + h * 512:ci * 1024 + h * 512 + 512],
                    start=(ci == 0), stop=(ci == DC - 1)),
                    waits=([(wsem, wdone), (sDVE, norm_done[NH - 1]),
                            (sACT, k.cnt[sACT] if ot == 0 else 0),
                            (sDVE, st_s1.get(ot - 1, (0, 0))[1])]
                           if (h == 0 and ci == 0) else []), inc=sPE)
        pe_wo[ot] = pe
        dv = None
        for h in range(2):
            dv = k.emit("vector", lambda e, h=h, par=par: e.tensor_copy(
                sstg[par][:, h * 512:h * 512 + 512], c.ps[0 + h][:, :]),
                waits=[(sPE, pe), ("so", k.cnt["so"])], inc=sDVE)
        sid = k.dma("sync", d["s1"][ot * 128:(ot + 1) * 128, :], sstg[par], "so",
                    waits=[(sDVE, dv)])
        st_s1[ot] = (sid, dv)
    c.s1_done = k.cnt["so"]
    c.att_done_pe = pe_av.get(NH - 1, 0)
    c.att_done_dve = norm_done.get(NH - 1, 0)
    c.wo_done_pe = pe_wo.get(DC - 1, 0)


def build_p2(stage2=99):
    nc = bass.Bass("TRN2", target_bir_lowering=False)
    c = Ctx()
    c.d = {}
    d = c.d
    dr = lambda name, shape, dt, kind: nc.dram_tensor(name, shape, dt, kind=kind).ap()
    d["xn1"] = dr("xn1", [D, T], BF16, "ExternalInput")
    for nm, sh in (("wqal", [QRC, 128, DC, 128]), ("wqbl", [48, 128, QRC, 128]),
                   ("wkval", [4, 128, DC, 128]), ("wkvrl", [128, DC, 64]),
                   ("wknl", [NH, 128, KVRC, 128]), ("wvl", [8, 128, KVRC, 512]),
                   ("wol", [DC, 128, DC, 128])):
        d[nm] = dr(nm, sh, F32, "ExternalInput")
    d["gqa"] = dr("gqa", [128, QRC], F32, "ExternalInput")
    d["gkva"] = dr("gkva", [128, KVRC], F32, "ExternalInput")
    d["rt"] = dr("rt", [RD, RD], BF16, "ExternalInput")
    d["cos"] = dr("cos", [RD, T], F32, "ExternalInput")
    d["sin"] = dr("sin", [RD, T], F32, "ExternalInput")
    d["ones_row"] = dr("ones_row", [1, 128], F32, "ExternalInput")
    d["ones_col"] = dr("ones_col", [128, 1], BF16, "ExternalInput")
    d["ones_col_f"] = dr("ones_col_f", [128, 1], F32, "ExternalInput")
    d["misc_f"] = dr("misc_f", [1, 2], F32, "ExternalInput")
    d["s1"] = dr("s1", [D, T], F32, "ExternalOutput")
    d["q"] = dr("q", [NH * QK, T], BF16, "Internal")
    d["knc"] = dr("knc", [NH * 128, T], BF16, "Internal")
    d["vc"] = dr("vc", [T, NH * VD], BF16, "Internal")
    d["krc"] = dr("krc", [RD, T], BF16, "Internal")
    d["kng"] = dr("kng", [NCORES * NH * 128, T], BF16, "Internal")
    d["vg"] = dr("vg", [SEQ, NH * VD], BF16, "Internal")
    d["krg"] = dr("krg", [NCORES * RD, T], BF16, "Internal")

    stack = ExitStack()
    with stack:
        k = K(nc, stack)
        build_common(nc, k, stack, c)
        emit_p2(nc, k, stack, c, load_xn1=True, stage2=stage2)
        k.wait("sync", "so", k.cnt["so"])
        k.wait("sync", sOut, k.cnt[sOut])
        k.finish()
    return nc


def p2_inputs(core, ins, xn1=None):
    tr = lambda a: np.ascontiguousarray(np.asarray(a).T)
    t0 = core * T
    inv = (10000.0 ** (-(np.arange(0, RD, 2, dtype=np.float64) / RD)))
    freqs = np.outer(np.arange(t0, t0 + T, dtype=np.float64), inv)
    emb = np.concatenate([freqs, freqs], 1)
    def til2(w, nt, nc_, no):
        w = np.asarray(w)
        return np.ascontiguousarray(w.reshape(nc_, 128, nt, no).transpose(2, 1, 0, 3))
    wkvb = np.asarray(ins["L1_attn_w_kvb"]).reshape(KVR, NH, 2, 128)
    wkn = np.ascontiguousarray(wkvb[:, :, 0, :].reshape(KVR, NH * 128))
    wv = np.ascontiguousarray(wkvb[:, :, 1, :].reshape(KVR, NH * 128))
    wkva = np.asarray(ins["L1_attn_w_kva"])
    m = {
        "wqal": til2(ins["L1_attn_w_qa"], QRC, DC, 128),
        "wqbl": til2(ins["L1_attn_w_qb"], 48, QRC, 128),
        "wkval": til2(wkva[:, 0:KVR], 4, DC, 128),
        "wkvrl": np.ascontiguousarray(wkva[:, KVR:].reshape(DC, 128, RD).transpose(1, 0, 2)),
        "wknl": til2(wkn, NH, KVRC, 128),
        "wvl": til2(wv, 8, KVRC, 512),
        "wol": til2(ins["L1_attn_w_o"], DC, DC, 128),
        "gqa": np.ascontiguousarray(np.asarray(ins["L1_attn_g_qa"]).reshape(QRC, 128).T.astype(np.float32)),
        "gkva": np.ascontiguousarray(np.asarray(ins["L1_attn_g_kva"]).reshape(KVRC, 128).T.astype(np.float32)),
        "rt": _make_rt(),
        "cos": np.ascontiguousarray(np.cos(emb).T.astype(np.float32)),
        "sin": np.ascontiguousarray(np.sin(emb).T.astype(np.float32)),
        "ones_row": np.ones((1, 128), np.float32),
        "ones_col": np.ones((128, 1), ml_dtypes.bfloat16),
        "ones_col_f": np.ones((128, 1), np.float32),
        "misc_f": np.array([[1.0, 1e-5]], np.float32),
    }
    if xn1 is not None:
        m["xn1"] = np.ascontiguousarray(xn1[t0:t0 + T].T.astype(ml_dtypes.bfloat16))
    return m


def _make_rt():
    rt = np.zeros((RD, RD), np.float32)
    half = RD // 2
    for j in range(half):
        rt[j, j + half] = 1.0
    for j in range(half, RD):
        rt[j, j - half] = -1.0
    return rt.astype(ml_dtypes.bfloat16)


def emit_p3_rest(nc, k, stack, c, xn1done, stage3=99):
    d = c.d
    SINK = c.SINK
    xn1t = c.ab(0, 32)
    hp = SINK[:, 896:928]
    hpo = SINK[:, 928:960]
    Hres = SINK[:, 200:328]
    hres_done = c.p3s["hres_done"]
    for g in range(8):
        k.dmat("sync", xn1t[:, g * 4096:(g + 1) * 4096],
               d["xn1"][:, g * 128:(g + 1) * 128], "tm",
               waits=([("x", xn1done), (sPE, getattr(c, "att_done_pe", 0)),
                       (sDVE, getattr(c, "att_done_dve", 0))] if g == 0 else []))
    tmdone = k.cnt["tm"]

    xp = c.AF32[:, 0:1024]
    stg = [c.ab(64, 1), c.ab(65, 1)]
    for g in range(8):
        par = g % 2
        for n in range(4):
            if n == 0:
                k.emit("vector", lambda e, g=g, n=n: e.tensor_scalar(
                    xp, xn1t[:, g * 4096 + n * 1024:g * 4096 + n * 1024 + 1024],
                    hp[:, g * 4 + n:g * 4 + n + 1], None, ALU.mult),
                    waits=[("tm", tmdone), ("xps", k.cnt["xps"])])
            else:
                k.emit("vector", lambda e, g=g, n=n: e.scalar_tensor_tensor(
                    xp, xn1t[:, g * 4096 + n * 1024:g * 4096 + n * 1024 + 1024],
                    hp[:, g * 4 + n:g * 4 + n + 1], xp, ALU.mult, ALU.add))
        dvs = k.emit("vector", lambda e, par=par: e.tensor_copy(stg[par], xp),
                     waits=[(sPE, getattr(c, "wo_done_pe", 0)) if g == 0 else (sPE, 0)],
                     inc=sDVE)
        k.dma("sync", d["xpre"][g * 128:(g + 1) * 128, :], stg[par], "xps",
              waits=[(sDVE, dvs)])

    xpf = c.ab(72, 8)
    for cc2 in range(8):
        k.dmat("sync", xpf[:, cc2 * 1024:(cc2 + 1) * 1024],
               d["xpre"][:, cc2 * 128:(cc2 + 1) * 128], "tm2",
               waits=[("xps", k.cnt["xps"])] if cc2 == 0 else [])
    tm2 = k.cnt["tm2"]
    ystg = [c.ab(64, 1), c.ab(65, 1)]
    pe_y = {}
    st_y = {}
    for ot in range(8):
        par = ot % 2
        k.dma("gpsimd", c.W2SB[:, par * 1024:par * 1024 + 1024],
              d["wbl"][ot].rearrange("d c o -> d (c o)"), f"yw{par}",
              waits=[(sPE, pe_y.get(ot - 2, 0))])
        ywc = k.cnt[f"yw{par}"]
        pe = None
        for h in range(2):
            for ci in range(8):
                pe = k.emit("tensor", lambda e, ci=ci, h=h, par=par: e.matmul(
                    c.ps[0 + h][:, :], c.W2SB[:, par * 1024 + ci * 128:par * 1024 + ci * 128 + 128],
                    xpf[:, ci * 1024 + h * 512:ci * 1024 + h * 512 + 512],
                    start=(ci == 0), stop=(ci == 7)),
                    waits=([(f"yw{par}", ywc), ("tm2", tm2),
                            (sDVE, st_y.get(ot - 1, 0))] if (h == 0 and ci == 0) else []),
                    inc=sPE)
        pe_y[ot] = pe
        dvs = None
        for h in range(2):
            dvs = k.emit("vector", lambda e, h=h, ot=ot: e.tensor_copy(
                ystg[ot % 2][:, h * 512:h * 512 + 512], c.ps[0 + h][:, :]),
                waits=[(sPE, pe), ("ys", k.cnt["ys"])] +
                      ([("xps", k.cnt["xps"])] if ot == 0 else []), inc=sDVE)
        st_y[ot] = dvs
        k.dma("sync", d["y"][ot * 128:(ot + 1) * 128, :], ystg[ot % 2], "ys",
              waits=[(sDVE, dvs)])
    ytm = c.ab(64, 8)
    for g in range(8):
        k.dmat("sync", ytm[:, g * 1024:(g + 1) * 1024],
               d["y"][:, g * 128:(g + 1) * 128], "tm3",
               waits=[("ys", k.cnt["ys"]), (sDVE, k.cnt[sDVE])] if g == 0 else [])
    tm3 = k.cnt["tm3"]

    gin = c.AF32[:, 0:4096]
    g2stg = c.ab(72, 4)
    for g in range(8):
        for i in range(4):
            for j in range(4):
                sc_ap = Hres[:, g * 16 + 4 * i + j:g * 16 + 4 * i + j + 1]
                if j == 0:
                    k.emit("vector", lambda e, g=g, i=i, j=j, sc_ap=sc_ap: e.tensor_scalar(
                        gin[:, i * 1024:(i + 1) * 1024],
                        xn1t[:, g * 4096 + j * 1024:g * 4096 + j * 1024 + 1024],
                        sc_ap, None, ALU.mult),
                        waits=[(sDVE, hres_done), ("tm", tmdone),
                               ("g2s", k.cnt["g2s"])] if i == 0 else [])
                else:
                    k.emit("vector", lambda e, g=g, i=i, j=j, sc_ap=sc_ap: e.scalar_tensor_tensor(
                        gin[:, i * 1024:(i + 1) * 1024],
                        xn1t[:, g * 4096 + j * 1024:g * 4096 + j * 1024 + 1024],
                        sc_ap, gin[:, i * 1024:(i + 1) * 1024], ALU.mult, ALU.add))
        for n in range(4):
            k.emit("vector", lambda e, g=g, n=n: e.scalar_tensor_tensor(
                gin[:, n * 1024:(n + 1) * 1024], ytm[:, g * 1024:(g + 1) * 1024],
                hpo[:, g * 4 + n:g * 4 + n + 1], gin[:, n * 1024:(n + 1) * 1024],
                ALU.mult, ALU.add),
                waits=[("tm3", tm3)] if (g == 0 and n == 0) else [])
        a = k.emit("scalar", lambda e, g=g: e.activation(g2stg[:, :], gin, AF.Silu),
                   waits=[(sDVE, k.cnt[sDVE])] +
                         ([(sPE, pe_y[7])] if g == 0 else []), inc=sACT)
        k.dma("scalar", d["g2tm"][g * 128:(g + 1) * 128, :], g2stg[:, :], "g2s",
              waits=[(sACT, a)])

    g2c = [c.ab(76, 1), c.ab(77, 1)]
    for ci in range(DC):
        par = ci % 2
        k.dmat("sync", g2c[par], d["g2tm"][:, ci * 128:(ci + 1) * 128], "tm4",
               waits=([("g2s", k.cnt["g2s"])] if ci == 0 else []) +
                     [(sDVE, k.cnt[sDVE] if ci >= 2 else 0)])
        x1c = c.AF32[:, (par * 2) * 1024:(par * 2) * 1024 + 1024]
        s1c = c.AF32[:, (par * 2 + 1) * 1024:(par * 2 + 1) * 1024 + 1024]
        k.dma("gpsimd", x1c, d["x1"][ci * 128:(ci + 1) * 128, :], "fx",
              waits=[(sDVE, k.cnt[sDVE] if ci >= 2 else 0)] +
                    ([("so", k.cnt["so"]), (sACT, k.cnt[sACT])] if ci == 0 else []))
        k.dma("gpsimd", s1c, d["s1"][ci * 128:(ci + 1) * 128, :], "fx")
        fxc = k.cnt["fx"]
        tmc = k.cnt["tm4"]
        dvs = k.emit("vector", lambda e, par=par, x1c=x1c, s1c=s1c: e.tensor_mul(
            s1c, s1c, g2c[par]), waits=[("fx", fxc), ("tm4", tmc), ("fo", k.cnt["fo"])], inc=sDVE)
        dvs = k.emit("vector", lambda e, x1c=x1c, s1c=s1c: e.tensor_add(
            x1c, x1c, s1c), inc=sDVE)
        k.dma("scalar", d["out"][ci * 128:(ci + 1) * 128, :], x1c, "fo",
              waits=[(sDVE, dvs)])
    c.out_done = k.cnt["fo"]




def emit_p3(nc, k, stack, c, load_xn1=True, stage3=99, part="all"):
    d = c.d
    SINK = c.SINK
    LOGF = c.LOGF
    EYE = c.EYE
    TAU_I = 10.0
    xn1 = c.ab(32, 32)
    xn1t = c.ab(0, 32)
    ld3 = [
        k.dma("sync", c.EYE[:, :], d["eyef"], sIn),
        k.dma("sync", c.SM[:, 147:163], d["resb"], sIn),
        k.dma("sync", c.SM[:, 163:167], d["preb"], sIn),
        k.dma("sync", c.SM[:, 167:171], d["postb"], sIn),
        k.dma("sync", c.SM[:, 171:172], d["gatec"], sIn),
        k.dma("sync", c.SM[0:1, 172:173], d["eps2"], sIn),
    ]
    resb = c.SM[:, 147:163]
    preb = c.SM[:, 163:167]
    postb = c.SM[:, 167:171]
    gatec = c.SM[:, 171:172]
    eps2 = c.SM[0:1, 172:173]
    c3 = k.cnt[sIn]
    c.const_done = c3

    if load_xn1:
        for i in range(4):
            k.dma("gpsimd", sview(c.ab(32 + 8 * i, 8), 1024),
                  d["xn1"][i * 1024:(i + 1) * 1024, :].rearrange("(c d) t -> d c t", d=128),
                  "x")
    xn1done = k.cnt["x"]
    if part == "rest":
        emit_p3_rest(nc, k, stack, c, xn1done, stage3)
        return

    sqb = [c.W2SB[:, 0:512], c.W2SB[:, 512:1024]]
    pe_after = {}
    last = None
    for ci in range(DC):
        par = ci % 2
        a = None
        for h in range(2):
            waits = [("x", xn1done)]
            if ci >= 2 and h == 0:
                waits.append((sPE, pe_after[ci - 2]))
            if h == 1:
                waits.append((sPE, last))
            a = k.emit("scalar", lambda e, ci=ci, par=par, h=h: e.activation(
                sqb[par], xn1[:, ci * 1024 + h * 512:ci * 1024 + h * 512 + 512], AF.Square),
                waits=waits, inc=sACT)
            last = k.emit("tensor", lambda e, par=par, h=h, ci=ci: e.matmul(
                c.ps[6 + h][0:1, :], c.ones_col, sqb[par],
                start=(ci == 0), stop=(ci == DC - 1)),
                waits=[(sACT, a), (sIn, c.const_done)], inc=sPE)
        pe_after[ci] = last
    rr = c.ROWS[0:1, 0:1024]
    dvc = None
    for h in range(2):
        dvc = k.emit("vector", lambda e, h=h: e.tensor_copy(
            rr[:, h * 512:h * 512 + 512], c.ps[6 + h][0:1, :]),
            waits=[(sPE, last)], inc=sDVE)
    sc = k.emit("scalar", lambda e: e.activation(rr, rr, AF.Sqrt, scale=1.0 / D, bias=eps2),
                waits=[(sDVE, dvc), (sIn, c3)], inc=sACT)
    dv = k.emit("vector", lambda e: e.reciprocal(rr, rr), waits=[(sACT, sc)], inc=sDVE)
    k.wait("vector", sDVE, dv)
    rstd_dv = dv

    k.dma("gpsimd", sview(c.W2SB[:, 2048:2048 + DC * 24], 24), wview(d["wall"], 0, DC, 0, 24),
          "lw", waits=[(sPE, last)])
    lwc = k.cnt["lw"]
    pe = None
    for h in range(2):
        for ci in range(DC):
            pe = k.emit("tensor", lambda e, ci=ci, h=h: e.matmul(
                c.ps[0 + h][0:24, :], c.W2SB[:, 2048 + ci * 24:2048 + ci * 24 + 24],
                xn1[:, ci * 1024 + h * 512:ci * 1024 + h * 512 + 512],
                start=(ci == 0), stop=(ci == DC - 1)),
                waits=([("lw", lwc), ("x", xn1done)] if (h == 0 and ci == 0) else []), inc=sPE)
    dv = None
    for h in range(2):
        dv = k.emit("vector", lambda e, h=h: e.tensor_copy(
            LOGF[:, h * 512:h * 512 + 512], c.ps[0 + h][0:24, :]),
            waits=[(sPE, pe)], inc=sDVE)
    pe_tr = None
    for g in range(8):
        pe_tr = k.emit("tensor", lambda e, g=g: e.transpose(
            c.ps[2][:, g * 24:g * 24 + 24], LOGF[:, g * 128:(g + 1) * 128], EYE[0:24, 0:24]),
            waits=[(sDVE, dv)] if g == 0 else [], inc=sPE)
    ztmp = SINK[:, 512:704]
    dv_z = k.emit("vector", lambda e: e.tensor_copy(ztmp, c.ps[2][:, 0:192]),
                  waits=[(sPE, pe_tr)], inc=sDVE)
    pe_tr = None
    for g in range(8):
        pe_tr = k.emit("tensor", lambda e, g=g: e.transpose(
            c.ps[4][:, g:g + 1], rr[:, g * 128:(g + 1) * 128], EYE[0:1, 0:1]),
            waits=[(sDVE, rstd_dv), (sIn, c3)] if g == 0 else [], inc=sPE)
    rrms = SINK[:, 192:200]
    dv_rrms = k.emit("vector", lambda e: e.tensor_copy(rrms, c.ps[4][:, 0:8]),
                     waits=[(sPE, pe_tr)], inc=sDVE)

    Z = SINK[:, 0:128]
    E = SINK[:, 704:832]
    L1a = SINK[:, 832:864]
    L2a = SINK[:, 864:896]
    hp = SINK[:, 896:928]
    hpo = SINK[:, 928:960]
    hscr = SINK[:, 960:976]
    b10 = SINK[:, 976:992]
    Hres = SINK[:, 200:328]
    k.emit("vector", lambda e: e.tensor_scalar(b10, resb, TAU_I, None, ALU.mult),
           waits=[(sIn, c3)])
    for g in range(8):
        k.emit("vector", lambda e, g=g: e.tensor_scalar(
            Z[:, g * 16:(g + 1) * 16], ztmp[:, g * 24:g * 24 + 16],
            rrms[:, g:g + 1], TAU_I, ALU.mult, ALU.mult),
            waits=[(sDVE, dv_z), (sDVE, dv_rrms)] if g == 0 else [])
        k.emit("vector", lambda e, g=g: e.tensor_add(
            Z[:, g * 16:(g + 1) * 16], Z[:, g * 16:(g + 1) * 16], b10))
        k.emit("vector", lambda e, g=g: e.tensor_scalar(
            hp[:, g * 4:(g + 1) * 4], ztmp[:, g * 24 + 16:g * 24 + 20],
            rrms[:, g:g + 1], None, ALU.mult))
        k.emit("vector", lambda e, g=g: e.tensor_add(
            hp[:, g * 4:(g + 1) * 4], hp[:, g * 4:(g + 1) * 4], preb))
        k.emit("vector", lambda e, g=g: e.tensor_scalar(
            hpo[:, g * 4:(g + 1) * 4], ztmp[:, g * 24 + 20:g * 24 + 24],
            rrms[:, g:g + 1], None, ALU.mult))
        dvh = k.emit("vector", lambda e, g=g: e.tensor_add(
            hpo[:, g * 4:(g + 1) * 4], hpo[:, g * 4:(g + 1) * 4], postb))

    for name, hl in (("hp", hp), ("hpo", hpo)):
        a = k.emit("scalar", lambda e, hl=hl: e.activation(hl, hl, AF.Exp),
                   waits=[(sDVE, k.cnt[sDVE])], inc=sACT)
        dvs = k.emit("vector", lambda e, hl=hl: e.reduce_sum(
            hscr[:, 0:8], hl.rearrange("p (g n) -> p g n", n=4), axis=AX.X),
            waits=[(sACT, a)], inc=sDVE)
        dvs = k.emit("vector", lambda e: e.reciprocal(hscr[:, 0:8], hscr[:, 0:8]), inc=sDVE)
        k.wait("vector", sDVE, dvs)
        for g in range(8):
            k.emit("vector", lambda e, g=g, hl=hl: e.tensor_scalar(
                hl[:, g * 4:(g + 1) * 4], hl[:, g * 4:(g + 1) * 4],
                hscr[:, g:g + 1], None, ALU.mult))

    def sink_views(buf, fix_i=None, fix_j=None):
        v3 = buf.rearrange("p (g i j) -> p g i j", i=4, j=4)
        if fix_i is not None:
            return v3[:, :, fix_i, :]
        return v3[:, :, :, fix_j]

    def l3(buf, n=4):
        return buf.rearrange("p (g n) -> p g n", n=n)

    k.emit("vector", lambda e: e.memset(L2a, 0.0))
    for it in range(50):
        for i in range(4):
            k.emit("vector", lambda e, i=i: e.tensor_sub(
                sink_views(E, fix_i=i), sink_views(Z, fix_i=i), l3(L2a)))
        a = k.emit("scalar", lambda e: e.activation(E, E, AF.Exp),
                   waits=[(sDVE, k.cnt[sDVE])], inc=sACT)
        dvs = k.emit("vector", lambda e: e.reduce_sum(
            L1a, E.rearrange("p (gi j) -> p gi j", j=4), axis=AX.X),
            waits=[(sACT, a)], inc=sDVE)
        a = k.emit("scalar", lambda e: e.activation(L1a, L1a, AF.Ln),
                   waits=[(sDVE, dvs)], inc=sACT)
        k.wait("vector", sACT, a)
        for j in range(4):
            k.emit("vector", lambda e, j=j: e.tensor_sub(
                sink_views(E, fix_j=j), sink_views(Z, fix_j=j), l3(L1a)))
        a = k.emit("scalar", lambda e: e.activation(E, E, AF.Exp),
                   waits=[(sDVE, k.cnt[sDVE])], inc=sACT)
        dvs = k.emit("vector", lambda e: e.reduce_sum(
            l3(L2a), E.rearrange("p (g i j) -> p g j i", i=4, j=4), axis=AX.X),
            waits=[(sACT, a)], inc=sDVE)
        a = k.emit("scalar", lambda e: e.activation(L2a, L2a, AF.Ln),
                   waits=[(sDVE, dvs)], inc=sACT)
        k.wait("vector", sACT, a)
    for i in range(4):
        k.emit("vector", lambda e, i=i: e.tensor_sub(
            sink_views(Hres, fix_i=i), sink_views(Z, fix_i=i), l3(L2a)))
    for j in range(4):
        k.emit("vector", lambda e, j=j: e.tensor_sub(
            sink_views(Hres, fix_j=j), sink_views(Hres, fix_j=j), l3(L1a)))
    a = k.emit("scalar", lambda e: e.activation(Hres, Hres, AF.Exp),
               waits=[(sDVE, k.cnt[sDVE])], inc=sACT)
    gg = SINK[:, 992:993]
    om = SINK[:, 993:994]
    a2 = k.emit("scalar", lambda e: e.activation(gg, gatec, AF.Sigmoid),
                waits=[(sIn, c3)], inc=sACT)
    dvs = k.emit("vector", lambda e: e.tensor_scalar(om, gg, -1.0, 1.0, ALU.mult, ALU.add),
                 waits=[(sACT, a2)], inc=sDVE)
    dvs = k.emit("vector", lambda e: e.tensor_scalar(Hres, Hres, gg, None, ALU.mult),
                 waits=[(sACT, a)], inc=sDVE)
    for g in range(8):
        dvs = k.emit("vector", lambda e, g=g: e.tensor_scalar(
            Hres.rearrange("p (g i j) -> p g i j", i=4, j=4)[:, g, :, :]
            .rearrange("p i j -> p (i j)")[:, 0:16:5],
            Hres.rearrange("p (g i j) -> p g i j", i=4, j=4)[:, g, :, :]
            .rearrange("p i j -> p (i j)")[:, 0:16:5],
            om, None, ALU.add))
    hres_done = dvs
    c.p3s = dict(hres_done=hres_done)

    if stage3 <= 0:
        dvx = k.emit("vector", lambda e: e.tensor_copy(c.AF32[:, 0:1024], SINK[:, 0:1024]), inc=sDVE)
        k.dma("sync", d["out"][0:128, :], c.AF32[:, 0:1024], sOut, waits=[(sDVE, dvx)])
        return

    if part == "all":
        emit_p3_rest(nc, k, stack, c, xn1done, stage3)


def build_p3(stage3=99):
    nc = bass.Bass("TRN2", target_bir_lowering=False)
    c = Ctx()
    c.d = {}
    d = c.d
    dr = lambda name, shape, dt, kind: nc.dram_tensor(name, shape, dt, kind=kind).ap()
    d["xn1"] = dr("xn1", [D, T], BF16, "ExternalInput")
    d["x1"] = dr("x1", [D, T], F32, "ExternalInput")
    d["s1"] = dr("s1", [D, T], F32, "ExternalInput")
    d["wall"] = dr("wall", [D, 24], F32, "ExternalInput")
    d["wbl"] = dr("wbl", [8, 128, 8, 128], F32, "ExternalInput")
    d["resb"] = dr("resb", [128, 16], F32, "ExternalInput")
    d["preb"] = dr("preb", [128, 4], F32, "ExternalInput")
    d["postb"] = dr("postb", [128, 4], F32, "ExternalInput")
    d["gatec"] = dr("gatec", [128, 1], F32, "ExternalInput")
    d["eps2"] = dr("eps2", [1, 1], F32, "ExternalInput")
    d["eyef"] = dr("eyef", [24, 24], F32, "ExternalInput")
    d["ones_row"] = dr("ones_row", [1, 128], F32, "ExternalInput")
    d["ones_col"] = dr("ones_col", [128, 1], BF16, "ExternalInput")
    d["ones_col_f"] = dr("ones_col_f", [128, 1], F32, "ExternalInput")
    d["misc_f"] = dr("misc_f", [1, 2], F32, "ExternalInput")
    d["out"] = dr("out", [D, T], F32, "ExternalOutput")
    d["xpre"] = dr("xpre", [T, 1024], BF16, "Internal")
    d["y"] = dr("y", [1024, T], BF16, "Internal")
    d["g2tm"] = dr("g2tm", [T, D], BF16, "Internal")

    stack = ExitStack()
    with stack:
        k = K(nc, stack)
        build_common(nc, k, stack, c)
        emit_p3(nc, k, stack, c, load_xn1=True, stage3=stage3)
        k.wait("sync", "fo", k.cnt["fo"])
        k.wait("sync", sOut, k.cnt[sOut])
        k.finish()
    return nc


def p3_inputs(core, ins, xn1=None, x1=None, s1=None):
    t0 = core * T
    wall = np.concatenate([np.asarray(ins["L1_ffn_W_res"]).T,
                           np.asarray(ins["L1_ffn_W_pre"]).T,
                           np.asarray(ins["L1_ffn_W_post"]).T], axis=1)
    m = {
        "wall": np.ascontiguousarray(wall.astype(np.float32)),
        "wbl": np.ascontiguousarray(np.asarray(ins["L1_ffn_W_branch"])
                                    .reshape(8, 128, 8, 128).transpose(2, 1, 0, 3)),
        "resb": np.ascontiguousarray(np.broadcast_to(
            np.asarray(ins["L1_ffn_H_res_base"]).reshape(1, 16), (128, 16)).astype(np.float32)),
        "preb": np.ascontiguousarray(np.broadcast_to(
            np.asarray(ins["L1_ffn_H_pre_base"]).reshape(1, 4), (128, 4)).astype(np.float32)),
        "postb": np.ascontiguousarray(np.broadcast_to(
            np.asarray(ins["L1_ffn_H_post_base"]).reshape(1, 4), (128, 4)).astype(np.float32)),
        "gatec": np.ascontiguousarray(np.broadcast_to(
            np.asarray(ins["L1_ffn_gate"]).reshape(1, 1), (128, 1)).astype(np.float32)),
        "eps2": np.array([[1.1920929e-07]], np.float32),
        "eyef": np.eye(24, dtype=np.float32),
        "ones_row": np.ones((1, 128), np.float32),
        "ones_col": np.ones((128, 1), ml_dtypes.bfloat16),
        "ones_col_f": np.ones((128, 1), np.float32),
        "misc_f": np.array([[1.0, 1e-5]], np.float32),
    }
    if xn1 is not None:
        m["xn1"] = np.ascontiguousarray(xn1[t0:t0 + T].T.astype(ml_dtypes.bfloat16))
        m["x1"] = np.ascontiguousarray(x1[t0:t0 + T].T.astype(np.float32))
        m["s1"] = np.ascontiguousarray(s1[t0:t0 + T].T.astype(np.float32))
    return m


def build_p1(stage=99):
    nc = bass.Bass("TRN2", target_bir_lowering=False)
    c = Ctx()
    c.d = {}
    d = c.d
    dr = lambda name, shape, dt, kind: nc.dram_tensor(name, shape, dt, kind=kind).ap()
    d["x"] = dr("x", [D, T], F32, "ExternalInput")
    for nm, sh in (("w1l", [HCN, 128, DC, 128]), ("w3l", [HCN, 128, DC, 128]),
                   ("w2l", [DC, 128, HCN, 128]),
                   ("a1t", [D, LR]), ("b1t", [LR, HID]), ("a3t", [D, LR]),
                   ("b3t", [LR, HID]), ("a2t", [HID, LR]), ("b2t", [LR, D])):
        d[nm] = dr(nm, sh, F32, "ExternalInput")
    d["ones_row"] = dr("ones_row", [1, 128], F32, "ExternalInput")
    d["ones_col"] = dr("ones_col", [128, 1], BF16, "ExternalInput")
    d["ones_col_f"] = dr("ones_col_f", [128, 1], F32, "ExternalInput")
    d["misc_f"] = dr("misc_f", [1, 2], F32, "ExternalInput")
    d["x1"] = dr("x1", [D, T], F32, "ExternalOutput")
    d["xn1"] = dr("xn1", [D, T], BF16, "ExternalOutput")
    d["part"] = dr("part", [D, T], F32, "Internal")

    stack = ExitStack()
    with stack:
        k = K(nc, stack)
        build_common(nc, k, stack, c)
        emit_p1(nc, k, stack, c, stage)
        k.wait("sync", sOut, k.cnt[sOut])
        k.finish()
    return nc


def p1_inputs(core, ins):
    x = np.asarray(ins["L0_attn_x"])[0]
    sl = slice(core * T, (core + 1) * T)
    tr = lambda a: np.ascontiguousarray(np.asarray(a).T)
    def til(w, nt, nc_, no):
        w = np.asarray(w)
        return np.ascontiguousarray(w.reshape(nt, no, nc_, 128).transpose(0, 3, 2, 1))
    return {
        "x": np.ascontiguousarray(x[sl].T),
        "w1l": til(ins["L0_ffn_w1"], HCN, DC, 128),
        "w3l": til(ins["L0_ffn_w3"], HCN, DC, 128),
        "w2l": til(ins["L0_ffn_w2"], DC, HCN, 128),
        "a1t": tr(ins["L0_ffn_lora_A1"]), "b1t": tr(ins["L0_ffn_lora_B1"]),
        "a3t": tr(ins["L0_ffn_lora_A3"]), "b3t": tr(ins["L0_ffn_lora_B3"]),
        "a2t": tr(ins["L0_ffn_lora_A2"]), "b2t": tr(ins["L0_ffn_lora_B2"]),
        "ones_row": np.ones((1, 128), np.float32),
        "ones_col": np.ones((128, 1), ml_dtypes.bfloat16),
        "ones_col_f": np.ones((128, 1), np.float32),
        "misc_f": np.array([[1.0, 1e-5]], np.float32),
    }


def build_fused():
    nc = bass.Bass("TRN2", target_bir_lowering=False)
    c = Ctx()
    c.d = {}
    d = c.d
    dr = lambda name, shape, dt, kind: nc.dram_tensor(name, shape, dt, kind=kind).ap()
    d["x"] = dr("x", [D, T], F32, "ExternalInput")
    for nm, sh in (("w1l", [HCN, 128, DC, 128]), ("w3l", [HCN, 128, DC, 128]),
                   ("w2l", [DC, 128, HCN, 128]),
                   ("a1t", [D, LR]), ("b1t", [LR, HID]), ("a3t", [D, LR]),
                   ("b3t", [LR, HID]), ("a2t", [HID, LR]), ("b2t", [LR, D])):
        d[nm] = dr(nm, sh, F32, "ExternalInput")
    for nm, sh in (("wqal", [QRC, 128, DC, 128]), ("wqbl", [48, 128, QRC, 128]),
                   ("wkval", [4, 128, DC, 128]), ("wkvrl", [128, DC, 64]),
                   ("wknl", [NH, 128, KVRC, 128]), ("wvl", [8, 128, KVRC, 512]),
                   ("wol", [DC, 128, DC, 128])):
        d[nm] = dr(nm, sh, F32, "ExternalInput")
    d["gqa"] = dr("gqa", [128, QRC], F32, "ExternalInput")
    d["gkva"] = dr("gkva", [128, KVRC], F32, "ExternalInput")
    d["rt"] = dr("rt", [RD, RD], BF16, "ExternalInput")
    d["cos"] = dr("cos", [RD, T], F32, "ExternalInput")
    d["sin"] = dr("sin", [RD, T], F32, "ExternalInput")
    d["wall"] = dr("wall", [D, 24], F32, "ExternalInput")
    d["wbl"] = dr("wbl", [8, 128, 8, 128], F32, "ExternalInput")
    d["resb"] = dr("resb", [128, 16], F32, "ExternalInput")
    d["preb"] = dr("preb", [128, 4], F32, "ExternalInput")
    d["postb"] = dr("postb", [128, 4], F32, "ExternalInput")
    d["gatec"] = dr("gatec", [128, 1], F32, "ExternalInput")
    d["eps2"] = dr("eps2", [1, 1], F32, "ExternalInput")
    d["eyef"] = dr("eyef", [24, 24], F32, "ExternalInput")
    d["ones_row"] = dr("ones_row", [1, 128], F32, "ExternalInput")
    d["ones_col"] = dr("ones_col", [128, 1], BF16, "ExternalInput")
    d["ones_col_f"] = dr("ones_col_f", [128, 1], F32, "ExternalInput")
    d["misc_f"] = dr("misc_f", [1, 2], F32, "ExternalInput")
    d["out"] = dr("out", [D, T], F32, "ExternalOutput")
    for nm, sh, dt in (("x1", [D, T], F32), ("xn1", [D, T], BF16), ("part", [D, T], F32),
                       ("s1", [D, T], F32), ("q", [NH * QK, T], BF16),
                       ("knc", [NH * 128, T], BF16), ("vc", [T, NH * VD], BF16),
                       ("krc", [RD, T], BF16),
                       ("kng", [NCORES * NH * 128, T], BF16), ("vg", [SEQ, NH * VD], BF16),
                       ("krg", [NCORES * RD, T], BF16),
                       ("xpre", [T, 1024], BF16), ("y", [1024, T], BF16),
                       ("g2tm", [T, D], BF16)):
        if nm in ("kng", "vg", "krg"):
            d[nm] = nc.dram_tensor(nm, sh, dt, kind="Internal",
                                   addr_space="Shared").ap()
        else:
            d[nm] = dr(nm, sh, dt, "Internal")

    stack = ExitStack()
    with stack:
        k = K(nc, stack)
        build_common(nc, k, stack, c)
        emit_p1(nc, k, stack, c, 99)
        for qn_ in ("gpsimd", "sync"):
            k.wait(qn_, sPE, k.cnt[sPE])
            k.wait(qn_, sDVE, k.cnt[sDVE])
            k.wait(qn_, sACT, k.cnt[sACT])
            k.wait(qn_, sOut, k.cnt[sOut])
        c.overlap_hook = lambda: emit_p3(nc, k, stack, c, load_xn1=False,
                                         stage3=99, part="hmat")
        emit_p2(nc, k, stack, c, load_xn1=False, stage2=99)
        emit_p3(nc, k, stack, c, load_xn1=False, stage3=99, part="rest")
        k.wait("sync", "fo", k.cnt["fo"])
        k.wait("sync", sOut, k.cnt[sOut])
        k.finish()
    return nc


def fused_inputs(core, ins):
    m = p1_inputs(core, ins)
    m.update(p2_inputs(core, ins))
    m.update(p3_inputs(core, ins))
    return m


def kernel(**inputs):
    ins = inputs
    cores = list(range(NCORES))
    nc = build_fused()
    res = run_bass_kernel_spmd(nc, [fused_inputs(cc, ins) for cc in cores], core_ids=cores)
    out = np.concatenate([res.results[cc]["out"].T for cc in cores], axis=0)
    return out.reshape(1, SEQ, D).astype(np.float32)


